# version 11 (final)
# baseline (speedup 1.0000x reference)
import contextlib
import os
import numpy as np
import ml_dtypes
import concourse.bass as bass
import concourse.mybir as mybir
from concourse.bass_utils import run_bass_kernel_spmd

F32 = mybir.dt.float32
BF16 = mybir.dt.bfloat16
AF = mybir.ActivationFunctionType
ALU = mybir.AluOpType
AX = mybir.AxisListType.X

NCORES = 8
T = 2048
D = 1024
NQB = 4
QB = 512
NTT = 16
NDC = 8
SEQ = 8192
NKT = 64
DFF = 4096
NEXP = 8
LN_EPS = 1e-5
RMS_EPS = 1.1920929e-07
KILL = -30000.0
VW = 130
L0_VW = 2 * NTT * VW
L1_VW = 8 * NTT * VW



def _rope_rf(n_total=SEQ, d_head=64):
    i = np.arange(d_head)
    theta = np.power(10000.0, -2.0 * (i // 2).astype(np.float64) / d_head)
    pos = np.arange(n_total, dtype=np.float64)
    ang = pos[:, None] * theta[None, :]
    sign = np.where(pos % 2 == 0, -1.0, 1.0)[:, None]
    return (np.cos(ang) + np.sin(ang) * sign).astype(np.float32)


def host_constants(core):
    r = core % 4
    p0 = 2048 * r
    out = {}
    rf = _rope_rf()[p0:p0 + T]
    out["rf"] = np.ascontiguousarray(np.tile(rf.T, (2, 1)))
    kill = np.zeros((NQB, NKT), dtype=np.float32)
    for qb in range(NQB):
        for kt in range(NKT):
            if kt >= 16 * r:
                kill[qb, kt] = 1.0
    out["kill"] = np.ascontiguousarray(
        np.broadcast_to(kill.reshape(1, NQB * NKT), (128, NQB * NKT))).astype(ml_dtypes.bfloat16)
    u = np.arange(896)[None, :]
    k = np.arange(128)[:, None]
    out["mask01"] = np.ascontiguousarray(
        (u >= k + 384).astype(np.float32)).astype(ml_dtypes.bfloat16)
    out["ident"] = np.eye(128, dtype=np.float32)
    return out


def tt_compact(G1, G2):
    A = np.empty((4, 512, 32), dtype=np.float32)
    B = np.empty((4, 512, 16), dtype=np.float32)
    for c in range(4):
        A[c] = np.repeat(np.asarray(G1)[0, :, c, :].T, 16, axis=0)
        B[c] = np.tile(np.asarray(G2)[c, :, 0, :].T, (32, 1))
    return A, B



class Ctr:
    def __init__(self, sem):
        self.sem = sem
        self.n = 0

    def inc(self, instr, k=1):
        instr.then_inc(self.sem, k)
        self.n += k
        return self.n


class P:
    pass


CTRS = ["sD", "sG", "sT", "sA", "sV", "sC"]


def _snap(p):
    return {k: getattr(p, k).n for k in CTRS}


def _wait(p, eng, s):
    for k, v in s.items():
        if v > 0 and k != "sC":
            eng.wait_ge(getattr(p, k).sem, v)


def VS(p, V, instr):
    p.sV.inc(instr)
    V.wait_ge(p.sV.sem, p.sV.n)


def AS(p, A, instr):
    p.sA.inc(instr)
    A.wait_ge(p.sA.sem, p.sA.n)



def build_program(part):
    nc = bass.Bass("TRN2", target_bir_lowering=False, debug=False)
    p = P()
    p.nc = nc
    st = contextlib.ExitStack()
    sA_ = part in ("A", "ALL")
    sB_ = part in ("B", "ALL")
    sC_ = part in ("C", "ALL")
    fused = part == "ALL"
    p.fused = fused

    inputs = []

    def din(name, shape, dt=F32):
        inputs.append(name)
        return nc.dram_tensor(name, list(shape), dt, kind="ExternalInput").ap()

    def dout(name, shape, dt=F32):
        return nc.dram_tensor(name, list(shape), dt, kind="ExternalOutput").ap()

    def dint(name, shape, dt):
        return nc.dram_tensor(name, list(shape), dt, kind="Internal").ap()

    if sA_ or sB_ == (part == "B"):
        pass
    x_in = din("x", [D, T]) if (sA_ or part == "B") else None
    p.ident = din("ident", [128, 128])
    if sA_:
        p.rf = din("rf", [128, T])
        p.w_q = din("w_q", [D, D])
        p.w_k = din("w_k", [D, 256])
        p.w_v = din("w_v", [D, 256])
        p.normw = din("normw", [128, 8])
    if sB_:
        p.kill_t = din("kill", [128, NQB * NKT], BF16)
        p.mask01 = din("mask01", [128, 896], BF16)
        p.w_o = din("w_o", [D, D])
        p.aR = din("aR", [4, 512, 32])
        p.bR = din("bR", [4, 512, 16])
        p.aI = din("aI", [4, 512, 32])
        p.bI = din("bI", [4, 512, 16])
        p.brbi = din("brbi", [128, 8])
        p.Wq1 = din("Wq1", [D, D])
        p.Wk1 = din("Wk1", [D, D])
        p.Wv1 = din("Wv1", [D, D])
        p.bq1 = din("bq1", [128, 8])
        p.bk1 = din("bk1", [128, 8])
        p.bv1 = din("bv1", [1, D])
    if sC_:
        p.Wo1 = din("Wo1", [D, D])
        p.bo1 = din("bo1", [128, 8])
        p.gwT = din("gwT", [D, 8])
        p.gbeb = din("gbeb", [2, 8])
        p.w1 = din("w1", [NEXP, D, DFF])
        p.b1c = din("b1c", [NEXP, 128, 32])
        p.w2 = din("w2", [NEXP, DFF, D])
        p.b2 = din("b2", [NEXP, D])
        p.mrow1d = dint("mrow1d", [NEXP, T], BF16)

    if fused:
        p.kg0s = [dint(f"kg0s{g}", [512, T], BF16) for g in range(2)]
        p.vg0s = [dint(f"vg0s{g}", [512, NTT * VW], BF16) for g in range(2)]
        p.kg1s = [dint(f"kg1s{g}", [512, T], BF16) for g in range(8)]
        p.vg1s = [dint(f"vg1s{g}", [512, NTT * VW], BF16) for g in range(8)]
        p.v_loc0s = [dint(f"v_loc0s{g}", [128, NTT * VW], BF16) for g in range(2)]
        p.v_loc1s = [dint(f"v_loc1s{g}", [128, NTT * VW], BF16) for g in range(8)]
        p.cc_k = {0: {}, 1: {}}
        p.cc_v = {0: {}, 1: {}}
    if sA_:
        if fused:
            p.k_loc0 = dint("k_loc0", [256, T], BF16)
            p.v_loc0 = dint("v_loc0", [128, L0_VW], BF16)
        else:
            p.k_loc0 = dout("k_loc0", [256, T], BF16)
            p.v_loc0 = dout("v_loc0", [128, L0_VW], BF16)
            p.qdump = dout("qdump", [D, T], BF16)
    if sB_:
        if fused:
            p.kg0 = dint("kg0", [4 * 256, T], BF16)
            p.vg0 = dint("vg0", [4 * 128, L0_VW], BF16)
        else:
            p.kg0 = din("kg0", [4 * 256, T], BF16)
            p.vg0 = din("vg0", [4 * 128, L0_VW], BF16)
            p.qdump_in = din("qdump", [D, T], BF16)
            p.k_loc0 = din("k_loc0", [256, T], BF16)
            p.v_loc0 = din("v_loc0", [128, L0_VW], BF16)
        if fused:
            p.k_loc1 = dint("k_loc1", [D, T], BF16)
            p.v_loc1 = dint("v_loc1", [128, L1_VW], BF16)
        else:
            p.k_loc1 = dout("k_loc1", [D, T], BF16)
            p.v_loc1 = dout("v_loc1", [128, L1_VW], BF16)
            p.x1dump = dout("x1", [D, T])
            p.q1dump = dout("q1", [D, T], BF16)
    if sC_:
        if fused:
            p.kg1 = dint("kg1", [4 * D, T], BF16)
            p.vg1 = dint("vg1", [4 * 128, L1_VW], BF16)
        else:
            p.kg1 = din("kg1", [4 * D, T], BF16)
            p.vg1 = din("vg1", [4 * 128, L1_VW], BF16)
            p.x1_in = din("x1", [D, T])
            p.q1_in = din("q1", [D, T], BF16)
        p.out = dout("out", [D, T])
    if fused:
        p.cc_in = dint("cc_in", [128, 128], F32)
        p.cc_out = dint("cc_out", [128, 128], F32)

    sb = lambda name, shape, dt: st.enter_context(nc.sbuf_tensor(name, list(shape), dt))
    p.xf = [sb(f"xf{c}", [128, T], F32) for c in range(NDC)]
    p.xn = [sb(f"xn{c}", [128, T], BF16) for c in range(NDC)]
    p.qa = [sb(f"qa{c}", [128, T], BF16) for c in range(NDC)]
    p.wb = sb("wb", [128, 8192], BF16)
    p.vsl = sb("vsl", [128, 8320], BF16)
    p.locsl = sb("locsl", [128, 4160], BF16)
    p.ubuf = sb("ubuf", [128, 4160], BF16)
    p.scr = sb("scr", [128, 2048], F32)
    p.wtmp = sb("wtmp", [128, 256], F32)
    p.attB = sb("attB", [64, T], BF16)
    p.rfsb = None
    p.masks = sb("masks", [128, 896], BF16) if sB_ else None
    p.killsb = sb("killsb", [128, 256], BF16) if sB_ else None
    p.biastbl = sb("biastbl", [128, 256], F32) if sB_ else None
    p.onesf = sb("onesf", [128, 128], F32)
    p.onesb = sb("onesb", [128, 512], BF16)
    p.identsb = sb("identsb", [128, 128], F32)

    p.bcol = sb("bcol", [128, 16], F32)
    p.brow = sb("brow", [1, 1056], F32)
    p.srow = sb("srow", [1, 1056], F32)
    p.cst = sb("cst", [128, 48], F32)
    p.ccol = sb("ccol", [128, 1], F32)
    p.epsc = sb("epsc", [1, 2], F32)
    p.cneg = sb("cneg", [128, 1], F32)
    p.rowscr = sb("rowscr", [1, 2048], BF16) if sC_ else None
    if sC_:
        p.mkall = p.scr[:, 1024:1152]
        p.browb = p.rowscr[0:1, 0:1536]
        p.mrowc = p.rowscr[0:1, 1536:2048]
        p.mrowst = p.rowscr[0:1, 0:2048]

    p.kfm = [lambda s=s: None for s in range(2)]
    p.pt = [p.ubuf[:, 0:1024], p.ubuf[:, 1024:2048]]
    p.vtb = p.ubuf[:, 0:2080]
    p.mkbc = p.ubuf[:, 0:512]
    p.b2row = p.attB[0:1, 0:1024]
    p.kfm = [p.locsl[:, 0:2048], p.locsl[:, 2048:4096]]
    p.attb_guard = 0
    p.pv_guard = 0
    p.pend = None
    p.norm_mults_cnt = 0
    p.lg = p.scr[0:8, 512:1024]
    p.lr = p.scr[:, 1024:1536]
    p.lr2 = p.scr[:, 1536:2048]
    vslf = p.vsl[:].bitcast(F32)
    p.lnr = vslf[0:1, 0:2048]
    p.lnmr = vslf[0:1, 2048:4096]
    p.rmsr = vslf[0:1, 2048:4096]

    ps = lambda name, shape: st.enter_context(nc.psum_tensor(name, list(shape), F32))
    p.ps512 = [ps(f"pb{i}", [128, 512]) for i in range(4)]
    p.ps1k = [ps(f"pk{i}", [128, 1024]) for i in range(2)]

    sem = lambda name: st.enter_context(nc.semaphore(name))
    for k in CTRS:
        setattr(p, k, Ctr(sem(k)))

    p.block = st.enter_context(nc.Block())

    dbg = int(os.environ.get("KDBG", "99"))
    emit_load(p, x_in, sA_, sB_, sC_, part)
    if sA_:
        emit_weight_gain_cast(p, p.w_q, cols=1024)
        if dbg >= 1:
            emit_ln(p, rms_after=(dbg >= 2))
        if dbg >= 3:
            emit_qkv0(p)
        if not fused:
            emit_dump(p, [(p.qdump, p.qa)])
    if sB_:
        if not fused:
            emit_load_chunks(p, p.qdump_in, p.qa)
        if dbg >= 10:
            emit_cstats(p, layer=0)
        if dbg >= 11:
            emit_attention(p, layer=0)
        if dbg >= 12:
            emit_oproj(p, w=p.w_o, bias=None, inner_res=True)
        if dbg >= 13:
            emit_ffn0_build(p)
            emit_ln(p, rms_after=False)
        if dbg >= 14:
            emit_ffn0(p)
            emit_weight_cast_dma(p, p.Wq1, rows=D, cols=D)
        if dbg >= 15:
            emit_ln(p, rms_after=False)
        if dbg >= 16:
            emit_qkv1(p)
        if not fused:
            emit_dump(p, [(p.x1dump, p.xf), (p.q1dump, p.qa)])
    if sC_:
        if not fused:
            emit_load_chunks(p, p.x1_in, p.xf)
            emit_load_chunks(p, p.q1_in, p.qa)
        emit_cstats(p, layer=1)
        emit_attention(p, layer=1)
        emit_oproj(p, w=p.Wo1, bias="bo1")
        emit_ln(p, rms_after=False, persist=True)
        emit_moe(p)
        emit_dump(p, [(p.out, p.xf)], final=True)
    else:
        emit_tail_wait(p)

    st.close()
    return nc, inputs



def emit_load(p, x_in, sA_, sB_, sC_, part):
    @p.block.sync
    def _(S):
        if x_in is not None:
            for c in range(NDC):
                p.sD.inc(S.dma_start(out=p.xf[c][:], in_=x_in[128 * c:128 * (c + 1), :]), 16)
        if sA_:
            p.sD.inc(S.dma_start(out=p.bcol[:, 8:16], in_=p.normw[:]), 16)
        p.sD.inc(S.dma_start(out=p.identsb[:], in_=p.ident[:]), 16)
        if sB_:
            p.sD.inc(S.dma_start(out=p.killsb[:], in_=p.kill_t[:]), 16)
            p.sD.inc(S.dma_start(out=p.masks[:], in_=p.mask01[:]), 16)

    @p.block.vector
    def _(V):
        p.sV.inc(V.memset(p.onesf[:], 1.0))
        p.sV.inc(V.memset(p.onesb[:], 1.0))
        p.sV.inc(V.memset(p.epsc[0:1, 0:1], LN_EPS))
        p.sV.inc(V.memset(p.epsc[0:1, 1:2], RMS_EPS))


def emit_load_chunks(p, dram, chunks):
    @p.block.sync
    def _(S):
        for c in range(NDC):
            p.sD.inc(S.dma_start(out=chunks[c][:], in_=dram[128 * c:128 * (c + 1), :]), 16)


def emit_dump(p, pairs, final=False):
    s = _snap(p)

    @p.block.sync
    def _(S):
        _wait(p, S, s)
        for dram, chunks in pairs:
            for c in range(NDC):
                p.sD.inc(S.dma_start(out=dram[128 * c:128 * (c + 1), :], in_=chunks[c][:]), 16)
        S.wait_ge(p.sD.sem, p.sD.n)


def emit_tail_wait(p):
    s = _snap(p)

    @p.block.sync
    def _(S):
        _wait(p, S, s)
        S.wait_ge(p.sD.sem, p.sD.n)
        S.wait_ge(p.sG.sem, p.sG.n)


def emit_gather(p, src_ap, dst_ap, layer, which, g):
    need = p.sD.n

    @p.block.gpsimd
    def _(G):
        G.wait_ge(p.sD.sem, need)
        cnt = p.sC.inc(G.collective_compute(
            "AllGather", ALU.bypass,
            replica_groups=[[0, 1, 2, 3], [4, 5, 6, 7]],
            ins=[src_ap.opt()], outs=[dst_ap.opt()]), 1)
        (p.cc_k if which == "k" else p.cc_v)[layer][g] = cnt


def emit_ln(p, rms_after, persist=False):
    for qb in range(NQB):
        qs = slice(QB * qb, QB * (qb + 1))
        rr_row = p.lnr[0:1, qs] if persist else p.brow[0:1, 0:512]
        mr_row = p.lnmr[0:1, qs] if persist else p.brow[0:1, 512:1024]
        s0 = _snap(p)

        @p.block.scalar
        def _(A):
            _wait(p, A, s0)
            for c in range(NDC):
                if c >= 2:
                    A.wait_ge(p.sT.sem, s0["sT"] + (c - 1))
                p.sA.inc(A.activation(out=p.scr[:, 512 * (c % 2):512 * (c % 2 + 1)],
                                      in_=p.xf[c][:, qs], func=AF.Square))

        @p.block.tensor
        def _(TE):
            _wait(p, TE, s0)
            for c in range(NDC):
                TE.matmul(p.ps512[0][0:1, :], p.onesf[:, 0:1], p.xf[c][:, qs],
                          start=(c == 0), stop=(c == NDC - 1))
                TE.wait_ge(p.sA.sem, s0["sA"] + c + 1)
                p.sT.inc(TE.matmul(p.ps512[1][0:1, :], p.onesf[:, 0:1],
                                   p.scr[:, 512 * (c % 2):512 * (c % 2 + 1)],
                                   start=(c == 0), stop=(c == NDC - 1)))

        s1 = _snap(p)

        @p.block.vector
        def _(V):
            V.wait_ge(p.sT.sem, s1["sT"])
            rr = rr_row
            V.tensor_scalar_mul(p.srow[0:1, 0:512], p.ps512[0][0:1, :], 1.0 / D)
            VS(p, V, V.tensor_scalar_mul(p.srow[0:1, 512:1024], p.ps512[1][0:1, :], 1.0 / D))
            VS(p, V, V.tensor_tensor(rr, p.srow[0:1, 0:512], p.srow[0:1, 0:512], ALU.mult))
            p.sV.inc(V.tensor_tensor(rr, p.srow[0:1, 512:1024], rr, ALU.subtract))

        s2 = _snap(p)

        @p.block.scalar
        def _(A):
            A.wait_ge(p.sV.sem, s2["sV"])
            rr = rr_row
            AS(p, A, A.activation(out=rr, in_=rr, func=AF.Ln, bias=p.epsc[0:1, 0:1]))
            p.sA.inc(A.activation(out=rr, in_=rr, func=AF.Exp, scale=-0.5))

        s3 = _snap(p)

        @p.block.vector
        def _(V):
            V.wait_ge(p.sA.sem, s3["sA"])
            p.sV.inc(V.tensor_tensor(mr_row, p.srow[0:1, 0:512],
                                     rr_row, ALU.mult))

        s4 = _snap(p)

        @p.block.tensor
        def _(TE):
            TE.wait_ge(p.sV.sem, s4["sV"])
            TE.matmul(p.ps512[2][:], p.onesf[0:1, :], rr_row,
                      start=True, stop=True)
            p.sT.inc(TE.matmul(p.ps512[3][:], p.onesf[0:1, :], mr_row,
                               start=True, stop=True))

        s5 = _snap(p)

        @p.block.vector
        def _(V):
            V.wait_ge(p.sT.sem, s5["sT"])
            for c in range(NDC):
                V.tensor_tensor(p.scr[:, 0:512], p.xf[c][:, qs], p.ps512[2][:], ALU.mult)
                p.sV.inc(V.tensor_tensor(p.xn[c][:, qs], p.scr[:, 0:512],
                                         p.ps512[3][:], ALU.subtract))

    if rms_after:
        emit_rms(p)


def emit_rms(p):
    for qb in range(NQB):
        qs = slice(QB * qb, QB * (qb + 1))
        s0 = _snap(p)

        @p.block.scalar
        def _(A):
            _wait(p, A, s0)
            for c in range(NDC):
                if c >= 2:
                    A.wait_ge(p.sT.sem, s0["sT"] + (c - 1))
                p.sA.inc(A.activation(out=p.scr[:, 512 * (c % 2):512 * (c % 2 + 1)],
                                      in_=p.xn[c][:, qs], func=AF.Square))

        @p.block.tensor
        def _(TE):
            _wait(p, TE, s0)
            for c in range(NDC):
                TE.wait_ge(p.sA.sem, s0["sA"] + c + 1)
                p.sT.inc(TE.matmul(p.ps512[1][0:1, :], p.onesf[:, 0:1],
                                   p.scr[:, 512 * (c % 2):512 * (c % 2 + 1)],
                                   start=(c == 0), stop=(c == NDC - 1)))

        s1 = _snap(p)

        @p.block.vector
        def _(V):
            V.wait_ge(p.sT.sem, s1["sT"])
            p.sV.inc(V.tensor_scalar_mul(p.rmsr[0:1, qs], p.ps512[1][0:1, :], 1.0 / D))

        s2 = _snap(p)

        @p.block.scalar
        def _(A):
            A.wait_ge(p.sV.sem, s2["sV"])
            AS(p, A, A.activation(out=p.rmsr[0:1, qs], in_=p.rmsr[0:1, qs],
                                  func=AF.Ln, bias=p.epsc[0:1, 1:2]))
            p.sA.inc(A.activation(out=p.rmsr[0:1, qs], in_=p.rmsr[0:1, qs],
                                  func=AF.Exp, scale=-0.5))


def emit_weight_gain_cast(p, w, cols):
    s0 = _snap(p)
    locf = p.locsl[:].bitcast(F32)

    @p.block.sync
    def _(S):
        _wait(p, S, s0)
        for c in range(NDC):
            if c >= 1:
                S.wait_ge(p.sV.sem, s0["sV"] + c)
            p.sD.inc(S.dma_start(out=locf[:, 0:cols],
                                 in_=w[128 * c:128 * (c + 1), :]), 16)

    @p.block.vector
    def _(V):
        _wait(p, V, s0)
        for c in range(NDC):
            V.wait_ge(p.sD.sem, s0["sD"] + 16 * (c + 1))
            p.sV.inc(V.tensor_scalar_mul(p.wb[:, cols * c: cols * (c + 1)],
                                         locf[:, 0:cols], p.bcol[:, 8 + c:9 + c]))


def emit_weight_cast_dma(p, w, rows, cols, off=0, dst=None):
    s0 = _snap(p)
    dst = p.wb if dst is None else dst

    @p.block.gpsimd
    def _(G):
        _wait(p, G, s0)
        for c in range(rows // 128):
            p.sG.inc(G.dma_start(out=dst[:, off + cols * c: off + cols * (c + 1)],
                                 in_=w[128 * c:128 * (c + 1), :]), 16)


def emit_proj_all(p, wcols, ocs, outs, rope, bias_cols, wsrc=None):
    wsrc = p.wb if wsrc is None else wsrc
    """qb-outer projections: outs[i][:, qb] = evac(sum_c wb[c, ocs[i]].T @ xn[c][:, qb]).
    rope: evac multiplies by (rf * rr) built per qb in scr[:, 1536:2048].
    V incs per qb: (1 rope-mult if rope) + n evacs. T incs: n proj mms per qb."""
    s0 = _snap(p)
    n = len(ocs)
    vper = n + (1 if rope else 0)

    def v_after_evac(qb, i):
        return s0["sV"] + qb * vper + (1 if rope else 0) + i + 1

    if rope:
        @p.block.sync
        def _(S):
            _wait(p, S, s0)
            for qb in range(NQB):
                if qb >= 1:
                    S.wait_ge(p.sV.sem, v_after_evac(qb - 1, n - 1))
                p.sD.inc(S.dma_start(out=p.scr[:, 1536:2048],
                                     in_=p.rf[:, QB * qb:QB * (qb + 1)]), 16)

    @p.block.tensor
    def _(TE):
        _wait(p, TE, s0)
        for qb in range(NQB):
            if rope:
                TE.wait_ge(p.sD.sem, s0["sD"] + 16 * (qb + 1))
                if qb >= 1:
                    TE.wait_ge(p.sV.sem, s0["sV"] + (qb - 1) * vper + 1)
                TE.matmul(p.ps512[2][:], p.onesf[0:1, :],
                          p.rmsr[0:1, QB * qb:QB * (qb + 1)], start=True, stop=True)
            for i, oc in enumerate(ocs):
                idx = qb * n + i
                bank = p.ps512[idx % 2]
                if idx >= 2:
                    q2, i2 = (idx - 2) // n, (idx - 2) % n
                    TE.wait_ge(p.sV.sem, v_after_evac(q2, i2))
                for c in range(NDC):
                    mm = TE.matmul(bank[:], wsrc[:, wcols * c + 128 * oc: wcols * c + 128 * (oc + 1)],
                                   p.xn[c][:, QB * qb:QB * (qb + 1)],
                                   start=(c == 0), stop=(c == NDC - 1))
                p.sT.inc(mm)

    @p.block.vector
    def _(V):
        for qb in range(NQB):
            if rope:
                V.wait_ge(p.sD.sem, s0["sD"] + 16 * (qb + 1))
                V.wait_ge(p.sT.sem, s0["sT"] + qb * n + 1)
                p.sV.inc(V.tensor_tensor(p.scr[:, 1536:2048], p.scr[:, 1536:2048],
                                         p.ps512[2][:], ALU.mult))
            for i, oc in enumerate(ocs):
                idx = qb * n + i
                bank = p.ps512[idx % 2]
                V.wait_ge(p.sT.sem, s0["sT"] + idx + 1)
                qs = slice(QB * qb, QB * (qb + 1))
                if rope:
                    ins = V.tensor_tensor(outs[i][:, qs], bank[:], p.scr[:, 1536:2048],
                                          ALU.mult)
                elif bias_cols is not None:
                    ins = V.tensor_scalar_add(outs[i][:, qs], bank[:], bias_cols[i])
                else:
                    ins = V.tensor_copy(outs[i][:, qs], bank[:])
                p.sV.inc(ins)


def emit_vproj(p, groups, wcols, bias_row, v_loc, rms=False, layer=0, wsrc=None):
    wsrc = p.wb if wsrc is None else wsrc
    """Token-major V projection into v_loc slab [128, groups*16*130].
    rms: multiply outputs by per-token rr (transposed from p.rmsr)."""
    s0 = _snap(p)

    @p.block.vector
    def _(V):
        _wait(p, V, s0)
        p.sV.inc(V.memset(p.vtb.rearrange("p (l w) -> p l w", w=VW)[:, :, 64:65], 1.0))
        p.sV.inc(V.memset(p.vtb.rearrange("p (l w) -> p l w", w=VW)[:, :, 129:130], 1.0))

    if isinstance(v_loc, list):
        v_dst = lambda g: v_loc[g][:]
    else:
        v_dst = lambda g: v_loc[:, NTT * VW * g: NTT * VW * (g + 1)]
    vper = 2 if rms else 1
    for g in range(groups):
        s1 = _snap(p)

        @p.block.tensor
        def _(TE):
            _wait(p, TE, s1)
            for lt in range(NTT):
                bank = p.ps512[lt % 2]
                if rms:
                    if lt >= 1:
                        TE.wait_ge(p.sV.sem, s1["sV"] + vper * (lt - 1) + 1)
                    TE.transpose(p.ps512[2][:, 0:1],
                                 p.rmsr[0:1, 128 * lt:128 * (lt + 1)],
                                 p.identsb[0:1, 0:1])
                if lt >= 2:
                    TE.wait_ge(p.sV.sem, s1["sV"] + vper * (lt - 1))
                for c in range(NDC):
                    mm = TE.matmul(bank[:, 0:128],
                                   p.xn[c][:, 128 * lt:128 * (lt + 1)],
                                   wsrc[:, wcols * c + 128 * g: wcols * c + 128 * (g + 1)],
                                   start=(c == 0), stop=(c == NDC - 1) and bias_row is None)
                if bias_row is not None:
                    mm = TE.matmul(bank[:, 0:128], p.onesf[0:1, :],
                                   bias_row[0:1, 128 * g:128 * (g + 1)],
                                   start=False, stop=True)
                p.sT.inc(mm)

        @p.block.vector
        def _(V):
            for lt in range(NTT):
                bank = p.ps512[lt % 2]
                V.wait_ge(p.sT.sem, s1["sT"] + lt + 1)
                if rms:
                    p.sV.inc(V.tensor_copy(p.cst[:, 13:14], p.ps512[2][:, 0:1]))
                    V.wait_ge(p.sV.sem, p.sV.n)
                    V.tensor_scalar_mul(p.vtb[:, VW * lt: VW * lt + 64],
                                        bank[:, 0:64], p.cst[:, 13:14])
                    p.sV.inc(V.tensor_scalar_mul(p.vtb[:, VW * lt + 65: VW * lt + 129],
                                                 bank[:, 64:128], p.cst[:, 13:14]))
                else:
                    V.tensor_copy(p.vtb[:, VW * lt: VW * lt + 64], bank[:, 0:64])
                    p.sV.inc(V.tensor_copy(p.vtb[:, VW * lt + 65: VW * lt + 129],
                                           bank[:, 64:128]))
        s2 = _snap(p)

        @p.block.sync
        def _(S):
            _wait(p, S, s2)
            p.sD.inc(S.dma_start(out=v_dst(g), in_=p.vtb[:]), 16)

        if p.fused:
            emit_gather(p, v_dst(g), (p.vg0s if layer == 0 else p.vg1s)[g],
                        layer, "v", g)


def emit_norm2_max(p, tiles, dstcol, init):
    s0 = _snap(p)
    n = len(tiles)
    nj = n * NQB
    assert nj <= 32

    @p.block.scalar
    def _(A):
        _wait(p, A, s0)
        for i in range(n):
            for qb in range(NQB):
                j = i * NQB + qb
                if j >= 2:
                    A.wait_ge(p.sT.sem, s0["sT"] + (j - 1))
                p.sA.inc(A.activation(out=p.scr[:, 512 * (j % 2):512 * (j % 2 + 1)],
                                      in_=tiles[i][:, QB * qb:QB * (qb + 1)],
                                      func=AF.Square))

    @p.block.tensor
    def _(TE):
        _wait(p, TE, s0)
        for j in range(nj):
            TE.wait_ge(p.sA.sem, s0["sA"] + j + 1)
            if j >= 1:
                TE.wait_ge(p.sV.sem, s0["sV"] + j)
            p.sT.inc(TE.matmul(p.ps512[3][0:1, :], p.onesf[:, 0:1],
                               p.scr[:, 512 * (j % 2):512 * (j % 2 + 1)],
                               start=True, stop=True))

    @p.block.vector
    def _(V):
        for j in range(nj):
            V.wait_ge(p.sT.sem, s0["sT"] + j + 1)
            p.sV.inc(V.reduce_max(p.cst[0:1, 16 + j:17 + j], p.ps512[3][0:1, :], axis=AX))
        V.wait_ge(p.sV.sem, p.sV.n)
        VS(p, V, V.reduce_max(p.cst[0:1, 15:16], p.cst[0:1, 16:16 + nj], axis=AX))
        if init:
            p.sV.inc(V.tensor_copy(p.cst[0:1, dstcol:dstcol + 1], p.cst[0:1, 15:16]))
        else:
            p.sV.inc(V.tensor_max(p.cst[0:1, dstcol:dstcol + 1],
                                  p.cst[0:1, dstcol:dstcol + 1], p.cst[0:1, 15:16]))
        V.wait_ge(p.sV.sem, p.sV.n)


def emit_kstat_allreduce(p):
    s = _snap(p)

    @p.block.vector
    def _(V):
        _wait(p, V, s)
        p.sV.inc(V.memset(p.scr[:, 0:128], 0.0))
        p.sV.inc(V.tensor_copy(p.scr[0:1, 0:1], p.cst[0:1, 1:2]))

    s1 = _snap(p)

    @p.block.gpsimd
    def _(G):
        _wait(p, G, s1)
        p.sG.inc(G.dma_start(out=p.cc_in[:], in_=p.scr[:, 0:128]), 16)
        G.wait_ge(p.sG.sem, p.sG.n)
        p.sC.inc(G.collective_compute(
            "AllReduce", ALU.max,
            replica_groups=[[0, 1, 2, 3], [4, 5, 6, 7]],
            ins=[p.cc_in.opt()], outs=[p.cc_out.opt()]), 1)
    arC = p.sC.n

    @p.block.scalar
    def _(A):
        A.wait_ge(p.sC.sem, arC)
        p.sG.inc(A.dma_start(out=p.cst[0:1, 1:2], in_=p.cc_out[0:1, 0:1]), 16)


def emit_cstats(p, layer):
    dbg2 = int(os.environ.get("KDBG2", "99"))
    emit_norm2_max(p, tiles=[p.qa[c] for c in range(NDC)], dstcol=0, init=True)
    if dbg2 < 1:
        s_ = _snap(p)

        @p.block.vector
        def _(V):
            _wait(p, V, s_)
            p.sV.inc(V.memset(p.cst[0:1, 1:2], 100.0))
        dbg2_done = True
    if not p.fused and dbg2 >= 1:
        kg = p.kg0 if layer == 0 else p.kg1
        rows = 1024 if layer == 0 else 4096
        for blk in range(min(dbg2, rows // 128)):
            s = _snap(p)

            @p.block.sync
            def _(S):
                _wait(p, S, s)
                p.sD.inc(S.dma_start(out=p.locsl[:, 0:T].bitcast(BF16),
                                     in_=kg[128 * blk:128 * (blk + 1), :]), 16)

            emit_norm2_max(p, tiles=[p.locsl[:, 0:T]], dstcol=1, init=(blk == 0))
    s2 = _snap(p)

    @p.block.scalar
    def _(A):
        _wait(p, A, s2)
        A.activation(out=p.cst[0:1, 2:3], in_=p.cst[0:1, 0:1], func=AF.Ln)
        p.sA.inc(A.activation(out=p.cst[0:1, 3:4], in_=p.cst[0:1, 1:2], func=AF.Ln))

    s3 = _snap(p)

    @p.block.vector
    def _(V):
        V.wait_ge(p.sA.sem, s3["sA"])
        VS(p, V, V.tensor_add(p.cst[0:1, 4:5], p.cst[0:1, 2:3], p.cst[0:1, 3:4]))
        p.sV.inc(V.tensor_scalar(p.cst[0:1, 4:5], p.cst[0:1, 4:5], 0.5,
                                 -float(np.log(8.0)), ALU.mult, ALU.add))

    s4 = _snap(p)

    @p.block.scalar
    def _(A):
        A.wait_ge(p.sV.sem, s4["sV"])
        p.sA.inc(A.activation(out=p.cst[0:1, 5:6], in_=p.cst[0:1, 4:5], func=AF.Exp))

    s5 = _snap(p)

    @p.block.tensor
    def _(TE):
        TE.wait_ge(p.sA.sem, s5["sA"])
        p.sT.inc(TE.matmul(p.ps512[3][:, 0:1], p.onesf[0:1, :], p.cst[0:1, 5:6],
                           start=True, stop=True))

    s6 = _snap(p)

    @p.block.vector
    def _(V):
        V.wait_ge(p.sT.sem, s6["sT"])
        VS(p, V, V.tensor_copy(p.ccol[:], p.ps512[3][:, 0:1]))
        p.sV.inc(V.tensor_scalar_mul(p.cneg[:], p.ccol[:], -1.0))
        if layer == 0:
            p.sV.inc(V.tensor_scalar(p.biastbl[:], p.killsb[:], KILL, p.ccol[:],
                                     ALU.mult, ALU.subtract))
        V.wait_ge(p.sV.sem, p.sV.n)


def emit_attention(p, layer):
    kg = (p.kg0 if layer == 0 else p.kg1) if not p.fused else None
    vg = (p.vg0 if layer == 0 else p.vg1) if not p.fused else None
    ngroups = 2 if layer == 0 else 8
    krows = 256 if layer == 0 else 1024
    sz = _snap(p)

    @p.block.vector
    def _(V):
        _wait(p, V, sz)
        p.sV.inc(V.memset(p.lr[:], 0.0))
        p.sV.inc(V.memset(p.lr2[:], 0.0))

    sbase = _snap(p)

    for grp in range(ngroups):
        s0 = _snap(p)

        @p.block.sync
        def _(S):
            if p.fused:
                S.wait_ge(p.sD.sem, p.sD.n)
                S.wait_ge(p.sT.sem, s0["sT"])
                S.wait_ge(p.sC.sem, max(p.cc_k[layer][grp], p.cc_v[layer][grp]))
                p.sD.inc(S.dma_start(
                    out=p.wb[:].rearrange("p (r n) -> p r n", r=4),
                    in_=(p.kg0s if layer == 0 else p.kg1s)[grp]
                        .rearrange("(r f) n -> f r n", f=128)), 16)
                p.sD.inc(S.dma_start(
                    out=p.vsl[:, 0:4 * NTT * VW].rearrange("p (r n) -> p r n", r=4),
                    in_=(p.vg0s if layer == 0 else p.vg1s)[grp]
                        .rearrange("(r p) n -> p r n", p=128)), 16)
            else:
                _wait(p, S, s0)
                S.wait_ge(p.sD.sem, p.sD.n)
                p.sD.inc(S.dma_start(
                    out=p.wb[:].rearrange("p (r n) -> p r n", r=4),
                    in_=kg.rearrange("(r f) n -> f r n", f=krows)[128 * grp:128 * (grp + 1)]), 16)
                p.sD.inc(S.dma_start(
                    out=p.vsl[:].rearrange("p (r n) -> p r n", r=4),
                    in_=vg.rearrange("(r p) n -> p r n", p=128)[:, :, NTT * VW * grp: NTT * VW * (grp + 1)]), 16)
            if layer == 0:
                p.sD.inc(S.dma_start(out=p.locsl[:, 0:2048],
                                     in_=p.k_loc0[128 * grp:128 * (grp + 1), :]), 16)
                vl0 = p.v_loc0s[grp][:] if p.fused \
                    else p.v_loc0[:, NTT * VW * grp: NTT * VW * (grp + 1)]
                p.sD.inc(S.dma_start(out=p.locsl[:, 2048:2048 + NTT * VW], in_=vl0), 16)

        hps = [grp] if layer == 1 else [grp, grp + 2, grp + 4, grp + 6]
        for hp in hps:
            for qb in range(NQB):
                emit_attn_block(p, layer, hp, qb, sbase)
    emit_norm_flush(p)


def emit_attn_block(p, layer, hp, qb, sbase):
    nloc = (4 * qb + 4) if layer == 0 else 0
    nglob = 48 if layer == 0 else NKT
    kts = list(range(nglob)) + list(range(NKT, NKT + nloc))
    R = len(kts)
    s0 = _snap(p)
    qs = slice(QB * qb, QB * (qb + 1))

    def k_lhsT(kt, half):
        if kt < NKT:
            src = p.wb[:, 2048 * (kt // 16) + 128 * (kt % 16): 2048 * (kt // 16) + 128 * (kt % 16) + 128]
        else:
            lt = kt - NKT
            src = p.locsl[:, 128 * lt:128 * (lt + 1)]
        return src[64 * half:64 * (half + 1), :]

    def v_lhsT(kt, half):
        if kt < NKT:
            base = 2080 * (kt // 16) + VW * (kt % 16)
            return p.vsl[:, base + 65 * half: base + 65 * (half + 1)]
        lt = kt - NKT
        base = 2048 + VW * lt
        return p.locsl[:, base + 65 * half: base + 65 * (half + 1)]

    def bias_ap(kt):
        if layer == 1 or kt >= NKT:
            return p.cneg[:]
        return p.biastbl[:, NKT * qb + kt: NKT * qb + kt + 1]

    def diag_mask(kt):
        if layer == 1 or kt < NKT:
            return None
        jj = (kt - NKT) - 4 * qb
        return jj if 0 <= jj <= 3 else None

    mask_rounds = {i: diag_mask(kts[i]) for i in range(R) if diag_mask(kts[i]) is not None}

    def pt_ready(i):
        if i in mask_rounds:
            nv = sum(1 for k_ in mask_rounds if k_ <= i)
            return (p.sV.sem, s0["sV"] + nv)
        return (p.sA.sem, s0["sA"] + i + 1)

    sSc = []

    @p.block.tensor
    def _(TE):
        TE.wait_ge(p.sD.sem, s0["sD"])
        if s0["sG"]:
            TE.wait_ge(p.sG.sem, s0["sG"])
        TE.wait_ge(p.sV.sem, sbase["sV"])
        for i in range(R):
            buf = i % 2
            if i == 0:
                TE.wait_ge(p.sA.sem, max(0, s0["sA"] - 1))
            else:
                TE.wait_ge(p.sA.sem, s0["sA"] + (i - 1) if i >= 2 else s0["sA"])
            kt = kts[i]
            TE.matmul(p.ps1k[buf][:, 0:512], k_lhsT(kt, 0), p.qa[hp][0:64, qs],
                      start=True, stop=True, tile_position=(0, 0))
            mm = TE.matmul(p.ps1k[buf][:, 512:1024], k_lhsT(kt, 1), p.qa[hp][64:128, qs],
                           start=True, stop=True, tile_position=(64, 0))
            sSc.append(p.sT.inc(mm))
            if i == 10 and p.pend is not None:
                _norm_bcast(p, TE)
            if i >= 1:
                _pv_round(p, TE, i - 1, kts, v_lhsT, R, pt_ready, s0)
        _pv_round(p, TE, R - 1, kts, v_lhsT, R, pt_ready, s0)
    lastpv = p.sT.n

    @p.block.scalar
    def _(A):
        A.wait_ge(p.sV.sem, sbase["sV"])
        A.wait_ge(p.sD.sem, s0["sD"])
        for i in range(R):
            buf = i % 2
            A.wait_ge(p.sT.sem, sSc[i])
            p.sA.inc(A.activation(out=p.pt[buf], in_=p.ps1k[buf][:], func=AF.Exp,
                                  scale=0.125, bias=bias_ap(kts[i])))

    if mask_rounds:
        @p.block.vector
        def _(V):
            for i, jj in sorted(mask_rounds.items()):
                V.wait_ge(p.sA.sem, s0["sA"] + i + 1)
                buf = i % 2
                msl = p.masks[:, 384 - 128 * jj: 896 - 128 * jj]
                V.tensor_tensor(p.pt[buf][:, 0:512], p.pt[buf][:, 0:512], msl, ALU.mult)
                p.sV.inc(V.tensor_tensor(p.pt[buf][:, 512:1024], p.pt[buf][:, 512:1024],
                                         msl, ALU.mult))

    _norm_mults(p)
    emit_attn_store(p, hp, qb, lastpv)


def _pv_round(p, TE, i, kts, v_lhsT, R, pt_ready, s0):
    buf = i % 2
    sem, val = pt_ready(i)
    TE.wait_ge(sem, val)
    kt = kts[i]
    first = (i == 0)
    last = (i == R - 1)
    if first:
        TE.wait_ge(p.sV.sem, p.pv_guard)
    TE.matmul(p.ps512[0][0:65, :], v_lhsT(kt, 0), p.pt[buf][:, 0:512],
              start=first, stop=last)
    mm = TE.matmul(p.ps512[1][0:65, :], v_lhsT(kt, 1), p.pt[buf][:, 512:1024],
                   start=first, stop=last)
    if last:
        p.sT.inc(mm)


def _norm_bcast(p, TE):
    pend = p.pend
    TE.wait_ge(p.sV.sem, pend["rec"])
    if p.norm_mults_cnt:
        TE.wait_ge(p.sV.sem, p.norm_mults_cnt)
    TE.matmul(p.ps512[2][:], p.onesf[:], p.lr[:], start=True, stop=True)
    pend["bcast"] = p.sT.inc(
        TE.matmul(p.ps512[3][:], p.onesf[:], p.lr2[:], start=True, stop=True))


def _norm_mults(p, flush=False):
    pend = p.pend
    if pend is None:
        return
    if flush:
        s = _snap(p)

        @p.block.tensor
        def _(TE):
            _wait(p, TE, s)
            _norm_bcast(p, TE)
    hp, qb, qs = pend["hp"], pend["qb"], pend["qs"]

    @p.block.vector
    def _(V):
        V.wait_ge(p.sT.sem, pend["bcast"])
        V.tensor_tensor(p.qa[hp][0:64, qs], p.qa[hp][0:64, qs],
                        p.ps512[2][0:64, :], ALU.mult)
        p.sV.inc(V.tensor_tensor(p.attB[:, qs], p.attB[:, qs],
                                 p.ps512[3][0:64, :], ALU.mult))
    p.norm_mults_cnt = p.sV.n
    if qb == NQB - 1:
        @p.block.sync
        def _(S):
            S.wait_ge(p.sV.sem, p.norm_mults_cnt)
            p.sD.inc(S.dma_start(out=p.qa[hp][64:128, :], in_=p.attB[:]), 16)
        p.attb_guard = p.sD.n
    p.pend = None


def emit_attn_store(p, hp, qb, lastpv):
    qs = slice(QB * qb, QB * (qb + 1))

    @p.block.vector
    def _(V):
        V.wait_ge(p.sT.sem, lastpv)
        if qb == 0 and p.attb_guard:
            V.wait_ge(p.sD.sem, p.attb_guard)
        V.tensor_copy(p.qa[hp][0:64, qs], p.ps512[0][0:64, :])
        V.tensor_copy(p.attB[:, qs], p.ps512[1][0:64, :])
        V.tensor_scalar_add(p.scr[64:65, 0:512], p.ps512[0][64:65, :], 1e-38)
        p.sV.inc(V.tensor_scalar_add(p.scr[64:65, 512:1024],
                                     p.ps512[1][64:65, :], 1e-38))
        p.pv_guard = p.sV.n
        V.reciprocal(p.lr[64:65, :], p.scr[64:65, 0:512])
        p.sV.inc(V.reciprocal(p.lr2[64:65, :], p.scr[64:65, 512:1024]))
    p.pend = {"hp": hp, "qb": qb, "qs": qs, "rec": p.sV.n, "bcast": None}


def emit_norm_flush(p):
    _norm_mults(p, flush=True)


def emit_oproj(p, w, bias, inner_res=False):
    emit_weight_cast_dma(p, w, rows=D, cols=D)
    s0 = _snap(p)
    if bias is not None:
        @p.block.sync
        def _(S):
            p.sD.inc(S.dma_start(out=p.bcol[:, 0:8], in_=getattr(p, bias)[:]), 16)
    s1 = _snap(p)

    @p.block.tensor
    def _(TE):
        _wait(p, TE, s1)
        for oc in range(NDC):
            for qb in range(NQB):
                i = oc * NQB + qb
                bank = p.ps512[i % 2]
                if i >= 2:
                    TE.wait_ge(p.sV.sem, s1["sV"] + (i - 1))
                for c in range(NDC):
                    mm = TE.matmul(bank[:], p.wb[:, 1024 * c + 128 * oc: 1024 * c + 128 * (oc + 1)],
                                   p.qa[c][:, QB * qb:QB * (qb + 1)],
                                   start=(c == 0), stop=(c == NDC - 1))
                p.sT.inc(mm)

    @p.block.vector
    def _(V):
        for oc in range(NDC):
            for qb in range(NQB):
                i = oc * NQB + qb
                bank = p.ps512[i % 2]
                V.wait_ge(p.sT.sem, s1["sT"] + i + 1)
                qs = slice(QB * qb, QB * (qb + 1))
                if inner_res:
                    V.tensor_add(p.scr[:, 0:512], bank[:], p.xf[oc][:, qs])
                    ins = V.tensor_tensor(p.xf[oc][:, qs], p.scr[:, 0:512],
                                          p.xn[oc][:, qs], ALU.add)
                elif bias is None:
                    ins = V.tensor_add(p.xf[oc][:, qs], bank[:], p.xf[oc][:, qs])
                else:
                    ins = V.scalar_tensor_tensor(p.xf[oc][:, qs], bank[:],
                                                 p.bcol[:, oc:oc + 1], p.xf[oc][:, qs],
                                                 ALU.add, ALU.add)
                p.sV.inc(ins)


def emit_ffn0_build(p):
    wsl = p.vsl
    s0 = _snap(p)

    @p.block.sync
    def _(S):
        _wait(p, S, s0)
        p.sD.inc(S.dma_start(out=p.bcol[:, 0:8], in_=p.brbi[:]), 16)

    locf = p.locsl[:].bitcast(F32)
    s1 = _snap(p)

    @p.block.sync
    def _(S):
        _wait(p, S, s1)
        for sec, (a_t, b_t, off) in enumerate(
                [(p.aR, p.bR, 0)] * 4 + [(p.aI, p.bI, 2048)] * 4):
            ch = sec % 4
            base = 192 * sec
            for c in range(4):
                p.sD.inc(S.dma_start(out=p.scr[:, base + 32 * c: base + 32 * (c + 1)],
                                     in_=a_t[c, 128 * ch:128 * (ch + 1), :]), 16)
                p.sD.inc(S.dma_start(out=p.scr[:, base + 128 + 16 * c: base + 128 + 16 * (c + 1)],
                                     in_=b_t[c, 128 * ch:128 * (ch + 1), :]), 16)
    ffD = p.sD.n

    @p.block.vector
    def _(V):
        _wait(p, V, s1)
        V.wait_ge(p.sD.sem, ffD)
        for sec in range(8):
            off = 0 if sec < 4 else 2048
            ch = sec % 4
            base = 192 * sec
            for c in range(4):
                a_b = p.scr[:, base + 32 * c: base + 32 * (c + 1)].unsqueeze(2).broadcast_to([128, 32, 16])
                b_b = p.scr[:, base + 128 + 16 * c: base + 128 + 16 * (c + 1)].unsqueeze(1).broadcast_to([128, 32, 16])
                if c == 0:
                    V.tensor_tensor(locf[:, 0:512].rearrange("p (b e) -> p b e", b=32),
                                    a_b, b_b, ALU.mult)
                else:
                    V.tensor_tensor(locf[:, 512:1024].rearrange("p (b e) -> p b e", b=32),
                                    a_b, b_b, ALU.mult)
                    V.tensor_add(locf[:, 0:512], locf[:, 0:512], locf[:, 512:1024])
            p.sV.inc(V.tensor_copy(wsl[:, off + 512 * ch: off + 512 * (ch + 1)],
                                   locf[:, 0:512]))

    s3 = _snap(p)

    @p.block.vector
    def _(V):
        _wait(p, V, s3)
        for ch in range(4):
            p.sV.inc(V.tensor_scalar_mul(wsl[:, 4096 + 512 * ch: 4096 + 512 * (ch + 1)],
                                         wsl[:, 2048 + 512 * ch: 2048 + 512 * (ch + 1)], -1.0))


def emit_ffn0(p):
    wsl = p.vsl
    s4 = _snap(p)

    @p.block.tensor
    def _(TE):
        _wait(p, TE, s4)
        for oc in range(8):
            real = oc < 4
            o = oc if real else oc - 4
            for qb in range(NQB):
                i = oc * NQB + qb
                bank = p.ps512[i % 2]
                if i >= 2:
                    TE.wait_ge(p.sV.sem, s4["sV"] + (i - 1))
                qsl = slice(QB * qb, QB * (qb + 1))
                for ic in range(4):
                    w_off = (0 if real else 2048) + 512 * ic + 128 * o
                    TE.matmul(bank[:], wsl[:, w_off:w_off + 128], p.xn[ic][:, qsl],
                              start=(ic == 0), stop=False)
                for ic in range(4):
                    w_off = (4096 if real else 0) + 512 * ic + 128 * o
                    mm = TE.matmul(bank[:], wsl[:, w_off:w_off + 128], p.xn[4 + ic][:, qsl],
                                   start=False, stop=(ic == 3))
                p.sT.inc(mm)

    @p.block.vector
    def _(V):
        for oc in range(8):
            for qb in range(NQB):
                i = oc * NQB + qb
                bank = p.ps512[i % 2]
                V.wait_ge(p.sT.sem, s4["sT"] + i + 1)
                qsl = slice(QB * qb, QB * (qb + 1))
                p.sV.inc(V.scalar_tensor_tensor(p.xf[oc][:, qsl], bank[:],
                                                p.bcol[:, oc:oc + 1], p.xf[oc][:, qsl],
                                                ALU.add, ALU.add))


def emit_qkv1(p):
    s0 = _snap(p)

    @p.block.sync
    def _(S):
        _wait(p, S, s0)
        p.sD.inc(S.dma_start(out=p.bcol[:, 0:8], in_=p.bq1[:]), 16)
        p.sD.inc(S.dma_start(out=p.bcol[:, 8:16], in_=p.bk1[:]), 16)
        p.sD.inc(S.dma_start(out=p.brow[0:1, 0:1024], in_=p.bv1[:]), 16)

    emit_weight_cast_dma(p, p.Wk1, rows=D, cols=D, dst=p.vsl)
    emit_proj_all(p, wcols=1024, ocs=list(range(NDC)), outs=p.qa, rope=False,
                  bias_cols=[p.bcol[:, oc:oc + 1] for oc in range(NDC)])
    emit_weight_cast_dma(p, p.Wv1, rows=D, cols=D)
    for pair in range(4):
        ocs = [2 * pair, 2 * pair + 1]
        emit_proj_all(p, wcols=1024, ocs=ocs, outs=p.kfm, rope=False,
                      bias_cols=[p.bcol[:, 8 + oc:9 + oc] for oc in ocs], wsrc=p.vsl)
        emit_norm2_max(p, tiles=[p.kfm[0], p.kfm[1]], dstcol=1, init=(pair == 0))
        if pair == 3 and p.fused:
            emit_kstat_allreduce(p)
        s2 = _snap(p)

        @p.block.sync
        def _(S):
            _wait(p, S, s2)
            for j in range(2):
                oc = 2 * pair + j
                p.sD.inc(S.dma_start(out=p.k_loc1[128 * oc:128 * (oc + 1), :],
                                     in_=p.kfm[j][:]), 16)

        if p.fused:
            for j in range(2):
                g = 2 * pair + j
                emit_gather(p, p.k_loc1[128 * g:128 * (g + 1), :], p.kg1s[g], 1, "k", g)

    emit_vproj(p, groups=8, wcols=1024, bias_row=p.brow,
               v_loc=p.v_loc1s if p.fused else p.v_loc1, layer=1)


def emit_qkv0(p):
    dbg = int(os.environ.get("KDBG", "99"))
    if dbg >= 4:
        emit_proj_all(p, wcols=1024, ocs=list(range(NDC)), outs=p.qa, rope=True, bias_cols=None)
    if dbg < 6:
        return
    emit_weight_gain_cast(p, p.w_k, cols=256)
    emit_proj_all(p, wcols=256, ocs=[0, 1], outs=p.kfm, rope=True, bias_cols=None)
    s1 = _snap(p)

    @p.block.sync
    def _(S):
        _wait(p, S, s1)
        for oc in range(2):
            p.sD.inc(S.dma_start(out=p.k_loc0[128 * oc:128 * (oc + 1), :], in_=p.kfm[oc][:]), 16)

    if p.fused:
        for g in range(2):
            emit_gather(p, p.k_loc0[128 * g:128 * (g + 1), :], p.kg0s[g], 0, "k", g)

    emit_norm2_max(p, tiles=[p.kfm[0], p.kfm[1]], dstcol=1, init=True)
    if p.fused:
        emit_kstat_allreduce(p)
    if int(os.environ.get("KDBG", "99")) < 7:
        return
    emit_weight_gain_cast(p, p.w_v, cols=256)
    emit_vproj(p, groups=2, wcols=256, bias_row=None,
               v_loc=p.v_loc0s if p.fused else p.v_loc0, rms=True, layer=0)


def emit_moe(p):
    s0 = _snap(p)

    @p.block.sync
    def _(S):
        _wait(p, S, s0)
        for c in range(NDC):
            p.sD.inc(S.dma_start(out=p.wtmp[:, 8 * c:8 * (c + 1)],
                                 in_=p.gwT[128 * c:128 * (c + 1), :]), 16)
        p.sD.inc(S.dma_start(out=p.brow[0:1, 1024:1032], in_=p.gbeb[0:1, :]), 16)
        p.sD.inc(S.dma_start(out=p.brow[0:1, 1032:1040], in_=p.gbeb[1:2, :]), 16)

    s1 = _snap(p)

    @p.block.tensor
    def _(TE):
        _wait(p, TE, s1)
        for c in range(NDC):
            p.sT.inc(TE.matmul(p.ps512[3][0:1, 0:8], p.onesf[:, 0:1],
                               p.wtmp[:, 8 * c:8 * (c + 1)],
                               start=(c == 0), stop=(c == NDC - 1)))

    s2 = _snap(p)

    @p.block.vector
    def _(V):
        V.wait_ge(p.sT.sem, s2["sT"])
        p.sV.inc(V.tensor_copy(p.srow[0:1, 1024:1032], p.ps512[3][0:1, 0:8]))
        VS(p, V, V.tensor_add(p.srow[0:1, 1032:1040], p.brow[0:1, 1024:1032],
                              p.brow[0:1, 1032:1040]))
        p.sV.inc(V.tensor_scalar_mul(p.srow[0:1, 1040:1048],
                                     p.srow[0:1, 1032:1040], -1.0))
        V.wait_ge(p.sV.sem, p.sV.n)

    for qb in range(NQB):
        qsl = slice(QB * qb, QB * (qb + 1))
        s3 = _snap(p)

        @p.block.tensor
        def _(TE):
            _wait(p, TE, s3)
            for c in range(NDC):
                p.sT.inc(TE.matmul(p.ps1k[0][0:8, 0:512], p.wtmp[:, 8 * c:8 * (c + 1)],
                                   p.xf[c][:, qsl], start=(c == 0), stop=(c == NDC - 1)))

        s4 = _snap(p)

        @p.block.vector
        def _(V):
            V.wait_ge(p.sT.sem, s4["sT"])
            p.sV.inc(V.tensor_copy(p.lg, p.ps1k[0][0:8, 0:512]))

        for tt in range(4):
            gt = 4 * qb + tt
            s5 = _snap(p)

            @p.block.tensor
            def _(TE):
                _wait(p, TE, s5)
                TE.transpose(p.ps512[2][:, 0:8], p.lg[0:8, 128 * tt:128 * (tt + 1)],
                             p.identsb[0:8, 0:8])
                TE.transpose(p.ps512[3][:, 8:9],
                             p.lnr[0:1, qsl.start + 128 * tt: qsl.start + 128 * (tt + 1)],
                             p.identsb[0:1, 0:1])
                TE.matmul(p.ps512[3][:, 0:8],
                          p.lnmr[0:1, qsl.start + 128 * tt: qsl.start + 128 * (tt + 1)],
                          p.srow[0:1, 1024:1032], start=True, stop=False)
                p.sT.inc(TE.matmul(p.ps512[3][:, 0:8], p.onesf[0:1, :],
                                   p.srow[0:1, 1040:1048], start=False, stop=True))

            s6 = _snap(p)

            @p.block.vector
            def _(V):
                V.wait_ge(p.sT.sem, s6["sT"])
                VS(p, V, V.tensor_copy(p.cst[:, 10:11], p.ps512[3][:, 8:9]))
                VS(p, V, V.tensor_scalar_mul(p.scr[:, 0:8], p.ps512[2][:, 0:8],
                                             p.cst[:, 10:11]))
                VS(p, V, V.tensor_tensor(p.scr[:, 0:8], p.scr[:, 0:8],
                                         p.ps512[3][:, 0:8], ALU.subtract))
                VS(p, V, V.reduce_max(p.cst[:, 9:10], p.scr[:, 0:8], axis=AX))
                p.sV.inc(V.tensor_scalar(p.mkall[:, 8 * gt:8 * (gt + 1)], p.scr[:, 0:8],
                                         p.cst[:, 9:10], None, ALU.is_ge))

    s8 = _snap(p)
    for e in range(NEXP):
        s8e = _snap(p)

        @p.block.tensor
        def _(TE):
            _wait(p, TE, s8e)
            for gt in range(NTT):
                if gt >= 1:
                    TE.wait_ge(p.sV.sem, s8e["sV"] + gt)
                p.sT.inc(TE.transpose(p.ps512[2][0:1, 8:136],
                                      p.mkall[:, 8 * gt + e:8 * gt + e + 1],
                                      p.identsb[:]))

        @p.block.vector
        def _(V):
            for gt in range(NTT):
                V.wait_ge(p.sT.sem, s8e["sT"] + gt + 1)
                p.sV.inc(V.tensor_copy(p.mrowst[0:1, 128 * gt:128 * (gt + 1)],
                                       p.ps512[2][0:1, 8:136]))

        s8f = _snap(p)

        @p.block.sync
        def _(S):
            _wait(p, S, s8f)
            p.sD.inc(S.dma_start(out=p.mrow1d[e:e + 1, :], in_=p.mrowst), 16)

    s_moe = _snap(p)
    hT = {}
    hA = {}
    yT = {}
    yV = {}
    hbanks = [p.ps1k[0][:, 0:512], p.ps1k[0][:, 512:1024],
              p.ps1k[1][:, 0:512], p.ps1k[1][:, 512:1024]]
    w1slab = [p.wb[:, 0:4096], p.wb[:, 4096:8192]]
    w2slab = [p.vsl[:, 0:4096], p.vsl[:, 4096:8192]]
    htmp = [p.locsl[:, 0:512], p.locsl[:, 512:1024]]
    mkb = p.ubuf[:, 0:2048]

    for e in range(NEXP):
        se0 = _snap(p)

        @p.block.sync
        def _(S):
            _wait(p, S, se0)
            p.sD.inc(S.dma_start(out=p.mrowst, in_=p.mrow1d[e:e + 1, :]), 16)
            p.sD.inc(S.dma_start(out=p.wtmp[:, 64:96], in_=p.b1c[e]), 16)
        b1D = p.sD.n

        se1 = _snap(p)

        @p.block.tensor
        def _(TE):
            _wait(p, TE, se1)
            for j in range(4):
                if j >= 1:
                    TE.wait_ge(p.sV.sem, se1["sV"] + j)
                p.sT.inc(TE.matmul(hbanks[3], p.onesb[0:1, 0:128],
                                   p.mrowst[0:1, 512 * j:512 * (j + 1)],
                                   start=True, stop=True))

        @p.block.vector
        def _(V):
            for j in range(4):
                V.wait_ge(p.sT.sem, se1["sT"] + j + 1)
                p.sV.inc(V.tensor_copy(mkb[:, 512 * j:512 * (j + 1)], hbanks[3]))
        bcV = p.sV.n

        for fg in range(8):
            t = e * 8 + fg
            sl = t % 2

            @p.block.gpsimd
            def _(G):
                if t >= 2:
                    G.wait_ge(p.sT.sem, yT[t - 2])
                else:
                    _wait(p, G, s_moe)
                for c in range(NDC):
                    p.sG.inc(G.dma_start(
                        out=w1slab[sl][:, 512 * c:512 * (c + 1)],
                        in_=p.w1[e, 128 * c:128 * (c + 1), 512 * fg:512 * (fg + 1)]), 16)
            w1D = p.sG.n

            @p.block.gpsimd
            def _(G):
                for sf in range(4):
                    p.sG.inc(G.dma_start(
                        out=w2slab[sl][:, 1024 * sf:1024 * (sf + 1)],
                        in_=p.w2[e, 512 * fg + 128 * sf: 512 * fg + 128 * (sf + 1), :]), 16)
                if fg == 1:
                    p.sG.inc(G.dma_start(out=p.b2row[:], in_=p.b2[e:e + 1, :]), 16)
            w2D = p.sG.n

            hTbase = p.sT.n

            @p.block.tensor
            def _(TE):
                if t == 0:
                    _wait(p, TE, s_moe)
                TE.wait_ge(p.sG.sem, w1D)
                for qb in range(NQB):
                    qsl = slice(QB * qb, QB * (qb + 1))
                    for sf in range(4):
                        j = qb * 4 + sf
                        if t >= 1:
                            TE.wait_ge(p.sA.sem, hA[t - 1] + max(0, j - 3))
                        if fg == 0 and j == 3:
                            TE.wait_ge(p.sV.sem, bcV)
                        for c in range(NDC):
                            mm = TE.matmul(
                                hbanks[j % 4],
                                w1slab[sl][:, 512 * c + 128 * sf: 512 * c + 128 * (sf + 1)],
                                p.xn[c][:, qsl], start=(c == 0), stop=(c == NDC - 1))
                        p.sT.inc(mm)
            hT[t] = p.sT.n

            hAbase = p.sA.n
            vA = p.sV.n

            @p.block.scalar
            def _(A):
                if t == 0:
                    _wait(p, A, s_moe)
                if fg == 0:
                    A.wait_ge(p.sD.sem, b1D)
                for j in range(16):
                    sf = j % 4
                    fc = 4 * fg + sf
                    A.wait_ge(p.sT.sem, hTbase + j + 1)
                    if j >= 2:
                        A.wait_ge(p.sV.sem, vA + j - 1)
                    p.sA.inc(A.activation(out=htmp[j % 2], in_=hbanks[j % 4],
                                          func=AF.Relu,
                                          bias=p.wtmp[:, 64 + fc:65 + fc]))
            hA[t] = p.sA.n

            @p.block.vector
            def _(V):
                if t < 2:
                    _wait(p, V, s_moe)
                else:
                    V.wait_ge(p.sT.sem, yT[t - 2])
                for j in range(16):
                    qb, sf = j // 4, j % 4
                    qsl = slice(QB * qb, QB * (qb + 1))
                    V.wait_ge(p.sA.sem, hAbase + j + 1)
                    p.sV.inc(V.tensor_tensor(p.qa[4 * sl + sf][:, qsl],
                                             htmp[j % 2], mkb[:, qsl], ALU.mult))

            yTbase = p.sT.n
            vY = p.sV.n

            @p.block.tensor
            def _(TE):
                TE.wait_ge(p.sG.sem, w2D)
                for qb in range(NQB):
                    qsl = slice(QB * qb, QB * (qb + 1))
                    TE.wait_ge(p.sV.sem, vA + 4 * qb + 4)
                    for dc in range(NDC):
                        g = qb * 8 + dc
                        bank = p.ps512[g % 4]
                        if g >= 4:
                            TE.wait_ge(p.sV.sem, vY + g - 3)
                        elif t >= 1:
                            TE.wait_ge(p.sV.sem, yV[t - 1])
                        for sf in range(4):
                            mm = TE.matmul(
                                bank[:],
                                w2slab[sl][:, 1024 * sf + 128 * dc: 1024 * sf + 128 * (dc + 1)],
                                p.qa[4 * sl + sf][:, qsl],
                                start=(sf == 0), stop=(sf == 3 and fg != 7))
                        if fg == 7:
                            mm = TE.matmul(bank[:], p.b2row[0:1, 128 * dc:128 * (dc + 1)],
                                           p.mrowst[0:1, qsl], start=False, stop=True)
                        p.sT.inc(mm)
            yT[t] = p.sT.n

            @p.block.vector
            def _(V):
                for qb in range(NQB):
                    qsl = slice(QB * qb, QB * (qb + 1))
                    for dc in range(NDC):
                        g = qb * 8 + dc
                        V.wait_ge(p.sT.sem, yTbase + g + 1)
                        p.sV.inc(V.tensor_add(p.xf[dc][:, qsl], p.ps512[g % 4][:],
                                              p.xf[dc][:, qsl]))
            yV[t] = p.sV.n



_NC_CACHE = {}


def _get_prog(part):
    if part not in _NC_CACHE:
        _NC_CACHE[part] = build_program(part)
    return _NC_CACHE[part]


def _per_core_shared(inp):
    aR, bR = tt_compact(inp["L0_ffn_G1_R"], inp["L0_ffn_G2_R"])
    aI, bI = tt_compact(inp["L0_ffn_G1_I"], inp["L0_ffn_G2_I"])
    f32 = lambda a: np.ascontiguousarray(np.asarray(a), dtype=np.float32)
    colsof = lambda v: np.ascontiguousarray(f32(v).reshape(8, 128).T)
    shared = {
        "w_q": f32(inp["L0_attn_w_q"]), "w_k": f32(inp["L0_attn_w_k"]),
        "w_v": f32(inp["L0_attn_w_v"]), "w_o": f32(inp["L0_attn_w_o"]),
        "normw": colsof(inp["L0_attn_norm_w"]),
        "aR": aR, "bR": bR, "aI": aI, "bI": bI,
        "brbi": np.ascontiguousarray(np.concatenate(
            [f32(inp["L0_ffn_bias_real"]).reshape(4, 128).T,
             f32(inp["L0_ffn_bias_imag"]).reshape(4, 128).T], axis=1)),
        "Wq1": f32(inp["L1_attn_Wq"]), "Wk1": f32(inp["L1_attn_Wk"]),
        "Wv1": f32(inp["L1_attn_Wv"]), "Wo1": f32(inp["L1_attn_Wo"]),
        "bq1": colsof(inp["L1_attn_bq"]), "bk1": colsof(inp["L1_attn_bk"]),
        "bv1": f32(inp["L1_attn_bv"]).reshape(1, D),
        "bo1": colsof(inp["L1_attn_bo"]),
        "gwT": np.ascontiguousarray(f32(inp["L1_ffn_gate_w"]).T),
        "gbeb": np.ascontiguousarray(np.stack(
            [f32(inp["L1_ffn_gate_b"]), f32(inp["L1_ffn_expert_bias"])])),
        "w1": f32(inp["L1_ffn_w1"]),
        "b1c": np.ascontiguousarray(
            f32(inp["L1_ffn_b1"]).reshape(NEXP, 32, 128).transpose(0, 2, 1)),
        "w2": f32(inp["L1_ffn_w2"]), "b2": f32(inp["L1_ffn_b2"]),
    }
    return shared


def kernel(**inp):
    x = np.asarray(inp["L0_attn_x"], dtype=np.float32)
    shared = _per_core_shared(inp)
    consts = [host_constants(c) for c in range(NCORES)]
    xs = []
    for c in range(NCORES):
        b, r = c // 4, c % 4
        xs.append(np.ascontiguousarray(x[b, 2048 * r:2048 * (r + 1), :].T))

    def pick(names, extra):
        maps = []
        for c in range(NCORES):
            m = {}
            for n in names:
                if n == "x":
                    m[n] = xs[c]
                elif n in ("rf", "kill", "mask01", "ident"):
                    m[n] = consts[c][n]
                elif n in shared:
                    m[n] = shared[n]
                elif n in extra:
                    m[n] = extra[n][c]
                else:
                    raise KeyError(n)
            maps.append(m)
        return maps

    if os.environ.get("KMODE", "fused") == "fused":
        ncF, namesF = _get_prog("ALL")
        res = run_bass_kernel_spmd(ncF, pick(namesF, {}), core_ids=list(range(NCORES)))
        out = np.empty((2, SEQ, D), dtype=np.float32)
        for c in range(NCORES):
            b, r = c // 4, c % 4
            out[b, 2048 * r:2048 * (r + 1), :] = res.results[c]["out"].T
        return out

    ncA, namesA = _get_prog("A")
    resA = run_bass_kernel_spmd(ncA, pick(namesA, {}), core_ids=list(range(NCORES)))
    rA = resA.results
    kg0 = [np.concatenate([rA[g * 4 + i]["k_loc0"] for i in range(4)], axis=0)
           for g in range(2)]
    vg0 = [np.concatenate([rA[g * 4 + i]["v_loc0"] for i in range(4)], axis=0)
           for g in range(2)]
    extraB = {
        "qdump": [rA[c]["qdump"] for c in range(NCORES)],
        "kg0": [kg0[c // 4] for c in range(NCORES)],
        "vg0": [vg0[c // 4] for c in range(NCORES)],
        "k_loc0": [rA[c]["k_loc0"] for c in range(NCORES)],
        "v_loc0": [rA[c]["v_loc0"] for c in range(NCORES)],
    }
    ncB, namesB = _get_prog("B")
    resB = run_bass_kernel_spmd(ncB, pick(namesB, extraB), core_ids=list(range(NCORES)))
    rB = resB.results
    kg1 = [np.concatenate([rB[g * 4 + i]["k_loc1"] for i in range(4)], axis=0)
           for g in range(2)]
    vg1 = [np.concatenate([rB[g * 4 + i]["v_loc1"] for i in range(4)], axis=0)
           for g in range(2)]
    extraC = {
        "x1": [rB[c]["x1"] for c in range(NCORES)],
        "q1": [rB[c]["q1"] for c in range(NCORES)],
        "kg1": [kg1[c // 4] for c in range(NCORES)],
        "vg1": [vg1[c // 4] for c in range(NCORES)],
    }
    ncC, namesC = _get_prog("C")
    resC = run_bass_kernel_spmd(ncC, pick(namesC, extraC), core_ids=list(range(NCORES)))
    rC = resC.results

    out = np.empty((2, SEQ, D), dtype=np.float32)
    for c in range(NCORES):
        b, r = c // 4, c % 4
        out[b, 2048 * r:2048 * (r + 1), :] = rC[c]["out"].T
    return out


# version 12
# speedup vs baseline: 1.0087x; 1.0087x over previous
import contextlib
import os
import numpy as np
import ml_dtypes
import concourse.bass as bass
import concourse.mybir as mybir
from concourse.bass_utils import run_bass_kernel_spmd

F32 = mybir.dt.float32
BF16 = mybir.dt.bfloat16
AF = mybir.ActivationFunctionType
ALU = mybir.AluOpType
AX = mybir.AxisListType.X

NCORES = 8
T = 2048
D = 1024
NQB = 4
QB = 512
NTT = 16
NDC = 8
SEQ = 8192
NKT = 64
DFF = 4096
NEXP = 8
LN_EPS = 1e-5
RMS_EPS = 1.1920929e-07
KILL = -30000.0
VW = 130
L0_VW = 2 * NTT * VW
L1_VW = 8 * NTT * VW



def _rope_rf(n_total=SEQ, d_head=64):
    i = np.arange(d_head)
    theta = np.power(10000.0, -2.0 * (i // 2).astype(np.float64) / d_head)
    pos = np.arange(n_total, dtype=np.float64)
    ang = pos[:, None] * theta[None, :]
    sign = np.where(pos % 2 == 0, -1.0, 1.0)[:, None]
    return (np.cos(ang) + np.sin(ang) * sign).astype(np.float32)


def host_constants(core):
    r = core % 4
    p0 = 2048 * r
    out = {}
    rf = _rope_rf()[p0:p0 + T]
    out["rf"] = np.ascontiguousarray(np.tile(rf.T, (2, 1)))
    kill = np.zeros((NQB, NKT), dtype=np.float32)
    for qb in range(NQB):
        for kt in range(NKT):
            if kt >= 16 * r:
                kill[qb, kt] = 1.0
    out["kill"] = np.ascontiguousarray(
        np.broadcast_to(kill.reshape(1, NQB * NKT), (128, NQB * NKT))).astype(ml_dtypes.bfloat16)
    u = np.arange(896)[None, :]
    k = np.arange(128)[:, None]
    out["mask01"] = np.ascontiguousarray(
        (u >= k + 384).astype(np.float32)).astype(ml_dtypes.bfloat16)
    out["ident"] = np.eye(128, dtype=np.float32)
    return out


def tt_compact(G1, G2):
    A = np.empty((4, 512, 32), dtype=np.float32)
    B = np.empty((4, 512, 16), dtype=np.float32)
    for c in range(4):
        A[c] = np.repeat(np.asarray(G1)[0, :, c, :].T, 16, axis=0)
        B[c] = np.tile(np.asarray(G2)[c, :, 0, :].T, (32, 1))
    return A, B


def tt_pack(aR, bR, aI, bI):
    out = np.empty((128, 1536), dtype=np.float32)
    for sec in range(8):
        a_t, b_t = (aR, bR) if sec < 4 else (aI, bI)
        ch = sec % 4
        base = 192 * sec
        for c in range(4):
            out[:, base + 32 * c: base + 32 * (c + 1)] = a_t[c, 128 * ch:128 * (ch + 1), :]
            out[:, base + 128 + 16 * c: base + 128 + 16 * (c + 1)] = b_t[c, 128 * ch:128 * (ch + 1), :]
    return np.ascontiguousarray(out)



class Ctr:
    def __init__(self, sem):
        self.sem = sem
        self.n = 0

    def inc(self, instr, k=1):
        instr.then_inc(self.sem, k)
        self.n += k
        return self.n


class P:
    pass


CTRS = ["sD", "sG", "sT", "sA", "sV", "sC"]


def _snap(p):
    return {k: getattr(p, k).n for k in CTRS}


def _wait(p, eng, s):
    for k, v in s.items():
        if v > 0 and k != "sC":
            eng.wait_ge(getattr(p, k).sem, v)


def VS(p, V, instr):
    p.sV.inc(instr)
    V.wait_ge(p.sV.sem, p.sV.n)


def AS(p, A, instr):
    p.sA.inc(instr)
    A.wait_ge(p.sA.sem, p.sA.n)



def build_program(part):
    nc = bass.Bass("TRN2", target_bir_lowering=False, debug=False)
    p = P()
    p.nc = nc
    st = contextlib.ExitStack()
    sA_ = part in ("A", "ALL")
    sB_ = part in ("B", "ALL")
    sC_ = part in ("C", "ALL")
    fused = part == "ALL"
    p.fused = fused

    inputs = []

    def din(name, shape, dt=F32):
        inputs.append(name)
        return nc.dram_tensor(name, list(shape), dt, kind="ExternalInput").ap()

    def dout(name, shape, dt=F32):
        return nc.dram_tensor(name, list(shape), dt, kind="ExternalOutput").ap()

    def dint(name, shape, dt):
        return nc.dram_tensor(name, list(shape), dt, kind="Internal").ap()

    if sA_ or sB_ == (part == "B"):
        pass
    x_in = din("x", [D, T]) if (sA_ or part == "B") else None
    p.ident = din("ident", [128, 128])
    if sA_:
        p.rf = din("rf", [128, T])
        p.w_q = din("w_q", [D, D])
        p.w_k = din("w_k", [D, 256])
        p.w_v = din("w_v", [D, 256])
        p.normw = din("normw", [128, 8])
    if sB_:
        p.kill_t = din("kill", [128, NQB * NKT], BF16)
        p.mask01 = din("mask01", [128, 896], BF16)
        p.w_o = din("w_o", [D, D])
        p.fftt = din("fftt", [128, 1536])
        p.brbi = din("brbi", [128, 8])
        p.Wq1 = din("Wq1", [D, D])
        p.Wk1 = din("Wk1", [D, D])
        p.Wv1 = din("Wv1", [D, D])
        p.bq1 = din("bq1", [128, 8])
        p.bk1 = din("bk1", [128, 8])
        p.bv1 = din("bv1", [1, D])
    if sC_:
        p.Wo1 = din("Wo1", [D, D])
        p.bo1 = din("bo1", [128, 8])
        p.gwT = din("gwT", [D, 8])
        p.gbeb = din("gbeb", [2, 8])
        p.w1 = din("w1", [NEXP, D, DFF])
        p.b1c = din("b1c", [NEXP, 128, 32])
        p.w2 = din("w2", [NEXP, DFF, D])
        p.b2 = din("b2", [NEXP, D])
        p.mrow1d = dint("mrow1d", [NEXP, T], BF16)

    if fused:
        p.kg0s = [dint(f"kg0s{g}", [512, T], BF16) for g in range(2)]
        p.vg0s = [dint(f"vg0s{g}", [512, NTT * VW], BF16) for g in range(2)]
        p.kg1s = [dint(f"kg1s{g}", [512, T], BF16) for g in range(8)]
        p.vg1s = [dint(f"vg1s{g}", [512, NTT * VW], BF16) for g in range(8)]
        p.v_loc0s = [dint(f"v_loc0s{g}", [128, NTT * VW], BF16) for g in range(2)]
        p.v_loc1s = [dint(f"v_loc1s{g}", [128, NTT * VW], BF16) for g in range(8)]
        p.cc_k = {0: {}, 1: {}}
        p.cc_v = {0: {}, 1: {}}
    if sA_:
        if fused:
            p.k_loc0 = dint("k_loc0", [256, T], BF16)
            p.v_loc0 = dint("v_loc0", [128, L0_VW], BF16)
        else:
            p.k_loc0 = dout("k_loc0", [256, T], BF16)
            p.v_loc0 = dout("v_loc0", [128, L0_VW], BF16)
            p.qdump = dout("qdump", [D, T], BF16)
    if sB_:
        if fused:
            p.kg0 = dint("kg0", [4 * 256, T], BF16)
            p.vg0 = dint("vg0", [4 * 128, L0_VW], BF16)
        else:
            p.kg0 = din("kg0", [4 * 256, T], BF16)
            p.vg0 = din("vg0", [4 * 128, L0_VW], BF16)
            p.qdump_in = din("qdump", [D, T], BF16)
            p.k_loc0 = din("k_loc0", [256, T], BF16)
            p.v_loc0 = din("v_loc0", [128, L0_VW], BF16)
        if fused:
            p.k_loc1 = dint("k_loc1", [D, T], BF16)
            p.v_loc1 = dint("v_loc1", [128, L1_VW], BF16)
        else:
            p.k_loc1 = dout("k_loc1", [D, T], BF16)
            p.v_loc1 = dout("v_loc1", [128, L1_VW], BF16)
            p.x1dump = dout("x1", [D, T])
            p.q1dump = dout("q1", [D, T], BF16)
    if sC_:
        if fused:
            p.kg1 = dint("kg1", [4 * D, T], BF16)
            p.vg1 = dint("vg1", [4 * 128, L1_VW], BF16)
        else:
            p.kg1 = din("kg1", [4 * D, T], BF16)
            p.vg1 = din("vg1", [4 * 128, L1_VW], BF16)
            p.x1_in = din("x1", [D, T])
            p.q1_in = din("q1", [D, T], BF16)
        p.out = dout("out", [D, T])
    if fused:
        p.cc_in = dint("cc_in", [128, 128], F32)
        p.cc_out = dint("cc_out", [128, 128], F32)

    sb = lambda name, shape, dt: st.enter_context(nc.sbuf_tensor(name, list(shape), dt))
    p.xf = [sb(f"xf{c}", [128, T], F32) for c in range(NDC)]
    p.xn = [sb(f"xn{c}", [128, T], BF16) for c in range(NDC)]
    p.qa = [sb(f"qa{c}", [128, T], BF16) for c in range(NDC)]
    p.wb = sb("wb", [128, 8192], BF16)
    p.vsl = sb("vsl", [128, 8320], BF16)
    p.locsl = sb("locsl", [128, 4160], BF16)
    p.ubuf = sb("ubuf", [128, 4160], BF16)
    p.scr = sb("scr", [128, 2048], F32)
    p.wtmp = sb("wtmp", [128, 256], F32)
    p.attB = sb("attB", [64, T], BF16)
    p.rfsb = None
    p.masks = sb("masks", [128, 896], BF16) if sB_ else None
    p.killsb = sb("killsb", [128, 256], BF16) if sB_ else None
    p.biastbl = sb("biastbl", [128, 256], F32) if sB_ else None
    p.onesf = sb("onesf", [128, 128], F32)
    p.onesb = sb("onesb", [128, 512], BF16)
    p.identsb = sb("identsb", [128, 128], F32)

    p.bcol = sb("bcol", [128, 16], F32)
    p.brow = sb("brow", [1, 1056], F32)
    p.srow = sb("srow", [1, 1056], F32)
    p.cst = sb("cst", [128, 48], F32)
    p.ccol = sb("ccol", [128, 1], F32)
    p.epsc = sb("epsc", [1, 2], F32)
    p.cneg = sb("cneg", [128, 1], F32)
    p.rowscr = sb("rowscr", [1, 2048], BF16) if sC_ else None
    if sC_:
        p.mkall = p.scr[:, 1024:1152]
        p.browb = p.rowscr[0:1, 0:1536]
        p.mrowc = p.rowscr[0:1, 1536:2048]
        p.mrowst = p.rowscr[0:1, 0:2048]

    p.kfm = [lambda s=s: None for s in range(2)]
    p.pt = [p.ubuf[:, 0:1024], p.ubuf[:, 1024:2048]]
    p.vtb = p.ubuf[:, 0:2080]
    p.mkbc = p.ubuf[:, 0:512]
    p.b2row = p.attB[0:1, 0:1024]
    p.kfm = [p.locsl[:, 0:2048], p.locsl[:, 2048:4096]]
    p.attb_guard = 0
    p.pv_guard = 0
    p.pend = None
    p.norm_mults_cnt = 0
    p.lg = p.scr[0:8, 512:1024]
    p.lr = p.scr[:, 1024:1536]
    p.lr2 = p.scr[:, 1536:2048]
    vslf = p.vsl[:].bitcast(F32)
    p.lnr = vslf[0:1, 0:2048]
    p.lnmr = vslf[0:1, 2048:4096]
    p.rmsr = vslf[0:1, 2048:4096]

    ps = lambda name, shape: st.enter_context(nc.psum_tensor(name, list(shape), F32))
    p.ps512 = [ps(f"pb{i}", [128, 512]) for i in range(4)]
    p.ps1k = [ps(f"pk{i}", [128, 1024]) for i in range(2)]

    sem = lambda name: st.enter_context(nc.semaphore(name))
    for k in CTRS:
        setattr(p, k, Ctr(sem(k)))

    p.block = st.enter_context(nc.Block())

    dbg = int(os.environ.get("KDBG", "99"))
    emit_load(p, x_in, sA_, sB_, sC_, part)
    if sA_:
        emit_weight_gain_cast(p, p.w_q, cols=1024)
        if dbg >= 1:
            emit_ln(p, rms_after=(dbg >= 2))
        if dbg >= 3:
            emit_qkv0(p)
        if not fused:
            emit_dump(p, [(p.qdump, p.qa)])
    if sB_:
        if not fused:
            emit_load_chunks(p, p.qdump_in, p.qa)
        if dbg >= 10:
            emit_cstats(p, layer=0)
        if dbg >= 11:
            emit_attention(p, layer=0)
        if dbg >= 12:
            emit_oproj(p, w=p.w_o, bias=None, inner_res=True)
        if dbg >= 13:
            emit_ffn0_build(p)
            emit_ln(p, rms_after=False)
        if dbg >= 14:
            emit_ffn0(p)
            emit_weight_cast_dma(p, p.Wq1, rows=D, cols=D)
        if dbg >= 15:
            emit_ln(p, rms_after=False)
        if dbg >= 16:
            emit_qkv1(p)
        if not fused:
            emit_dump(p, [(p.x1dump, p.xf), (p.q1dump, p.qa)])
    if sC_:
        if not fused:
            emit_load_chunks(p, p.x1_in, p.xf)
            emit_load_chunks(p, p.q1_in, p.qa)
        emit_cstats(p, layer=1)
        emit_attention(p, layer=1)
        emit_oproj(p, w=p.Wo1, bias="bo1")
        emit_ln(p, rms_after=False, persist=True)
        emit_moe(p)
        emit_dump(p, [(p.out, p.xf)], final=True)
    else:
        emit_tail_wait(p)

    st.close()
    return nc, inputs



def emit_load(p, x_in, sA_, sB_, sC_, part):
    @p.block.sync
    def _(S):
        if x_in is not None:
            for c in range(NDC):
                p.sD.inc(S.dma_start(out=p.xf[c][:], in_=x_in[128 * c:128 * (c + 1), :]), 16)
        if sA_:
            p.sD.inc(S.dma_start(out=p.bcol[:, 8:16], in_=p.normw[:]), 16)
        p.sD.inc(S.dma_start(out=p.identsb[:], in_=p.ident[:]), 16)
        if sB_:
            p.sD.inc(S.dma_start(out=p.killsb[:], in_=p.kill_t[:]), 16)
            p.sD.inc(S.dma_start(out=p.masks[:], in_=p.mask01[:]), 16)

    @p.block.vector
    def _(V):
        p.sV.inc(V.memset(p.onesf[:], 1.0))
        p.sV.inc(V.memset(p.onesb[:], 1.0))
        p.sV.inc(V.memset(p.epsc[0:1, 0:1], LN_EPS))
        p.sV.inc(V.memset(p.epsc[0:1, 1:2], RMS_EPS))


def emit_load_chunks(p, dram, chunks):
    @p.block.sync
    def _(S):
        for c in range(NDC):
            p.sD.inc(S.dma_start(out=chunks[c][:], in_=dram[128 * c:128 * (c + 1), :]), 16)


def emit_dump(p, pairs, final=False):
    s = _snap(p)

    @p.block.sync
    def _(S):
        _wait(p, S, s)
        for dram, chunks in pairs:
            for c in range(NDC):
                p.sD.inc(S.dma_start(out=dram[128 * c:128 * (c + 1), :], in_=chunks[c][:]), 16)
        S.wait_ge(p.sD.sem, p.sD.n)


def emit_tail_wait(p):
    s = _snap(p)

    @p.block.sync
    def _(S):
        _wait(p, S, s)
        S.wait_ge(p.sD.sem, p.sD.n)
        S.wait_ge(p.sG.sem, p.sG.n)


def emit_gather(p, src_ap, dst_ap, layer, which, g):
    need = p.sD.n

    @p.block.gpsimd
    def _(G):
        G.wait_ge(p.sD.sem, need)
        cnt = p.sC.inc(G.collective_compute(
            "AllGather", ALU.bypass,
            replica_groups=[[0, 1, 2, 3], [4, 5, 6, 7]],
            ins=[src_ap.opt()], outs=[dst_ap.opt()]), 1)
        (p.cc_k if which == "k" else p.cc_v)[layer][g] = cnt


def emit_ln(p, rms_after, persist=False):
    for qb in range(NQB):
        qs = slice(QB * qb, QB * (qb + 1))
        rr_row = p.lnr[0:1, qs] if persist else p.brow[0:1, 0:512]
        mr_row = p.lnmr[0:1, qs] if persist else p.brow[0:1, 512:1024]
        s0 = _snap(p)

        @p.block.scalar
        def _(A):
            _wait(p, A, s0)
            for c in range(NDC):
                if c >= 2:
                    A.wait_ge(p.sT.sem, s0["sT"] + (c - 1))
                p.sA.inc(A.activation(out=p.scr[:, 512 * (c % 2):512 * (c % 2 + 1)],
                                      in_=p.xf[c][:, qs], func=AF.Square))

        @p.block.tensor
        def _(TE):
            _wait(p, TE, s0)
            for c in range(NDC):
                TE.matmul(p.ps512[0][0:1, :], p.onesf[:, 0:1], p.xf[c][:, qs],
                          start=(c == 0), stop=(c == NDC - 1))
                TE.wait_ge(p.sA.sem, s0["sA"] + c + 1)
                p.sT.inc(TE.matmul(p.ps512[1][0:1, :], p.onesf[:, 0:1],
                                   p.scr[:, 512 * (c % 2):512 * (c % 2 + 1)],
                                   start=(c == 0), stop=(c == NDC - 1)))

        s1 = _snap(p)

        @p.block.vector
        def _(V):
            V.wait_ge(p.sT.sem, s1["sT"])
            rr = rr_row
            V.tensor_scalar_mul(p.srow[0:1, 0:512], p.ps512[0][0:1, :], 1.0 / D)
            VS(p, V, V.tensor_scalar_mul(p.srow[0:1, 512:1024], p.ps512[1][0:1, :], 1.0 / D))
            VS(p, V, V.tensor_tensor(rr, p.srow[0:1, 0:512], p.srow[0:1, 0:512], ALU.mult))
            p.sV.inc(V.tensor_tensor(rr, p.srow[0:1, 512:1024], rr, ALU.subtract))

        s2 = _snap(p)

        @p.block.scalar
        def _(A):
            A.wait_ge(p.sV.sem, s2["sV"])
            rr = rr_row
            AS(p, A, A.activation(out=rr, in_=rr, func=AF.Ln, bias=p.epsc[0:1, 0:1]))
            p.sA.inc(A.activation(out=rr, in_=rr, func=AF.Exp, scale=-0.5))

        s3 = _snap(p)

        @p.block.vector
        def _(V):
            V.wait_ge(p.sA.sem, s3["sA"])
            p.sV.inc(V.tensor_tensor(mr_row, p.srow[0:1, 0:512],
                                     rr_row, ALU.mult))

        s4 = _snap(p)

        @p.block.tensor
        def _(TE):
            TE.wait_ge(p.sV.sem, s4["sV"])
            TE.matmul(p.ps512[2][:], p.onesf[0:1, :], rr_row,
                      start=True, stop=True)
            p.sT.inc(TE.matmul(p.ps512[3][:], p.onesf[0:1, :], mr_row,
                               start=True, stop=True))

        s5 = _snap(p)

        @p.block.vector
        def _(V):
            V.wait_ge(p.sT.sem, s5["sT"])
            for c in range(NDC):
                V.tensor_tensor(p.scr[:, 0:512], p.xf[c][:, qs], p.ps512[2][:], ALU.mult)
                p.sV.inc(V.tensor_tensor(p.xn[c][:, qs], p.scr[:, 0:512],
                                         p.ps512[3][:], ALU.subtract))

    if rms_after:
        emit_rms(p)


def emit_rms(p):
    for qb in range(NQB):
        qs = slice(QB * qb, QB * (qb + 1))
        s0 = _snap(p)

        @p.block.scalar
        def _(A):
            _wait(p, A, s0)
            for c in range(NDC):
                if c >= 2:
                    A.wait_ge(p.sT.sem, s0["sT"] + (c - 1))
                p.sA.inc(A.activation(out=p.scr[:, 512 * (c % 2):512 * (c % 2 + 1)],
                                      in_=p.xn[c][:, qs], func=AF.Square))

        @p.block.tensor
        def _(TE):
            _wait(p, TE, s0)
            for c in range(NDC):
                TE.wait_ge(p.sA.sem, s0["sA"] + c + 1)
                p.sT.inc(TE.matmul(p.ps512[1][0:1, :], p.onesf[:, 0:1],
                                   p.scr[:, 512 * (c % 2):512 * (c % 2 + 1)],
                                   start=(c == 0), stop=(c == NDC - 1)))

        s1 = _snap(p)

        @p.block.vector
        def _(V):
            V.wait_ge(p.sT.sem, s1["sT"])
            p.sV.inc(V.tensor_scalar_mul(p.rmsr[0:1, qs], p.ps512[1][0:1, :], 1.0 / D))

        s2 = _snap(p)

        @p.block.scalar
        def _(A):
            A.wait_ge(p.sV.sem, s2["sV"])
            AS(p, A, A.activation(out=p.rmsr[0:1, qs], in_=p.rmsr[0:1, qs],
                                  func=AF.Ln, bias=p.epsc[0:1, 1:2]))
            p.sA.inc(A.activation(out=p.rmsr[0:1, qs], in_=p.rmsr[0:1, qs],
                                  func=AF.Exp, scale=-0.5))


def emit_weight_gain_cast(p, w, cols):
    s0 = _snap(p)
    locf = p.locsl[:].bitcast(F32)

    @p.block.sync
    def _(S):
        _wait(p, S, s0)
        for c in range(NDC):
            if c >= 1:
                S.wait_ge(p.sV.sem, s0["sV"] + c)
            p.sD.inc(S.dma_start(out=locf[:, 0:cols],
                                 in_=w[128 * c:128 * (c + 1), :]), 16)

    @p.block.vector
    def _(V):
        _wait(p, V, s0)
        for c in range(NDC):
            V.wait_ge(p.sD.sem, s0["sD"] + 16 * (c + 1))
            p.sV.inc(V.tensor_scalar_mul(p.wb[:, cols * c: cols * (c + 1)],
                                         locf[:, 0:cols], p.bcol[:, 8 + c:9 + c]))


def emit_weight_cast_dma(p, w, rows, cols, off=0, dst=None):
    s0 = _snap(p)
    dst = p.wb if dst is None else dst

    @p.block.gpsimd
    def _(G):
        _wait(p, G, s0)
        for c in range(rows // 128):
            p.sG.inc(G.dma_start(out=dst[:, off + cols * c: off + cols * (c + 1)],
                                 in_=w[128 * c:128 * (c + 1), :]), 16)


def emit_proj_all(p, wcols, ocs, outs, rope, bias_cols, wsrc=None):
    wsrc = p.wb if wsrc is None else wsrc
    """qb-outer projections: outs[i][:, qb] = evac(sum_c wb[c, ocs[i]].T @ xn[c][:, qb]).
    rope: evac multiplies by (rf * rr) built per qb in scr[:, 1536:2048].
    V incs per qb: (1 rope-mult if rope) + n evacs. T incs: n proj mms per qb."""
    s0 = _snap(p)
    n = len(ocs)
    vper = n + (1 if rope else 0)

    def v_after_evac(qb, i):
        return s0["sV"] + qb * vper + (1 if rope else 0) + i + 1

    if rope:
        @p.block.sync
        def _(S):
            _wait(p, S, s0)
            for qb in range(NQB):
                if qb >= 1:
                    S.wait_ge(p.sV.sem, v_after_evac(qb - 1, n - 1))
                p.sD.inc(S.dma_start(out=p.scr[:, 1536:2048],
                                     in_=p.rf[:, QB * qb:QB * (qb + 1)]), 16)

    @p.block.tensor
    def _(TE):
        _wait(p, TE, s0)
        for qb in range(NQB):
            if rope:
                TE.wait_ge(p.sD.sem, s0["sD"] + 16 * (qb + 1))
                if qb >= 1:
                    TE.wait_ge(p.sV.sem, s0["sV"] + (qb - 1) * vper + 1)
                TE.matmul(p.ps512[2][:], p.onesf[0:1, :],
                          p.rmsr[0:1, QB * qb:QB * (qb + 1)], start=True, stop=True)
            for i, oc in enumerate(ocs):
                idx = qb * n + i
                bank = p.ps512[idx % 2]
                if idx >= 2:
                    q2, i2 = (idx - 2) // n, (idx - 2) % n
                    TE.wait_ge(p.sV.sem, v_after_evac(q2, i2))
                for c in range(NDC):
                    mm = TE.matmul(bank[:], wsrc[:, wcols * c + 128 * oc: wcols * c + 128 * (oc + 1)],
                                   p.xn[c][:, QB * qb:QB * (qb + 1)],
                                   start=(c == 0), stop=(c == NDC - 1))
                p.sT.inc(mm)

    @p.block.vector
    def _(V):
        for qb in range(NQB):
            if rope:
                V.wait_ge(p.sD.sem, s0["sD"] + 16 * (qb + 1))
                V.wait_ge(p.sT.sem, s0["sT"] + qb * n + 1)
                p.sV.inc(V.tensor_tensor(p.scr[:, 1536:2048], p.scr[:, 1536:2048],
                                         p.ps512[2][:], ALU.mult))
            for i, oc in enumerate(ocs):
                idx = qb * n + i
                bank = p.ps512[idx % 2]
                V.wait_ge(p.sT.sem, s0["sT"] + idx + 1)
                qs = slice(QB * qb, QB * (qb + 1))
                if rope:
                    ins = V.tensor_tensor(outs[i][:, qs], bank[:], p.scr[:, 1536:2048],
                                          ALU.mult)
                elif bias_cols is not None:
                    ins = V.tensor_scalar_add(outs[i][:, qs], bank[:], bias_cols[i])
                else:
                    ins = V.tensor_copy(outs[i][:, qs], bank[:])
                p.sV.inc(ins)


def emit_vproj(p, groups, wcols, bias_row, v_loc, rms=False, layer=0, wsrc=None):
    wsrc = p.wb if wsrc is None else wsrc
    """Token-major V projection into v_loc slab [128, groups*16*130].
    rms: multiply outputs by per-token rr (transposed from p.rmsr)."""
    s0 = _snap(p)

    @p.block.vector
    def _(V):
        _wait(p, V, s0)
        p.sV.inc(V.memset(p.vtb.rearrange("p (l w) -> p l w", w=VW)[:, :, 64:65], 1.0))
        p.sV.inc(V.memset(p.vtb.rearrange("p (l w) -> p l w", w=VW)[:, :, 129:130], 1.0))

    if isinstance(v_loc, list):
        v_dst = lambda g: v_loc[g][:]
    else:
        v_dst = lambda g: v_loc[:, NTT * VW * g: NTT * VW * (g + 1)]
    vper = 2 if rms else 1
    for g in range(groups):
        s1 = _snap(p)

        @p.block.tensor
        def _(TE):
            _wait(p, TE, s1)
            for lt in range(NTT):
                bank = p.ps512[lt % 2]
                if rms:
                    if lt >= 1:
                        TE.wait_ge(p.sV.sem, s1["sV"] + vper * (lt - 1) + 1)
                    TE.transpose(p.ps512[2][:, 0:1],
                                 p.rmsr[0:1, 128 * lt:128 * (lt + 1)],
                                 p.identsb[0:1, 0:1])
                if lt >= 2:
                    TE.wait_ge(p.sV.sem, s1["sV"] + vper * (lt - 1))
                for c in range(NDC):
                    mm = TE.matmul(bank[:, 0:128],
                                   p.xn[c][:, 128 * lt:128 * (lt + 1)],
                                   wsrc[:, wcols * c + 128 * g: wcols * c + 128 * (g + 1)],
                                   start=(c == 0), stop=(c == NDC - 1) and bias_row is None)
                if bias_row is not None:
                    mm = TE.matmul(bank[:, 0:128], p.onesf[0:1, :],
                                   bias_row[0:1, 128 * g:128 * (g + 1)],
                                   start=False, stop=True)
                p.sT.inc(mm)

        @p.block.vector
        def _(V):
            for lt in range(NTT):
                bank = p.ps512[lt % 2]
                V.wait_ge(p.sT.sem, s1["sT"] + lt + 1)
                if rms:
                    p.sV.inc(V.tensor_copy(p.cst[:, 13:14], p.ps512[2][:, 0:1]))
                    V.wait_ge(p.sV.sem, p.sV.n)
                    V.tensor_scalar_mul(p.vtb[:, VW * lt: VW * lt + 64],
                                        bank[:, 0:64], p.cst[:, 13:14])
                    p.sV.inc(V.tensor_scalar_mul(p.vtb[:, VW * lt + 65: VW * lt + 129],
                                                 bank[:, 64:128], p.cst[:, 13:14]))
                else:
                    V.tensor_copy(p.vtb[:, VW * lt: VW * lt + 64], bank[:, 0:64])
                    p.sV.inc(V.tensor_copy(p.vtb[:, VW * lt + 65: VW * lt + 129],
                                           bank[:, 64:128]))
        s2 = _snap(p)

        @p.block.sync
        def _(S):
            _wait(p, S, s2)
            p.sD.inc(S.dma_start(out=v_dst(g), in_=p.vtb[:]), 16)

        if p.fused:
            emit_gather(p, v_dst(g), (p.vg0s if layer == 0 else p.vg1s)[g],
                        layer, "v", g)


def emit_norm2_max(p, tiles, dstcol, init):
    s0 = _snap(p)
    n = len(tiles)
    nj = n * NQB
    assert nj <= 32

    @p.block.scalar
    def _(A):
        _wait(p, A, s0)
        for i in range(n):
            for qb in range(NQB):
                j = i * NQB + qb
                if j >= 2:
                    A.wait_ge(p.sT.sem, s0["sT"] + (j - 1))
                p.sA.inc(A.activation(out=p.scr[:, 512 * (j % 2):512 * (j % 2 + 1)],
                                      in_=tiles[i][:, QB * qb:QB * (qb + 1)],
                                      func=AF.Square))

    @p.block.tensor
    def _(TE):
        _wait(p, TE, s0)
        for j in range(nj):
            TE.wait_ge(p.sA.sem, s0["sA"] + j + 1)
            if j >= 1:
                TE.wait_ge(p.sV.sem, s0["sV"] + j)
            p.sT.inc(TE.matmul(p.ps512[3][0:1, :], p.onesf[:, 0:1],
                               p.scr[:, 512 * (j % 2):512 * (j % 2 + 1)],
                               start=True, stop=True))

    @p.block.vector
    def _(V):
        for j in range(nj):
            V.wait_ge(p.sT.sem, s0["sT"] + j + 1)
            p.sV.inc(V.reduce_max(p.cst[0:1, 16 + j:17 + j], p.ps512[3][0:1, :], axis=AX))
        V.wait_ge(p.sV.sem, p.sV.n)
        VS(p, V, V.reduce_max(p.cst[0:1, 15:16], p.cst[0:1, 16:16 + nj], axis=AX))
        if init:
            p.sV.inc(V.tensor_copy(p.cst[0:1, dstcol:dstcol + 1], p.cst[0:1, 15:16]))
        else:
            p.sV.inc(V.tensor_max(p.cst[0:1, dstcol:dstcol + 1],
                                  p.cst[0:1, dstcol:dstcol + 1], p.cst[0:1, 15:16]))
        V.wait_ge(p.sV.sem, p.sV.n)


def emit_kstat_allreduce(p):
    s = _snap(p)

    @p.block.vector
    def _(V):
        _wait(p, V, s)
        p.sV.inc(V.memset(p.scr[:, 0:128], 0.0))
        p.sV.inc(V.tensor_copy(p.scr[0:1, 0:1], p.cst[0:1, 1:2]))

    s1 = _snap(p)

    @p.block.gpsimd
    def _(G):
        _wait(p, G, s1)
        p.sG.inc(G.dma_start(out=p.cc_in[:], in_=p.scr[:, 0:128]), 16)
        G.wait_ge(p.sG.sem, p.sG.n)
        p.sC.inc(G.collective_compute(
            "AllReduce", ALU.max,
            replica_groups=[[0, 1, 2, 3], [4, 5, 6, 7]],
            ins=[p.cc_in.opt()], outs=[p.cc_out.opt()]), 1)
    p.ar_cnt = p.sC.n


def emit_cstats(p, layer):
    dbg2 = int(os.environ.get("KDBG2", "99"))
    emit_norm2_max(p, tiles=[p.qa[c] for c in range(NDC)], dstcol=0, init=True)
    if p.fused:
        arC = p.ar_cnt

        @p.block.scalar
        def _(A):
            A.wait_ge(p.sC.sem, arC)
            p.sG.inc(A.dma_start(out=p.cst[0:1, 1:2], in_=p.cc_out[0:1, 0:1]), 16)
    if dbg2 < 1:
        s_ = _snap(p)

        @p.block.vector
        def _(V):
            _wait(p, V, s_)
            p.sV.inc(V.memset(p.cst[0:1, 1:2], 100.0))
        dbg2_done = True
    if not p.fused and dbg2 >= 1:
        kg = p.kg0 if layer == 0 else p.kg1
        rows = 1024 if layer == 0 else 4096
        for blk in range(min(dbg2, rows // 128)):
            s = _snap(p)

            @p.block.sync
            def _(S):
                _wait(p, S, s)
                p.sD.inc(S.dma_start(out=p.locsl[:, 0:T].bitcast(BF16),
                                     in_=kg[128 * blk:128 * (blk + 1), :]), 16)

            emit_norm2_max(p, tiles=[p.locsl[:, 0:T]], dstcol=1, init=(blk == 0))
    s2 = _snap(p)

    @p.block.scalar
    def _(A):
        _wait(p, A, s2)
        A.activation(out=p.cst[0:1, 2:3], in_=p.cst[0:1, 0:1], func=AF.Ln)
        p.sA.inc(A.activation(out=p.cst[0:1, 3:4], in_=p.cst[0:1, 1:2], func=AF.Ln))

    s3 = _snap(p)

    @p.block.vector
    def _(V):
        V.wait_ge(p.sA.sem, s3["sA"])
        VS(p, V, V.tensor_add(p.cst[0:1, 4:5], p.cst[0:1, 2:3], p.cst[0:1, 3:4]))
        p.sV.inc(V.tensor_scalar(p.cst[0:1, 4:5], p.cst[0:1, 4:5], 0.5,
                                 -float(np.log(8.0)), ALU.mult, ALU.add))

    s4 = _snap(p)

    @p.block.scalar
    def _(A):
        A.wait_ge(p.sV.sem, s4["sV"])
        p.sA.inc(A.activation(out=p.cst[0:1, 5:6], in_=p.cst[0:1, 4:5], func=AF.Exp))

    s5 = _snap(p)

    @p.block.tensor
    def _(TE):
        TE.wait_ge(p.sA.sem, s5["sA"])
        p.sT.inc(TE.matmul(p.ps512[3][:, 0:1], p.onesf[0:1, :], p.cst[0:1, 5:6],
                           start=True, stop=True))

    s6 = _snap(p)

    @p.block.vector
    def _(V):
        V.wait_ge(p.sT.sem, s6["sT"])
        VS(p, V, V.tensor_copy(p.ccol[:], p.ps512[3][:, 0:1]))
        p.sV.inc(V.tensor_scalar_mul(p.cneg[:], p.ccol[:], -1.0))
        if layer == 0:
            p.sV.inc(V.tensor_scalar(p.biastbl[:], p.killsb[:], KILL, p.ccol[:],
                                     ALU.mult, ALU.subtract))
        V.wait_ge(p.sV.sem, p.sV.n)


def emit_attention(p, layer):
    kg = (p.kg0 if layer == 0 else p.kg1) if not p.fused else None
    vg = (p.vg0 if layer == 0 else p.vg1) if not p.fused else None
    ngroups = 2 if layer == 0 else 8
    krows = 256 if layer == 0 else 1024
    sz = _snap(p)

    @p.block.vector
    def _(V):
        _wait(p, V, sz)
        p.sV.inc(V.memset(p.lr[:], 0.0))
        p.sV.inc(V.memset(p.lr2[:], 0.0))

    sbase = _snap(p)

    for grp in range(ngroups):
        s0 = _snap(p)

        @p.block.sync
        def _(S):
            if p.fused:
                S.wait_ge(p.sD.sem, p.sD.n)
                S.wait_ge(p.sT.sem, s0["sT"])
                S.wait_ge(p.sC.sem, max(p.cc_k[layer][grp], p.cc_v[layer][grp]))
                p.sD.inc(S.dma_start(
                    out=p.wb[:].rearrange("p (r n) -> p r n", r=4),
                    in_=(p.kg0s if layer == 0 else p.kg1s)[grp]
                        .rearrange("(r f) n -> f r n", f=128)), 16)
                p.sD.inc(S.dma_start(
                    out=p.vsl[:, 0:4 * NTT * VW].rearrange("p (r n) -> p r n", r=4),
                    in_=(p.vg0s if layer == 0 else p.vg1s)[grp]
                        .rearrange("(r p) n -> p r n", p=128)), 16)
            else:
                _wait(p, S, s0)
                S.wait_ge(p.sD.sem, p.sD.n)
                p.sD.inc(S.dma_start(
                    out=p.wb[:].rearrange("p (r n) -> p r n", r=4),
                    in_=kg.rearrange("(r f) n -> f r n", f=krows)[128 * grp:128 * (grp + 1)]), 16)
                p.sD.inc(S.dma_start(
                    out=p.vsl[:].rearrange("p (r n) -> p r n", r=4),
                    in_=vg.rearrange("(r p) n -> p r n", p=128)[:, :, NTT * VW * grp: NTT * VW * (grp + 1)]), 16)
            if layer == 0:
                p.sD.inc(S.dma_start(out=p.locsl[:, 0:2048],
                                     in_=p.k_loc0[128 * grp:128 * (grp + 1), :]), 16)
                vl0 = p.v_loc0s[grp][:] if p.fused \
                    else p.v_loc0[:, NTT * VW * grp: NTT * VW * (grp + 1)]
                p.sD.inc(S.dma_start(out=p.locsl[:, 2048:2048 + NTT * VW], in_=vl0), 16)

        hps = [grp] if layer == 1 else [grp, grp + 2, grp + 4, grp + 6]
        for hp in hps:
            for qb in range(NQB):
                emit_attn_block(p, layer, hp, qb, sbase)
    emit_norm_flush(p)


def emit_attn_block(p, layer, hp, qb, sbase):
    nloc = (4 * qb + 4) if layer == 0 else 0
    nglob = 48 if layer == 0 else NKT
    kts = list(range(nglob)) + list(range(NKT, NKT + nloc))
    R = len(kts)
    s0 = _snap(p)
    qs = slice(QB * qb, QB * (qb + 1))

    def k_lhsT(kt, half):
        if kt < NKT:
            src = p.wb[:, 2048 * (kt // 16) + 128 * (kt % 16): 2048 * (kt // 16) + 128 * (kt % 16) + 128]
        else:
            lt = kt - NKT
            src = p.locsl[:, 128 * lt:128 * (lt + 1)]
        return src[64 * half:64 * (half + 1), :]

    def v_lhsT(kt, half):
        if kt < NKT:
            base = 2080 * (kt // 16) + VW * (kt % 16)
            return p.vsl[:, base + 65 * half: base + 65 * (half + 1)]
        lt = kt - NKT
        base = 2048 + VW * lt
        return p.locsl[:, base + 65 * half: base + 65 * (half + 1)]

    def bias_ap(kt):
        if layer == 1 or kt >= NKT:
            return p.cneg[:]
        return p.biastbl[:, NKT * qb + kt: NKT * qb + kt + 1]

    def diag_mask(kt):
        if layer == 1 or kt < NKT:
            return None
        jj = (kt - NKT) - 4 * qb
        return jj if 0 <= jj <= 3 else None

    mask_rounds = {i: diag_mask(kts[i]) for i in range(R) if diag_mask(kts[i]) is not None}

    def pt_ready(i):
        if i in mask_rounds:
            nv = sum(1 for k_ in mask_rounds if k_ <= i)
            return (p.sV.sem, s0["sV"] + nv)
        return (p.sA.sem, s0["sA"] + i + 1)

    sSc = []

    @p.block.tensor
    def _(TE):
        TE.wait_ge(p.sD.sem, s0["sD"])
        if s0["sG"]:
            TE.wait_ge(p.sG.sem, s0["sG"])
        TE.wait_ge(p.sV.sem, sbase["sV"])
        for i in range(R):
            buf = i % 2
            if i == 0:
                TE.wait_ge(p.sA.sem, max(0, s0["sA"] - 1))
            else:
                TE.wait_ge(p.sA.sem, s0["sA"] + (i - 1) if i >= 2 else s0["sA"])
            kt = kts[i]
            TE.matmul(p.ps1k[buf][:, 0:512], k_lhsT(kt, 0), p.qa[hp][0:64, qs],
                      start=True, stop=True, tile_position=(0, 0))
            mm = TE.matmul(p.ps1k[buf][:, 512:1024], k_lhsT(kt, 1), p.qa[hp][64:128, qs],
                           start=True, stop=True, tile_position=(64, 0))
            sSc.append(p.sT.inc(mm))
            if i == 10 and p.pend is not None:
                _norm_bcast(p, TE)
            if i >= 1:
                _pv_round(p, TE, i - 1, kts, v_lhsT, R, pt_ready, s0)
        _pv_round(p, TE, R - 1, kts, v_lhsT, R, pt_ready, s0)
    lastpv = p.sT.n

    @p.block.scalar
    def _(A):
        A.wait_ge(p.sV.sem, sbase["sV"])
        A.wait_ge(p.sD.sem, s0["sD"])
        for i in range(R):
            buf = i % 2
            A.wait_ge(p.sT.sem, sSc[i])
            p.sA.inc(A.activation(out=p.pt[buf], in_=p.ps1k[buf][:], func=AF.Exp,
                                  scale=0.125, bias=bias_ap(kts[i])))

    if mask_rounds:
        @p.block.vector
        def _(V):
            for i, jj in sorted(mask_rounds.items()):
                V.wait_ge(p.sA.sem, s0["sA"] + i + 1)
                buf = i % 2
                msl = p.masks[:, 384 - 128 * jj: 896 - 128 * jj]
                V.tensor_tensor(p.pt[buf][:, 0:512], p.pt[buf][:, 0:512], msl, ALU.mult)
                p.sV.inc(V.tensor_tensor(p.pt[buf][:, 512:1024], p.pt[buf][:, 512:1024],
                                         msl, ALU.mult))

    _norm_mults(p)
    emit_attn_store(p, hp, qb, lastpv)


def _pv_round(p, TE, i, kts, v_lhsT, R, pt_ready, s0):
    buf = i % 2
    sem, val = pt_ready(i)
    TE.wait_ge(sem, val)
    kt = kts[i]
    first = (i == 0)
    last = (i == R - 1)
    if first:
        TE.wait_ge(p.sV.sem, p.pv_guard)
    TE.matmul(p.ps512[0][0:65, :], v_lhsT(kt, 0), p.pt[buf][:, 0:512],
              start=first, stop=last)
    mm = TE.matmul(p.ps512[1][0:65, :], v_lhsT(kt, 1), p.pt[buf][:, 512:1024],
                   start=first, stop=last)
    if last:
        p.sT.inc(mm)


def _norm_bcast(p, TE):
    pend = p.pend
    TE.wait_ge(p.sV.sem, pend["rec"])
    if p.norm_mults_cnt:
        TE.wait_ge(p.sV.sem, p.norm_mults_cnt)
    TE.matmul(p.ps512[2][:], p.onesf[:], p.lr[:], start=True, stop=True)
    pend["bcast"] = p.sT.inc(
        TE.matmul(p.ps512[3][:], p.onesf[:], p.lr2[:], start=True, stop=True))


def _norm_mults(p, flush=False):
    pend = p.pend
    if pend is None:
        return
    if flush:
        s = _snap(p)

        @p.block.tensor
        def _(TE):
            _wait(p, TE, s)
            _norm_bcast(p, TE)
    hp, qb, qs = pend["hp"], pend["qb"], pend["qs"]

    @p.block.vector
    def _(V):
        V.wait_ge(p.sT.sem, pend["bcast"])
        V.tensor_tensor(p.qa[hp][0:64, qs], p.qa[hp][0:64, qs],
                        p.ps512[2][0:64, :], ALU.mult)
        p.sV.inc(V.tensor_tensor(p.attB[:, qs], p.attB[:, qs],
                                 p.ps512[3][0:64, :], ALU.mult))
    p.norm_mults_cnt = p.sV.n
    if qb == NQB - 1:
        @p.block.sync
        def _(S):
            S.wait_ge(p.sV.sem, p.norm_mults_cnt)
            p.sD.inc(S.dma_start(out=p.qa[hp][64:128, :], in_=p.attB[:]), 16)
        p.attb_guard = p.sD.n
    p.pend = None


def emit_attn_store(p, hp, qb, lastpv):
    qs = slice(QB * qb, QB * (qb + 1))

    @p.block.vector
    def _(V):
        V.wait_ge(p.sT.sem, lastpv)
        if qb == 0 and p.attb_guard:
            V.wait_ge(p.sD.sem, p.attb_guard)
        V.tensor_copy(p.qa[hp][0:64, qs], p.ps512[0][0:64, :])
        V.tensor_copy(p.attB[:, qs], p.ps512[1][0:64, :])
        V.tensor_scalar_add(p.scr[64:65, 0:512], p.ps512[0][64:65, :], 1e-38)
        p.sV.inc(V.tensor_scalar_add(p.scr[64:65, 512:1024],
                                     p.ps512[1][64:65, :], 1e-38))
        p.pv_guard = p.sV.n
        V.reciprocal(p.lr[64:65, :], p.scr[64:65, 0:512])
        p.sV.inc(V.reciprocal(p.lr2[64:65, :], p.scr[64:65, 512:1024]))
    p.pend = {"hp": hp, "qb": qb, "qs": qs, "rec": p.sV.n, "bcast": None}


def emit_norm_flush(p):
    _norm_mults(p, flush=True)


def emit_oproj(p, w, bias, inner_res=False):
    emit_weight_cast_dma(p, w, rows=D, cols=D)
    s0 = _snap(p)
    if bias is not None:
        @p.block.sync
        def _(S):
            p.sD.inc(S.dma_start(out=p.bcol[:, 0:8], in_=getattr(p, bias)[:]), 16)
    s1 = _snap(p)

    @p.block.tensor
    def _(TE):
        _wait(p, TE, s1)
        for oc in range(NDC):
            for qb in range(NQB):
                i = oc * NQB + qb
                bank = p.ps512[i % 2]
                if i >= 2:
                    TE.wait_ge(p.sV.sem, s1["sV"] + (i - 1))
                for c in range(NDC):
                    mm = TE.matmul(bank[:], p.wb[:, 1024 * c + 128 * oc: 1024 * c + 128 * (oc + 1)],
                                   p.qa[c][:, QB * qb:QB * (qb + 1)],
                                   start=(c == 0), stop=(c == NDC - 1))
                p.sT.inc(mm)

    @p.block.vector
    def _(V):
        for oc in range(NDC):
            for qb in range(NQB):
                i = oc * NQB + qb
                bank = p.ps512[i % 2]
                V.wait_ge(p.sT.sem, s1["sT"] + i + 1)
                qs = slice(QB * qb, QB * (qb + 1))
                if inner_res:
                    V.tensor_add(p.scr[:, 0:512], bank[:], p.xf[oc][:, qs])
                    ins = V.tensor_tensor(p.xf[oc][:, qs], p.scr[:, 0:512],
                                          p.xn[oc][:, qs], ALU.add)
                elif bias is None:
                    ins = V.tensor_add(p.xf[oc][:, qs], bank[:], p.xf[oc][:, qs])
                else:
                    ins = V.scalar_tensor_tensor(p.xf[oc][:, qs], bank[:],
                                                 p.bcol[:, oc:oc + 1], p.xf[oc][:, qs],
                                                 ALU.add, ALU.add)
                p.sV.inc(ins)


def emit_ffn0_build(p):
    wsl = p.vsl
    s0 = _snap(p)

    @p.block.sync
    def _(S):
        _wait(p, S, s0)
        p.sD.inc(S.dma_start(out=p.bcol[:, 0:8], in_=p.brbi[:]), 16)

    locf = p.locsl[:].bitcast(F32)
    s1 = _snap(p)

    @p.block.sync
    def _(S):
        _wait(p, S, s1)
        p.sD.inc(S.dma_start(out=p.scr[:, 0:1536], in_=p.fftt[:]), 16)
    ffD = p.sD.n

    @p.block.vector
    def _(V):
        _wait(p, V, s1)
        V.wait_ge(p.sD.sem, ffD)
        for sec in range(8):
            off = 0 if sec < 4 else 2048
            ch = sec % 4
            base = 192 * sec
            for c in range(4):
                a_b = p.scr[:, base + 32 * c: base + 32 * (c + 1)].unsqueeze(2).broadcast_to([128, 32, 16])
                b_b = p.scr[:, base + 128 + 16 * c: base + 128 + 16 * (c + 1)].unsqueeze(1).broadcast_to([128, 32, 16])
                if c == 0:
                    V.tensor_tensor(locf[:, 0:512].rearrange("p (b e) -> p b e", b=32),
                                    a_b, b_b, ALU.mult)
                else:
                    V.tensor_tensor(locf[:, 512:1024].rearrange("p (b e) -> p b e", b=32),
                                    a_b, b_b, ALU.mult)
                    V.tensor_add(locf[:, 0:512], locf[:, 0:512], locf[:, 512:1024])
            p.sV.inc(V.tensor_copy(wsl[:, off + 512 * ch: off + 512 * (ch + 1)],
                                   locf[:, 0:512]))

    s3 = _snap(p)

    @p.block.vector
    def _(V):
        _wait(p, V, s3)
        for ch in range(4):
            p.sV.inc(V.tensor_scalar_mul(wsl[:, 4096 + 512 * ch: 4096 + 512 * (ch + 1)],
                                         wsl[:, 2048 + 512 * ch: 2048 + 512 * (ch + 1)], -1.0))


def emit_ffn0(p):
    wsl = p.vsl
    s4 = _snap(p)

    @p.block.tensor
    def _(TE):
        _wait(p, TE, s4)
        for oc in range(8):
            real = oc < 4
            o = oc if real else oc - 4
            for qb in range(NQB):
                i = oc * NQB + qb
                bank = p.ps512[i % 2]
                if i >= 2:
                    TE.wait_ge(p.sV.sem, s4["sV"] + (i - 1))
                qsl = slice(QB * qb, QB * (qb + 1))
                for ic in range(4):
                    w_off = (0 if real else 2048) + 512 * ic + 128 * o
                    TE.matmul(bank[:], wsl[:, w_off:w_off + 128], p.xn[ic][:, qsl],
                              start=(ic == 0), stop=False)
                for ic in range(4):
                    w_off = (4096 if real else 0) + 512 * ic + 128 * o
                    mm = TE.matmul(bank[:], wsl[:, w_off:w_off + 128], p.xn[4 + ic][:, qsl],
                                   start=False, stop=(ic == 3))
                p.sT.inc(mm)

    @p.block.vector
    def _(V):
        for oc in range(8):
            for qb in range(NQB):
                i = oc * NQB + qb
                bank = p.ps512[i % 2]
                V.wait_ge(p.sT.sem, s4["sT"] + i + 1)
                qsl = slice(QB * qb, QB * (qb + 1))
                p.sV.inc(V.scalar_tensor_tensor(p.xf[oc][:, qsl], bank[:],
                                                p.bcol[:, oc:oc + 1], p.xf[oc][:, qsl],
                                                ALU.add, ALU.add))


def emit_qkv1(p):
    s0 = _snap(p)

    @p.block.sync
    def _(S):
        _wait(p, S, s0)
        p.sD.inc(S.dma_start(out=p.bcol[:, 0:8], in_=p.bq1[:]), 16)
        p.sD.inc(S.dma_start(out=p.bcol[:, 8:16], in_=p.bk1[:]), 16)
        p.sD.inc(S.dma_start(out=p.brow[0:1, 0:1024], in_=p.bv1[:]), 16)

    emit_weight_cast_dma(p, p.Wk1, rows=D, cols=D, dst=p.vsl)
    emit_proj_all(p, wcols=1024, ocs=list(range(NDC)), outs=p.qa, rope=False,
                  bias_cols=[p.bcol[:, oc:oc + 1] for oc in range(NDC)])
    emit_weight_cast_dma(p, p.Wv1, rows=D, cols=D)
    for pair in range(4):
        ocs = [2 * pair, 2 * pair + 1]
        emit_proj_all(p, wcols=1024, ocs=ocs, outs=p.kfm, rope=False,
                      bias_cols=[p.bcol[:, 8 + oc:9 + oc] for oc in ocs], wsrc=p.vsl)
        emit_norm2_max(p, tiles=[p.kfm[0], p.kfm[1]], dstcol=1, init=(pair == 0))
        if pair == 3 and p.fused:
            emit_kstat_allreduce(p)
        s2 = _snap(p)

        @p.block.sync
        def _(S):
            _wait(p, S, s2)
            for j in range(2):
                oc = 2 * pair + j
                p.sD.inc(S.dma_start(out=p.k_loc1[128 * oc:128 * (oc + 1), :],
                                     in_=p.kfm[j][:]), 16)

        if p.fused:
            for j in range(2):
                g = 2 * pair + j
                emit_gather(p, p.k_loc1[128 * g:128 * (g + 1), :], p.kg1s[g], 1, "k", g)

    emit_vproj(p, groups=8, wcols=1024, bias_row=p.brow,
               v_loc=p.v_loc1s if p.fused else p.v_loc1, layer=1)


def emit_qkv0(p):
    dbg = int(os.environ.get("KDBG", "99"))
    if dbg >= 4:
        emit_proj_all(p, wcols=1024, ocs=list(range(NDC)), outs=p.qa, rope=True, bias_cols=None)
    if dbg < 6:
        return
    emit_weight_gain_cast(p, p.w_k, cols=256)
    emit_proj_all(p, wcols=256, ocs=[0, 1], outs=p.kfm, rope=True, bias_cols=None)
    s1 = _snap(p)

    @p.block.sync
    def _(S):
        _wait(p, S, s1)
        for oc in range(2):
            p.sD.inc(S.dma_start(out=p.k_loc0[128 * oc:128 * (oc + 1), :], in_=p.kfm[oc][:]), 16)

    if p.fused:
        for g in range(2):
            emit_gather(p, p.k_loc0[128 * g:128 * (g + 1), :], p.kg0s[g], 0, "k", g)

    emit_norm2_max(p, tiles=[p.kfm[0], p.kfm[1]], dstcol=1, init=True)
    if p.fused:
        emit_kstat_allreduce(p)
    if int(os.environ.get("KDBG", "99")) < 7:
        return
    emit_weight_gain_cast(p, p.w_v, cols=256)
    emit_vproj(p, groups=2, wcols=256, bias_row=None,
               v_loc=p.v_loc0s if p.fused else p.v_loc0, rms=True, layer=0)


def emit_moe(p):
    s0 = _snap(p)

    @p.block.sync
    def _(S):
        _wait(p, S, s0)
        for c in range(NDC):
            p.sD.inc(S.dma_start(out=p.wtmp[:, 8 * c:8 * (c + 1)],
                                 in_=p.gwT[128 * c:128 * (c + 1), :]), 16)
        p.sD.inc(S.dma_start(out=p.brow[0:1, 1024:1032], in_=p.gbeb[0:1, :]), 16)
        p.sD.inc(S.dma_start(out=p.brow[0:1, 1032:1040], in_=p.gbeb[1:2, :]), 16)

    s1 = _snap(p)

    @p.block.tensor
    def _(TE):
        _wait(p, TE, s1)
        for c in range(NDC):
            p.sT.inc(TE.matmul(p.ps512[3][0:1, 0:8], p.onesf[:, 0:1],
                               p.wtmp[:, 8 * c:8 * (c + 1)],
                               start=(c == 0), stop=(c == NDC - 1)))

    s2 = _snap(p)

    @p.block.vector
    def _(V):
        V.wait_ge(p.sT.sem, s2["sT"])
        p.sV.inc(V.tensor_copy(p.srow[0:1, 1024:1032], p.ps512[3][0:1, 0:8]))
        VS(p, V, V.tensor_add(p.srow[0:1, 1032:1040], p.brow[0:1, 1024:1032],
                              p.brow[0:1, 1032:1040]))
        p.sV.inc(V.tensor_scalar_mul(p.srow[0:1, 1040:1048],
                                     p.srow[0:1, 1032:1040], -1.0))
        V.wait_ge(p.sV.sem, p.sV.n)

    for qb in range(NQB):
        qsl = slice(QB * qb, QB * (qb + 1))
        s3 = _snap(p)

        @p.block.tensor
        def _(TE):
            _wait(p, TE, s3)
            for c in range(NDC):
                p.sT.inc(TE.matmul(p.ps1k[0][0:8, 0:512], p.wtmp[:, 8 * c:8 * (c + 1)],
                                   p.xf[c][:, qsl], start=(c == 0), stop=(c == NDC - 1)))

        s4 = _snap(p)

        @p.block.vector
        def _(V):
            V.wait_ge(p.sT.sem, s4["sT"])
            p.sV.inc(V.tensor_copy(p.lg, p.ps1k[0][0:8, 0:512]))

        for tt in range(4):
            gt = 4 * qb + tt
            s5 = _snap(p)

            @p.block.tensor
            def _(TE):
                _wait(p, TE, s5)
                TE.transpose(p.ps512[2][:, 0:8], p.lg[0:8, 128 * tt:128 * (tt + 1)],
                             p.identsb[0:8, 0:8])
                TE.transpose(p.ps512[3][:, 8:9],
                             p.lnr[0:1, qsl.start + 128 * tt: qsl.start + 128 * (tt + 1)],
                             p.identsb[0:1, 0:1])
                TE.matmul(p.ps512[3][:, 0:8],
                          p.lnmr[0:1, qsl.start + 128 * tt: qsl.start + 128 * (tt + 1)],
                          p.srow[0:1, 1024:1032], start=True, stop=False)
                p.sT.inc(TE.matmul(p.ps512[3][:, 0:8], p.onesf[0:1, :],
                                   p.srow[0:1, 1040:1048], start=False, stop=True))

            s6 = _snap(p)

            @p.block.vector
            def _(V):
                V.wait_ge(p.sT.sem, s6["sT"])
                VS(p, V, V.tensor_copy(p.cst[:, 10:11], p.ps512[3][:, 8:9]))
                VS(p, V, V.tensor_scalar_mul(p.scr[:, 0:8], p.ps512[2][:, 0:8],
                                             p.cst[:, 10:11]))
                VS(p, V, V.tensor_tensor(p.scr[:, 0:8], p.scr[:, 0:8],
                                         p.ps512[3][:, 0:8], ALU.subtract))
                VS(p, V, V.reduce_max(p.cst[:, 9:10], p.scr[:, 0:8], axis=AX))
                p.sV.inc(V.tensor_scalar(p.mkall[:, 8 * gt:8 * (gt + 1)], p.scr[:, 0:8],
                                         p.cst[:, 9:10], None, ALU.is_ge))

    s8 = _snap(p)
    for e in range(NEXP):
        s8e = _snap(p)

        @p.block.tensor
        def _(TE):
            _wait(p, TE, s8e)
            for gt in range(NTT):
                if gt >= 1:
                    TE.wait_ge(p.sV.sem, s8e["sV"] + gt)
                p.sT.inc(TE.transpose(p.ps512[2][0:1, 8:136],
                                      p.mkall[:, 8 * gt + e:8 * gt + e + 1],
                                      p.identsb[:]))

        @p.block.vector
        def _(V):
            for gt in range(NTT):
                V.wait_ge(p.sT.sem, s8e["sT"] + gt + 1)
                p.sV.inc(V.tensor_copy(p.mrowst[0:1, 128 * gt:128 * (gt + 1)],
                                       p.ps512[2][0:1, 8:136]))

        s8f = _snap(p)

        @p.block.sync
        def _(S):
            _wait(p, S, s8f)
            p.sD.inc(S.dma_start(out=p.mrow1d[e:e + 1, :], in_=p.mrowst), 16)

    s_moe = _snap(p)
    hT = {}
    hA = {}
    yT = {}
    yV = {}
    hbanks = [p.ps1k[0][:, 0:512], p.ps1k[0][:, 512:1024],
              p.ps1k[1][:, 0:512], p.ps1k[1][:, 512:1024]]
    w1slab = [p.wb[:, 0:4096], p.wb[:, 4096:8192]]
    w2slab = [p.vsl[:, 0:4096], p.vsl[:, 4096:8192]]
    htmp = [p.locsl[:, 0:512], p.locsl[:, 512:1024]]
    mkb = p.ubuf[:, 0:2048]

    for e in range(NEXP):
        se0 = _snap(p)

        @p.block.sync
        def _(S):
            _wait(p, S, se0)
            p.sD.inc(S.dma_start(out=p.mrowst, in_=p.mrow1d[e:e + 1, :]), 16)
            p.sD.inc(S.dma_start(out=p.wtmp[:, 64:96], in_=p.b1c[e]), 16)
        b1D = p.sD.n

        se1 = _snap(p)

        @p.block.tensor
        def _(TE):
            _wait(p, TE, se1)
            for j in range(4):
                if j >= 1:
                    TE.wait_ge(p.sV.sem, se1["sV"] + j)
                p.sT.inc(TE.matmul(hbanks[3], p.onesb[0:1, 0:128],
                                   p.mrowst[0:1, 512 * j:512 * (j + 1)],
                                   start=True, stop=True))

        @p.block.vector
        def _(V):
            for j in range(4):
                V.wait_ge(p.sT.sem, se1["sT"] + j + 1)
                p.sV.inc(V.tensor_copy(mkb[:, 512 * j:512 * (j + 1)], hbanks[3]))
        bcV = p.sV.n

        for fg in range(8):
            t = e * 8 + fg
            sl = t % 2

            @p.block.gpsimd
            def _(G):
                if t >= 2:
                    G.wait_ge(p.sT.sem, yT[t - 2])
                else:
                    _wait(p, G, s_moe)
                for c in range(NDC):
                    p.sG.inc(G.dma_start(
                        out=w1slab[sl][:, 512 * c:512 * (c + 1)],
                        in_=p.w1[e, 128 * c:128 * (c + 1), 512 * fg:512 * (fg + 1)]), 16)
            w1D = p.sG.n

            @p.block.gpsimd
            def _(G):
                for sf in range(4):
                    p.sG.inc(G.dma_start(
                        out=w2slab[sl][:, 1024 * sf:1024 * (sf + 1)],
                        in_=p.w2[e, 512 * fg + 128 * sf: 512 * fg + 128 * (sf + 1), :]), 16)
                if fg == 1:
                    p.sG.inc(G.dma_start(out=p.b2row[:], in_=p.b2[e:e + 1, :]), 16)
            w2D = p.sG.n

            hTbase = p.sT.n

            @p.block.tensor
            def _(TE):
                if t == 0:
                    _wait(p, TE, s_moe)
                TE.wait_ge(p.sG.sem, w1D)
                for qb in range(NQB):
                    qsl = slice(QB * qb, QB * (qb + 1))
                    for sf in range(4):
                        j = qb * 4 + sf
                        if t >= 1:
                            TE.wait_ge(p.sA.sem, hA[t - 1] + max(0, j - 3))
                        if fg == 0 and j == 3:
                            TE.wait_ge(p.sV.sem, bcV)
                        for c in range(NDC):
                            mm = TE.matmul(
                                hbanks[j % 4],
                                w1slab[sl][:, 512 * c + 128 * sf: 512 * c + 128 * (sf + 1)],
                                p.xn[c][:, qsl], start=(c == 0), stop=(c == NDC - 1))
                        p.sT.inc(mm)
            hT[t] = p.sT.n

            hAbase = p.sA.n
            vA = p.sV.n

            @p.block.scalar
            def _(A):
                if t == 0:
                    _wait(p, A, s_moe)
                if fg == 0:
                    A.wait_ge(p.sD.sem, b1D)
                for j in range(16):
                    sf = j % 4
                    fc = 4 * fg + sf
                    A.wait_ge(p.sT.sem, hTbase + j + 1)
                    if j >= 2:
                        A.wait_ge(p.sV.sem, vA + j - 1)
                    p.sA.inc(A.activation(out=htmp[j % 2], in_=hbanks[j % 4],
                                          func=AF.Relu,
                                          bias=p.wtmp[:, 64 + fc:65 + fc]))
            hA[t] = p.sA.n

            @p.block.vector
            def _(V):
                if t < 2:
                    _wait(p, V, s_moe)
                else:
                    V.wait_ge(p.sT.sem, yT[t - 2])
                for j in range(16):
                    qb, sf = j // 4, j % 4
                    qsl = slice(QB * qb, QB * (qb + 1))
                    V.wait_ge(p.sA.sem, hAbase + j + 1)
                    p.sV.inc(V.tensor_tensor(p.qa[4 * sl + sf][:, qsl],
                                             htmp[j % 2], mkb[:, qsl], ALU.mult))

            yTbase = p.sT.n
            vY = p.sV.n

            @p.block.tensor
            def _(TE):
                TE.wait_ge(p.sG.sem, w2D)
                for qb in range(NQB):
                    qsl = slice(QB * qb, QB * (qb + 1))
                    TE.wait_ge(p.sV.sem, vA + 4 * qb + 4)
                    for dc in range(NDC):
                        g = qb * 8 + dc
                        bank = p.ps512[g % 4]
                        if g >= 4:
                            TE.wait_ge(p.sV.sem, vY + g - 3)
                        elif t >= 1:
                            TE.wait_ge(p.sV.sem, yV[t - 1])
                        for sf in range(4):
                            mm = TE.matmul(
                                bank[:],
                                w2slab[sl][:, 1024 * sf + 128 * dc: 1024 * sf + 128 * (dc + 1)],
                                p.qa[4 * sl + sf][:, qsl],
                                start=(sf == 0), stop=(sf == 3 and fg != 7))
                        if fg == 7:
                            mm = TE.matmul(bank[:], p.b2row[0:1, 128 * dc:128 * (dc + 1)],
                                           p.mrowst[0:1, qsl], start=False, stop=True)
                        p.sT.inc(mm)
            yT[t] = p.sT.n

            @p.block.vector
            def _(V):
                for qb in range(NQB):
                    qsl = slice(QB * qb, QB * (qb + 1))
                    for dc in range(NDC):
                        g = qb * 8 + dc
                        V.wait_ge(p.sT.sem, yTbase + g + 1)
                        p.sV.inc(V.tensor_add(p.xf[dc][:, qsl], p.ps512[g % 4][:],
                                              p.xf[dc][:, qsl]))
            yV[t] = p.sV.n



_NC_CACHE = {}


def _get_prog(part):
    if part not in _NC_CACHE:
        _NC_CACHE[part] = build_program(part)
    return _NC_CACHE[part]


def _per_core_shared(inp):
    aR, bR = tt_compact(inp["L0_ffn_G1_R"], inp["L0_ffn_G2_R"])
    aI, bI = tt_compact(inp["L0_ffn_G1_I"], inp["L0_ffn_G2_I"])
    f32 = lambda a: np.ascontiguousarray(np.asarray(a), dtype=np.float32)
    colsof = lambda v: np.ascontiguousarray(f32(v).reshape(8, 128).T)
    shared = {
        "w_q": f32(inp["L0_attn_w_q"]), "w_k": f32(inp["L0_attn_w_k"]),
        "w_v": f32(inp["L0_attn_w_v"]), "w_o": f32(inp["L0_attn_w_o"]),
        "normw": colsof(inp["L0_attn_norm_w"]),
        "fftt": tt_pack(aR, bR, aI, bI),
        "brbi": np.ascontiguousarray(np.concatenate(
            [f32(inp["L0_ffn_bias_real"]).reshape(4, 128).T,
             f32(inp["L0_ffn_bias_imag"]).reshape(4, 128).T], axis=1)),
        "Wq1": f32(inp["L1_attn_Wq"]), "Wk1": f32(inp["L1_attn_Wk"]),
        "Wv1": f32(inp["L1_attn_Wv"]), "Wo1": f32(inp["L1_attn_Wo"]),
        "bq1": colsof(inp["L1_attn_bq"]), "bk1": colsof(inp["L1_attn_bk"]),
        "bv1": f32(inp["L1_attn_bv"]).reshape(1, D),
        "bo1": colsof(inp["L1_attn_bo"]),
        "gwT": np.ascontiguousarray(f32(inp["L1_ffn_gate_w"]).T),
        "gbeb": np.ascontiguousarray(np.stack(
            [f32(inp["L1_ffn_gate_b"]), f32(inp["L1_ffn_expert_bias"])])),
        "w1": f32(inp["L1_ffn_w1"]),
        "b1c": np.ascontiguousarray(
            f32(inp["L1_ffn_b1"]).reshape(NEXP, 32, 128).transpose(0, 2, 1)),
        "w2": f32(inp["L1_ffn_w2"]), "b2": f32(inp["L1_ffn_b2"]),
    }
    return shared


def kernel(**inp):
    x = np.asarray(inp["L0_attn_x"], dtype=np.float32)
    shared = _per_core_shared(inp)
    consts = [host_constants(c) for c in range(NCORES)]
    xs = []
    for c in range(NCORES):
        b, r = c // 4, c % 4
        xs.append(np.ascontiguousarray(x[b, 2048 * r:2048 * (r + 1), :].T))

    def pick(names, extra):
        maps = []
        for c in range(NCORES):
            m = {}
            for n in names:
                if n == "x":
                    m[n] = xs[c]
                elif n in ("rf", "kill", "mask01", "ident"):
                    m[n] = consts[c][n]
                elif n in shared:
                    m[n] = shared[n]
                elif n in extra:
                    m[n] = extra[n][c]
                else:
                    raise KeyError(n)
            maps.append(m)
        return maps

    if os.environ.get("KMODE", "fused") == "fused":
        ncF, namesF = _get_prog("ALL")
        res = run_bass_kernel_spmd(ncF, pick(namesF, {}), core_ids=list(range(NCORES)))
        out = np.empty((2, SEQ, D), dtype=np.float32)
        for c in range(NCORES):
            b, r = c // 4, c % 4
            out[b, 2048 * r:2048 * (r + 1), :] = res.results[c]["out"].T
        return out

    ncA, namesA = _get_prog("A")
    resA = run_bass_kernel_spmd(ncA, pick(namesA, {}), core_ids=list(range(NCORES)))
    rA = resA.results
    kg0 = [np.concatenate([rA[g * 4 + i]["k_loc0"] for i in range(4)], axis=0)
           for g in range(2)]
    vg0 = [np.concatenate([rA[g * 4 + i]["v_loc0"] for i in range(4)], axis=0)
           for g in range(2)]
    extraB = {
        "qdump": [rA[c]["qdump"] for c in range(NCORES)],
        "kg0": [kg0[c // 4] for c in range(NCORES)],
        "vg0": [vg0[c // 4] for c in range(NCORES)],
        "k_loc0": [rA[c]["k_loc0"] for c in range(NCORES)],
        "v_loc0": [rA[c]["v_loc0"] for c in range(NCORES)],
    }
    ncB, namesB = _get_prog("B")
    resB = run_bass_kernel_spmd(ncB, pick(namesB, extraB), core_ids=list(range(NCORES)))
    rB = resB.results
    kg1 = [np.concatenate([rB[g * 4 + i]["k_loc1"] for i in range(4)], axis=0)
           for g in range(2)]
    vg1 = [np.concatenate([rB[g * 4 + i]["v_loc1"] for i in range(4)], axis=0)
           for g in range(2)]
    extraC = {
        "x1": [rB[c]["x1"] for c in range(NCORES)],
        "q1": [rB[c]["q1"] for c in range(NCORES)],
        "kg1": [kg1[c // 4] for c in range(NCORES)],
        "vg1": [vg1[c // 4] for c in range(NCORES)],
    }
    ncC, namesC = _get_prog("C")
    resC = run_bass_kernel_spmd(ncC, pick(namesC, extraC), core_ids=list(range(NCORES)))
    rC = resC.results

    out = np.empty((2, SEQ, D), dtype=np.float32)
    for c in range(NCORES):
        b, r = c // 4, c % 4
        out[b, 2048 * r:2048 * (r + 1), :] = rC[c]["out"].T
    return out


# version 13
# speedup vs baseline: 1.1116x; 1.1020x over previous
import contextlib
import os
import numpy as np
import ml_dtypes
import concourse.bass as bass
import concourse.mybir as mybir
from concourse.bass_utils import run_bass_kernel_spmd

F32 = mybir.dt.float32
BF16 = mybir.dt.bfloat16
AF = mybir.ActivationFunctionType
ALU = mybir.AluOpType
AX = mybir.AxisListType.X

NCORES = 8
T = 2048
D = 1024
NQB = 4
QB = 512
NTT = 16
NDC = 8
SEQ = 8192
NKT = 64
DFF = 4096
NEXP = 8
LN_EPS = 1e-5
RMS_EPS = 1.1920929e-07
KILL = -30000.0
VW = 130
L0_VW = 2 * NTT * VW
L1_VW = 8 * NTT * VW



def _rope_rf(n_total=SEQ, d_head=64):
    i = np.arange(d_head)
    theta = np.power(10000.0, -2.0 * (i // 2).astype(np.float64) / d_head)
    pos = np.arange(n_total, dtype=np.float64)
    ang = pos[:, None] * theta[None, :]
    sign = np.where(pos % 2 == 0, -1.0, 1.0)[:, None]
    return (np.cos(ang) + np.sin(ang) * sign).astype(np.float32)


def host_constants(core):
    r = core % 4
    p0 = 2048 * r
    out = {}
    rf = _rope_rf()[p0:p0 + T]
    out["rf"] = np.ascontiguousarray(np.tile(rf.T, (2, 1)))
    kill = np.zeros((NQB, NKT), dtype=np.float32)
    for qb in range(NQB):
        for kt in range(NKT):
            if kt >= 16 * r:
                kill[qb, kt] = 1.0
    out["kill"] = np.ascontiguousarray(
        np.broadcast_to(kill.reshape(1, NQB * NKT), (128, NQB * NKT))).astype(ml_dtypes.bfloat16)
    u = np.arange(896)[None, :]
    k = np.arange(128)[:, None]
    out["mask01"] = np.ascontiguousarray(
        (u >= k + 384).astype(np.float32)).astype(ml_dtypes.bfloat16)
    out["ident"] = np.eye(128, dtype=np.float32)
    return out


def tt_compact(G1, G2):
    A = np.empty((4, 512, 32), dtype=np.float32)
    B = np.empty((4, 512, 16), dtype=np.float32)
    for c in range(4):
        A[c] = np.repeat(np.asarray(G1)[0, :, c, :].T, 16, axis=0)
        B[c] = np.tile(np.asarray(G2)[c, :, 0, :].T, (32, 1))
    return A, B


def tt_pack(aR, bR, aI, bI):
    out = np.empty((128, 1536), dtype=np.float32)
    for sec in range(8):
        a_t, b_t = (aR, bR) if sec < 4 else (aI, bI)
        ch = sec % 4
        base = 192 * sec
        for c in range(4):
            out[:, base + 32 * c: base + 32 * (c + 1)] = a_t[c, 128 * ch:128 * (ch + 1), :]
            out[:, base + 128 + 16 * c: base + 128 + 16 * (c + 1)] = b_t[c, 128 * ch:128 * (ch + 1), :]
    return np.ascontiguousarray(out)



class Ctr:
    def __init__(self, sem):
        self.sem = sem
        self.n = 0

    def inc(self, instr, k=1):
        instr.then_inc(self.sem, k)
        self.n += k
        return self.n


class P:
    pass


CTRS = ["sD", "sG", "sT", "sA", "sV", "sC"]


def _snap(p):
    return {k: getattr(p, k).n for k in CTRS}


def _wait(p, eng, s):
    for k, v in s.items():
        if v > 0 and k != "sC":
            eng.wait_ge(getattr(p, k).sem, v)


def VS(p, V, instr):
    p.sV.inc(instr)
    V.wait_ge(p.sV.sem, p.sV.n)


def AS(p, A, instr):
    p.sA.inc(instr)
    A.wait_ge(p.sA.sem, p.sA.n)



def build_program(part):
    nc = bass.Bass("TRN2", target_bir_lowering=False, debug=False)
    p = P()
    p.nc = nc
    st = contextlib.ExitStack()
    sA_ = part in ("A", "ALL")
    sB_ = part in ("B", "ALL")
    sC_ = part in ("C", "ALL")
    fused = part == "ALL"
    p.fused = fused

    inputs = []

    def din(name, shape, dt=F32):
        inputs.append(name)
        return nc.dram_tensor(name, list(shape), dt, kind="ExternalInput").ap()

    def dout(name, shape, dt=F32):
        return nc.dram_tensor(name, list(shape), dt, kind="ExternalOutput").ap()

    def dint(name, shape, dt):
        return nc.dram_tensor(name, list(shape), dt, kind="Internal").ap()

    if sA_ or sB_ == (part == "B"):
        pass
    x_in = din("x", [D, T]) if (sA_ or part == "B") else None
    p.ident = din("ident", [128, 128])
    if sA_:
        p.rf = din("rf", [128, T])
        p.w_q = din("w_q", [D, D])
        p.w_k = din("w_k", [D, 256])
        p.w_v = din("w_v", [D, 256])
        p.normw = din("normw", [128, 8])
    if sB_:
        p.kill_t = din("kill", [128, NQB * NKT], BF16)
        p.mask01 = din("mask01", [128, 896], BF16)
        p.w_o = din("w_o", [D, D])
        p.fftt = din("fftt", [128, 1536])
        p.brbi = din("brbi", [128, 8])
        p.Wq1 = din("Wq1", [D, D])
        p.Wk1 = din("Wk1", [D, D])
        p.Wv1 = din("Wv1", [D, D])
        p.bq1 = din("bq1", [128, 8])
        p.bk1 = din("bk1", [128, 8])
        p.bv1 = din("bv1", [1, D])
    if sC_:
        p.Wo1 = din("Wo1", [D, D])
        p.bo1 = din("bo1", [128, 8])
        p.gwT = din("gwT", [D, 8])
        p.gbeb = din("gbeb", [2, 8])
        p.w1 = din("w1", [NEXP, D, DFF])
        p.b1c = din("b1c", [NEXP, 128, 32])
        p.w2 = din("w2", [NEXP, DFF, D])
        p.b2 = din("b2", [NEXP, D])
        p.mrow1d = dint("mrow1d", [NEXP, T], BF16)

    if fused:
        p.kg0s = [dint(f"kg0s{g}", [512, T], BF16) for g in range(2)]
        p.vg0s = [dint(f"vg0s{g}", [512, NTT * VW], BF16) for g in range(2)]
        p.kg1s = [dint(f"kg1s{g}", [512, T], BF16) for g in range(8)]
        p.vg1s = [dint(f"vg1s{g}", [512, NTT * VW], BF16) for g in range(8)]
        p.v_loc0s = [dint(f"v_loc0s{g}", [128, NTT * VW], BF16) for g in range(2)]
        p.v_loc1s = [dint(f"v_loc1s{g}", [128, NTT * VW], BF16) for g in range(8)]
        p.cc_k = {0: {}, 1: {}}
        p.cc_v = {0: {}, 1: {}}
    if sA_:
        if fused:
            p.k_loc0 = dint("k_loc0", [256, T], BF16)
            p.v_loc0 = dint("v_loc0", [128, L0_VW], BF16)
        else:
            p.k_loc0 = dout("k_loc0", [256, T], BF16)
            p.v_loc0 = dout("v_loc0", [128, L0_VW], BF16)
            p.qdump = dout("qdump", [D, T], BF16)
    if sB_:
        if fused:
            p.kg0 = dint("kg0", [4 * 256, T], BF16)
            p.vg0 = dint("vg0", [4 * 128, L0_VW], BF16)
        else:
            p.kg0 = din("kg0", [4 * 256, T], BF16)
            p.vg0 = din("vg0", [4 * 128, L0_VW], BF16)
            p.qdump_in = din("qdump", [D, T], BF16)
            p.k_loc0 = din("k_loc0", [256, T], BF16)
            p.v_loc0 = din("v_loc0", [128, L0_VW], BF16)
        if fused:
            p.k_loc1 = dint("k_loc1", [D, T], BF16)
            p.v_loc1 = dint("v_loc1", [128, L1_VW], BF16)
        else:
            p.k_loc1 = dout("k_loc1", [D, T], BF16)
            p.v_loc1 = dout("v_loc1", [128, L1_VW], BF16)
            p.x1dump = dout("x1", [D, T])
            p.q1dump = dout("q1", [D, T], BF16)
    if sC_:
        if fused:
            p.kg1 = dint("kg1", [4 * D, T], BF16)
            p.vg1 = dint("vg1", [4 * 128, L1_VW], BF16)
        else:
            p.kg1 = din("kg1", [4 * D, T], BF16)
            p.vg1 = din("vg1", [4 * 128, L1_VW], BF16)
            p.x1_in = din("x1", [D, T])
            p.q1_in = din("q1", [D, T], BF16)
        p.out = dout("out", [D, T])
    if fused:
        p.cc_in = dint("cc_in", [128, 128], F32)
        p.cc_out = dint("cc_out", [128, 128], F32)

    sb = lambda name, shape, dt: st.enter_context(nc.sbuf_tensor(name, list(shape), dt))
    p.xf = [sb(f"xf{c}", [128, T], F32) for c in range(NDC)]
    p.xn = [sb(f"xn{c}", [128, T], BF16) for c in range(NDC)]
    p.qa = [sb(f"qa{c}", [128, T], BF16) for c in range(NDC)]
    p.wb = sb("wb", [128, 8192], BF16)
    p.vsl = sb("vsl", [128, 8320], BF16)
    p.locsl = sb("locsl", [128, 4160], BF16)
    p.ubuf = sb("ubuf", [128, 4160], BF16)
    p.scr = sb("scr", [128, 2048], F32)
    p.wtmp = sb("wtmp", [128, 256], F32)
    p.attB = sb("attB", [64, T], BF16)
    p.rfsb = None
    p.masks = sb("masks", [128, 896], BF16) if sB_ else None
    p.killsb = sb("killsb", [128, 256], BF16) if sB_ else None
    p.biastbl = sb("biastbl", [128, 256], F32) if sB_ else None
    p.onesf = sb("onesf", [128, 128], F32)
    p.onesb = sb("onesb", [128, 512], BF16)
    p.identsb = sb("identsb", [128, 128], F32)

    p.bcol = sb("bcol", [128, 16], F32)
    p.brow = sb("brow", [1, 1056], F32)
    p.srow = sb("srow", [1, 1056], F32)
    p.cst = sb("cst", [128, 48], F32)
    p.ccol = sb("ccol", [128, 1], F32)
    p.epsc = sb("epsc", [1, 2], F32)
    p.cneg = sb("cneg", [128, 1], F32)
    p.rowscr = sb("rowscr", [1, 2048], BF16) if sC_ else None
    if sC_:
        p.mkall = p.scr[:, 1024:1152]
        p.browb = p.rowscr[0:1, 0:1536]
        p.mrowc = p.rowscr[0:1, 1536:2048]
        p.mrowst = p.rowscr[0:1, 0:2048]

    p.kfm = [lambda s=s: None for s in range(2)]
    p.pt = [p.ubuf[:, 0:1024], p.ubuf[:, 1024:2048]]
    p.vtb = p.ubuf[:, 0:2080]
    p.mkbc = p.ubuf[:, 0:512]
    p.b2row = p.attB[0:1, 0:1024]
    p.kfm = [p.locsl[:, 0:2048], p.locsl[:, 2048:4096]]
    p.attb_guard = 0
    p.pv_guard = 0
    p.pend = None
    p.norm_mults_cnt = 0
    p.lg = p.scr[0:8, 512:1024]
    p.lr = p.scr[:, 1024:1536]
    p.lr2 = p.scr[:, 1536:2048]
    vslf = p.vsl[:].bitcast(F32)
    p.lnr = vslf[0:1, 0:2048]
    p.lnmr = vslf[0:1, 2048:4096]
    p.rmsr = vslf[0:1, 2048:4096]

    ps = lambda name, shape: st.enter_context(nc.psum_tensor(name, list(shape), F32))
    p.ps512 = [ps(f"pb{i}", [128, 512]) for i in range(4)]
    p.ps1k = [ps(f"pk{i}", [128, 1024]) for i in range(2)]

    sem = lambda name: st.enter_context(nc.semaphore(name))
    for k in CTRS:
        setattr(p, k, Ctr(sem(k)))

    p.block = st.enter_context(nc.Block())

    dbg = int(os.environ.get("KDBG", "99"))
    emit_load(p, x_in, sA_, sB_, sC_, part)
    if sA_:
        emit_weight_gain_cast(p, p.w_q, cols=1024)
        if dbg >= 1:
            emit_ln(p, rms_after=(dbg >= 2))
        if dbg >= 3:
            emit_qkv0(p)
        if not fused:
            emit_dump(p, [(p.qdump, p.qa)])
    if sB_:
        if not fused:
            emit_load_chunks(p, p.qdump_in, p.qa)
        if dbg >= 10:
            emit_cstats(p, layer=0)
        if dbg >= 11:
            emit_attention(p, layer=0)
        if dbg >= 12:
            emit_oproj(p, w=p.w_o, bias=None, inner_res=True)
        if dbg >= 13:
            emit_ffn0_build(p)
            emit_ln(p, rms_after=False)
        if dbg >= 14:
            emit_ffn0(p)
            emit_weight_cast_dma(p, p.Wq1, rows=D, cols=D)
        if dbg >= 15:
            emit_ln(p, rms_after=False)
        if dbg >= 16:
            emit_qkv1(p)
        if not fused:
            emit_dump(p, [(p.x1dump, p.xf), (p.q1dump, p.qa)])
    if sC_:
        if not fused:
            emit_load_chunks(p, p.x1_in, p.xf)
            emit_load_chunks(p, p.q1_in, p.qa)
        emit_cstats(p, layer=1)
        emit_attention(p, layer=1)
        emit_oproj(p, w=p.Wo1, bias="bo1")
        emit_ln(p, rms_after=False, persist=True)
        emit_moe(p)
        emit_dump(p, [(p.out, p.xf)], final=True)
    else:
        emit_tail_wait(p)

    st.close()
    return nc, inputs



def emit_load(p, x_in, sA_, sB_, sC_, part):
    @p.block.sync
    def _(S):
        if x_in is not None:
            for c in range(NDC):
                p.sD.inc(S.dma_start(out=p.xf[c][:], in_=x_in[128 * c:128 * (c + 1), :]), 16)
        if sA_:
            p.sD.inc(S.dma_start(out=p.bcol[:, 8:16], in_=p.normw[:]), 16)
        p.sD.inc(S.dma_start(out=p.identsb[:], in_=p.ident[:]), 16)
        if sB_:
            p.sD.inc(S.dma_start(out=p.killsb[:], in_=p.kill_t[:]), 16)
            p.sD.inc(S.dma_start(out=p.masks[:], in_=p.mask01[:]), 16)

    @p.block.vector
    def _(V):
        p.sV.inc(V.memset(p.onesf[:], 1.0))
        p.sV.inc(V.memset(p.onesb[:], 1.0))
        p.sV.inc(V.memset(p.epsc[0:1, 0:1], LN_EPS))
        p.sV.inc(V.memset(p.epsc[0:1, 1:2], RMS_EPS))


def emit_load_chunks(p, dram, chunks):
    @p.block.sync
    def _(S):
        for c in range(NDC):
            p.sD.inc(S.dma_start(out=chunks[c][:], in_=dram[128 * c:128 * (c + 1), :]), 16)


def emit_dump(p, pairs, final=False):
    s = _snap(p)

    @p.block.sync
    def _(S):
        _wait(p, S, s)
        for dram, chunks in pairs:
            for c in range(NDC):
                p.sD.inc(S.dma_start(out=dram[128 * c:128 * (c + 1), :], in_=chunks[c][:]), 16)
        S.wait_ge(p.sD.sem, p.sD.n)


def emit_tail_wait(p):
    s = _snap(p)

    @p.block.sync
    def _(S):
        _wait(p, S, s)
        S.wait_ge(p.sD.sem, p.sD.n)
        S.wait_ge(p.sG.sem, p.sG.n)


def emit_gather(p, src_ap, dst_ap, layer, which, g):
    need = p.sD.n

    @p.block.gpsimd
    def _(G):
        G.wait_ge(p.sD.sem, need)
        cnt = p.sC.inc(G.collective_compute(
            "AllGather", ALU.bypass,
            replica_groups=[[0, 1, 2, 3], [4, 5, 6, 7]],
            ins=[src_ap.opt()], outs=[dst_ap.opt()]), 1)
        (p.cc_k if which == "k" else p.cc_v)[layer][g] = cnt


def emit_ln(p, rms_after, persist=False):
    for qb in range(NQB):
        qs = slice(QB * qb, QB * (qb + 1))
        rr_row = p.lnr[0:1, qs] if persist else p.brow[0:1, 0:512]
        mr_row = p.lnmr[0:1, qs] if persist else p.brow[0:1, 512:1024]
        s0 = _snap(p)

        @p.block.scalar
        def _(A):
            _wait(p, A, s0)
            for c in range(NDC):
                if c >= 2:
                    A.wait_ge(p.sT.sem, s0["sT"] + (c - 1))
                p.sA.inc(A.activation(out=p.scr[:, 512 * (c % 2):512 * (c % 2 + 1)],
                                      in_=p.xf[c][:, qs], func=AF.Square))

        @p.block.tensor
        def _(TE):
            _wait(p, TE, s0)
            for c in range(NDC):
                TE.matmul(p.ps512[0][0:1, :], p.onesf[:, 0:1], p.xf[c][:, qs],
                          start=(c == 0), stop=(c == NDC - 1))
                TE.wait_ge(p.sA.sem, s0["sA"] + c + 1)
                p.sT.inc(TE.matmul(p.ps512[1][0:1, :], p.onesf[:, 0:1],
                                   p.scr[:, 512 * (c % 2):512 * (c % 2 + 1)],
                                   start=(c == 0), stop=(c == NDC - 1)))

        s1 = _snap(p)

        @p.block.vector
        def _(V):
            V.wait_ge(p.sT.sem, s1["sT"])
            rr = rr_row
            V.tensor_scalar_mul(p.srow[0:1, 0:512], p.ps512[0][0:1, :], 1.0 / D)
            VS(p, V, V.tensor_scalar_mul(p.srow[0:1, 512:1024], p.ps512[1][0:1, :], 1.0 / D))
            VS(p, V, V.tensor_tensor(rr, p.srow[0:1, 0:512], p.srow[0:1, 0:512], ALU.mult))
            p.sV.inc(V.tensor_tensor(rr, p.srow[0:1, 512:1024], rr, ALU.subtract))

        s2 = _snap(p)

        @p.block.scalar
        def _(A):
            A.wait_ge(p.sV.sem, s2["sV"])
            rr = rr_row
            AS(p, A, A.activation(out=rr, in_=rr, func=AF.Ln, bias=p.epsc[0:1, 0:1]))
            p.sA.inc(A.activation(out=rr, in_=rr, func=AF.Exp, scale=-0.5))

        s3 = _snap(p)

        @p.block.vector
        def _(V):
            V.wait_ge(p.sA.sem, s3["sA"])
            p.sV.inc(V.tensor_tensor(mr_row, p.srow[0:1, 0:512],
                                     rr_row, ALU.mult))

        s4 = _snap(p)

        @p.block.tensor
        def _(TE):
            TE.wait_ge(p.sV.sem, s4["sV"])
            TE.matmul(p.ps512[2][:], p.onesf[0:1, :], rr_row,
                      start=True, stop=True)
            p.sT.inc(TE.matmul(p.ps512[3][:], p.onesf[0:1, :], mr_row,
                               start=True, stop=True))

        s5 = _snap(p)

        @p.block.vector
        def _(V):
            V.wait_ge(p.sT.sem, s5["sT"])
            for c in range(NDC):
                V.tensor_tensor(p.scr[:, 0:512], p.xf[c][:, qs], p.ps512[2][:], ALU.mult)
                p.sV.inc(V.tensor_tensor(p.xn[c][:, qs], p.scr[:, 0:512],
                                         p.ps512[3][:], ALU.subtract))

    if rms_after:
        emit_rms(p)


def emit_rms(p):
    for qb in range(NQB):
        qs = slice(QB * qb, QB * (qb + 1))
        s0 = _snap(p)

        @p.block.scalar
        def _(A):
            _wait(p, A, s0)
            for c in range(NDC):
                if c >= 2:
                    A.wait_ge(p.sT.sem, s0["sT"] + (c - 1))
                p.sA.inc(A.activation(out=p.scr[:, 512 * (c % 2):512 * (c % 2 + 1)],
                                      in_=p.xn[c][:, qs], func=AF.Square))

        @p.block.tensor
        def _(TE):
            _wait(p, TE, s0)
            for c in range(NDC):
                TE.wait_ge(p.sA.sem, s0["sA"] + c + 1)
                p.sT.inc(TE.matmul(p.ps512[1][0:1, :], p.onesf[:, 0:1],
                                   p.scr[:, 512 * (c % 2):512 * (c % 2 + 1)],
                                   start=(c == 0), stop=(c == NDC - 1)))

        s1 = _snap(p)

        @p.block.vector
        def _(V):
            V.wait_ge(p.sT.sem, s1["sT"])
            p.sV.inc(V.tensor_scalar_mul(p.rmsr[0:1, qs], p.ps512[1][0:1, :], 1.0 / D))

        s2 = _snap(p)

        @p.block.scalar
        def _(A):
            A.wait_ge(p.sV.sem, s2["sV"])
            AS(p, A, A.activation(out=p.rmsr[0:1, qs], in_=p.rmsr[0:1, qs],
                                  func=AF.Ln, bias=p.epsc[0:1, 1:2]))
            p.sA.inc(A.activation(out=p.rmsr[0:1, qs], in_=p.rmsr[0:1, qs],
                                  func=AF.Exp, scale=-0.5))


def emit_weight_gain_cast(p, w, cols):
    s0 = _snap(p)
    locf = p.locsl[:].bitcast(F32)

    @p.block.sync
    def _(S):
        _wait(p, S, s0)
        for c in range(NDC):
            if c >= 1:
                S.wait_ge(p.sV.sem, s0["sV"] + c)
            p.sD.inc(S.dma_start(out=locf[:, 0:cols],
                                 in_=w[128 * c:128 * (c + 1), :]), 16)

    @p.block.vector
    def _(V):
        _wait(p, V, s0)
        for c in range(NDC):
            V.wait_ge(p.sD.sem, s0["sD"] + 16 * (c + 1))
            p.sV.inc(V.tensor_scalar_mul(p.wb[:, cols * c: cols * (c + 1)],
                                         locf[:, 0:cols], p.bcol[:, 8 + c:9 + c]))


def emit_weight_cast_dma(p, w, rows, cols, off=0, dst=None):
    s0 = _snap(p)
    dst = p.wb if dst is None else dst

    @p.block.gpsimd
    def _(G):
        _wait(p, G, s0)
        for c in range(rows // 128):
            p.sG.inc(G.dma_start(out=dst[:, off + cols * c: off + cols * (c + 1)],
                                 in_=w[128 * c:128 * (c + 1), :]), 16)


def emit_proj_all(p, wcols, ocs, outs, rope, bias_cols, wsrc=None):
    wsrc = p.wb if wsrc is None else wsrc
    """qb-outer projections: outs[i][:, qb] = evac(sum_c wb[c, ocs[i]].T @ xn[c][:, qb]).
    rope: evac multiplies by (rf * rr) built per qb in scr[:, 1536:2048].
    V incs per qb: (1 rope-mult if rope) + n evacs. T incs: n proj mms per qb."""
    s0 = _snap(p)
    n = len(ocs)
    vper = n + (1 if rope else 0)

    def v_after_evac(qb, i):
        return s0["sV"] + qb * vper + (1 if rope else 0) + i + 1

    if rope:
        @p.block.sync
        def _(S):
            _wait(p, S, s0)
            for qb in range(NQB):
                if qb >= 1:
                    S.wait_ge(p.sV.sem, v_after_evac(qb - 1, n - 1))
                p.sD.inc(S.dma_start(out=p.scr[:, 1536:2048],
                                     in_=p.rf[:, QB * qb:QB * (qb + 1)]), 16)

    @p.block.tensor
    def _(TE):
        _wait(p, TE, s0)
        for qb in range(NQB):
            if rope:
                TE.wait_ge(p.sD.sem, s0["sD"] + 16 * (qb + 1))
                if qb >= 1:
                    TE.wait_ge(p.sV.sem, s0["sV"] + (qb - 1) * vper + 1)
                TE.matmul(p.ps512[2][:], p.onesf[0:1, :],
                          p.rmsr[0:1, QB * qb:QB * (qb + 1)], start=True, stop=True)
            for i, oc in enumerate(ocs):
                idx = qb * n + i
                bank = p.ps512[idx % 2]
                if idx >= 2:
                    q2, i2 = (idx - 2) // n, (idx - 2) % n
                    TE.wait_ge(p.sV.sem, v_after_evac(q2, i2))
                for c in range(NDC):
                    mm = TE.matmul(bank[:], wsrc[:, wcols * c + 128 * oc: wcols * c + 128 * (oc + 1)],
                                   p.xn[c][:, QB * qb:QB * (qb + 1)],
                                   start=(c == 0), stop=(c == NDC - 1))
                p.sT.inc(mm)

    @p.block.vector
    def _(V):
        for qb in range(NQB):
            if rope:
                V.wait_ge(p.sD.sem, s0["sD"] + 16 * (qb + 1))
                V.wait_ge(p.sT.sem, s0["sT"] + qb * n + 1)
                p.sV.inc(V.tensor_tensor(p.scr[:, 1536:2048], p.scr[:, 1536:2048],
                                         p.ps512[2][:], ALU.mult))
            for i, oc in enumerate(ocs):
                idx = qb * n + i
                bank = p.ps512[idx % 2]
                V.wait_ge(p.sT.sem, s0["sT"] + idx + 1)
                qs = slice(QB * qb, QB * (qb + 1))
                if rope:
                    ins = V.tensor_tensor(outs[i][:, qs], bank[:], p.scr[:, 1536:2048],
                                          ALU.mult)
                elif bias_cols is not None:
                    ins = V.tensor_scalar_add(outs[i][:, qs], bank[:], bias_cols[i])
                else:
                    ins = V.tensor_copy(outs[i][:, qs], bank[:])
                p.sV.inc(ins)


def emit_vproj(p, groups, wcols, bias_row, v_loc, rms=False, layer=0, wsrc=None):
    wsrc = p.wb if wsrc is None else wsrc
    """Token-major V projection into v_loc slab [128, groups*16*130].
    rms: multiply outputs by per-token rr (transposed from p.rmsr)."""
    s0 = _snap(p)

    @p.block.vector
    def _(V):
        _wait(p, V, s0)
        p.sV.inc(V.memset(p.vtb.rearrange("p (l w) -> p l w", w=VW)[:, :, 64:65], 1.0))
        p.sV.inc(V.memset(p.vtb.rearrange("p (l w) -> p l w", w=VW)[:, :, 129:130], 1.0))

    if isinstance(v_loc, list):
        v_dst = lambda g: v_loc[g][:]
    else:
        v_dst = lambda g: v_loc[:, NTT * VW * g: NTT * VW * (g + 1)]
    vper = 2 if rms else 1
    for g in range(groups):
        s1 = _snap(p)

        @p.block.tensor
        def _(TE):
            _wait(p, TE, s1)
            for lt in range(NTT):
                bank = p.ps512[lt % 2]
                if rms:
                    if lt >= 1:
                        TE.wait_ge(p.sV.sem, s1["sV"] + vper * (lt - 1) + 1)
                    TE.transpose(p.ps512[2][:, 0:1],
                                 p.rmsr[0:1, 128 * lt:128 * (lt + 1)],
                                 p.identsb[0:1, 0:1])
                if lt >= 2:
                    TE.wait_ge(p.sV.sem, s1["sV"] + vper * (lt - 1))
                for c in range(NDC):
                    mm = TE.matmul(bank[:, 0:128],
                                   p.xn[c][:, 128 * lt:128 * (lt + 1)],
                                   wsrc[:, wcols * c + 128 * g: wcols * c + 128 * (g + 1)],
                                   start=(c == 0), stop=(c == NDC - 1) and bias_row is None)
                if bias_row is not None:
                    mm = TE.matmul(bank[:, 0:128], p.onesf[0:1, :],
                                   bias_row[0:1, 128 * g:128 * (g + 1)],
                                   start=False, stop=True)
                p.sT.inc(mm)

        @p.block.vector
        def _(V):
            for lt in range(NTT):
                bank = p.ps512[lt % 2]
                V.wait_ge(p.sT.sem, s1["sT"] + lt + 1)
                if rms:
                    p.sV.inc(V.tensor_copy(p.cst[:, 13:14], p.ps512[2][:, 0:1]))
                    V.wait_ge(p.sV.sem, p.sV.n)
                    V.tensor_scalar_mul(p.vtb[:, VW * lt: VW * lt + 64],
                                        bank[:, 0:64], p.cst[:, 13:14])
                    p.sV.inc(V.tensor_scalar_mul(p.vtb[:, VW * lt + 65: VW * lt + 129],
                                                 bank[:, 64:128], p.cst[:, 13:14]))
                else:
                    V.tensor_copy(p.vtb[:, VW * lt: VW * lt + 64], bank[:, 0:64])
                    p.sV.inc(V.tensor_copy(p.vtb[:, VW * lt + 65: VW * lt + 129],
                                           bank[:, 64:128]))
        s2 = _snap(p)

        @p.block.sync
        def _(S):
            _wait(p, S, s2)
            p.sD.inc(S.dma_start(out=v_dst(g), in_=p.vtb[:]), 16)

        if p.fused:
            emit_gather(p, v_dst(g), (p.vg0s if layer == 0 else p.vg1s)[g],
                        layer, "v", g)


def emit_norm2_max(p, tiles, dstcol, init):
    s0 = _snap(p)
    n = len(tiles)
    nj = n * NQB
    assert nj <= 32

    @p.block.scalar
    def _(A):
        _wait(p, A, s0)
        for i in range(n):
            for qb in range(NQB):
                j = i * NQB + qb
                if j >= 2:
                    A.wait_ge(p.sT.sem, s0["sT"] + (j - 1))
                p.sA.inc(A.activation(out=p.scr[:, 512 * (j % 2):512 * (j % 2 + 1)],
                                      in_=tiles[i][:, QB * qb:QB * (qb + 1)],
                                      func=AF.Square))

    @p.block.tensor
    def _(TE):
        _wait(p, TE, s0)
        for j in range(nj):
            TE.wait_ge(p.sA.sem, s0["sA"] + j + 1)
            if j >= 1:
                TE.wait_ge(p.sV.sem, s0["sV"] + j)
            p.sT.inc(TE.matmul(p.ps512[3][0:1, :], p.onesf[:, 0:1],
                               p.scr[:, 512 * (j % 2):512 * (j % 2 + 1)],
                               start=True, stop=True))

    @p.block.vector
    def _(V):
        for j in range(nj):
            V.wait_ge(p.sT.sem, s0["sT"] + j + 1)
            p.sV.inc(V.reduce_max(p.cst[0:1, 16 + j:17 + j], p.ps512[3][0:1, :], axis=AX))
        V.wait_ge(p.sV.sem, p.sV.n)
        VS(p, V, V.reduce_max(p.cst[0:1, 15:16], p.cst[0:1, 16:16 + nj], axis=AX))
        if init:
            p.sV.inc(V.tensor_copy(p.cst[0:1, dstcol:dstcol + 1], p.cst[0:1, 15:16]))
        else:
            p.sV.inc(V.tensor_max(p.cst[0:1, dstcol:dstcol + 1],
                                  p.cst[0:1, dstcol:dstcol + 1], p.cst[0:1, 15:16]))
        V.wait_ge(p.sV.sem, p.sV.n)


def emit_kstat_allreduce(p):
    s = _snap(p)

    @p.block.vector
    def _(V):
        _wait(p, V, s)
        p.sV.inc(V.memset(p.scr[:, 0:128], 0.0))
        p.sV.inc(V.tensor_copy(p.scr[0:1, 0:1], p.cst[0:1, 1:2]))

    s1 = _snap(p)

    @p.block.gpsimd
    def _(G):
        _wait(p, G, s1)
        p.sG.inc(G.dma_start(out=p.cc_in[:], in_=p.scr[:, 0:128]), 16)
        G.wait_ge(p.sG.sem, p.sG.n)
        p.sC.inc(G.collective_compute(
            "AllReduce", ALU.max,
            replica_groups=[[0, 1, 2, 3], [4, 5, 6, 7]],
            ins=[p.cc_in.opt()], outs=[p.cc_out.opt()]), 1)
    arC = p.sC.n

    @p.block.scalar
    def _(A):
        A.wait_ge(p.sC.sem, arC)
        p.sG.inc(A.dma_start(out=p.cst[0:1, 1:2], in_=p.cc_out[0:1, 0:1]), 16)


def emit_cstats(p, layer):
    dbg2 = int(os.environ.get("KDBG2", "99"))
    emit_norm2_max(p, tiles=[p.qa[c] for c in range(NDC)], dstcol=0, init=True)
    if dbg2 < 1:
        s_ = _snap(p)

        @p.block.vector
        def _(V):
            _wait(p, V, s_)
            p.sV.inc(V.memset(p.cst[0:1, 1:2], 100.0))
        dbg2_done = True
    if not p.fused and dbg2 >= 1:
        kg = p.kg0 if layer == 0 else p.kg1
        rows = 1024 if layer == 0 else 4096
        for blk in range(min(dbg2, rows // 128)):
            s = _snap(p)

            @p.block.sync
            def _(S):
                _wait(p, S, s)
                p.sD.inc(S.dma_start(out=p.locsl[:, 0:T].bitcast(BF16),
                                     in_=kg[128 * blk:128 * (blk + 1), :]), 16)

            emit_norm2_max(p, tiles=[p.locsl[:, 0:T]], dstcol=1, init=(blk == 0))
    s2 = _snap(p)

    @p.block.scalar
    def _(A):
        _wait(p, A, s2)
        A.activation(out=p.cst[0:1, 2:3], in_=p.cst[0:1, 0:1], func=AF.Ln)
        p.sA.inc(A.activation(out=p.cst[0:1, 3:4], in_=p.cst[0:1, 1:2], func=AF.Ln))

    s3 = _snap(p)

    @p.block.vector
    def _(V):
        V.wait_ge(p.sA.sem, s3["sA"])
        VS(p, V, V.tensor_add(p.cst[0:1, 4:5], p.cst[0:1, 2:3], p.cst[0:1, 3:4]))
        p.sV.inc(V.tensor_scalar(p.cst[0:1, 4:5], p.cst[0:1, 4:5], 0.5,
                                 -float(np.log(8.0)), ALU.mult, ALU.add))

    s4 = _snap(p)

    @p.block.scalar
    def _(A):
        A.wait_ge(p.sV.sem, s4["sV"])
        p.sA.inc(A.activation(out=p.cst[0:1, 5:6], in_=p.cst[0:1, 4:5], func=AF.Exp))

    s5 = _snap(p)

    @p.block.tensor
    def _(TE):
        TE.wait_ge(p.sA.sem, s5["sA"])
        p.sT.inc(TE.matmul(p.ps512[3][:, 0:1], p.onesf[0:1, :], p.cst[0:1, 5:6],
                           start=True, stop=True))

    s6 = _snap(p)

    @p.block.vector
    def _(V):
        V.wait_ge(p.sT.sem, s6["sT"])
        VS(p, V, V.tensor_copy(p.ccol[:], p.ps512[3][:, 0:1]))
        p.sV.inc(V.tensor_scalar_mul(p.cneg[:], p.ccol[:], -1.0))
        if layer == 0:
            p.sV.inc(V.tensor_scalar(p.biastbl[:], p.killsb[:], KILL, p.ccol[:],
                                     ALU.mult, ALU.subtract))
        V.wait_ge(p.sV.sem, p.sV.n)


def emit_attention(p, layer):
    kg = (p.kg0 if layer == 0 else p.kg1) if not p.fused else None
    vg = (p.vg0 if layer == 0 else p.vg1) if not p.fused else None
    ngroups = 2 if layer == 0 else 8
    krows = 256 if layer == 0 else 1024
    sz = _snap(p)

    @p.block.vector
    def _(V):
        _wait(p, V, sz)
        p.sV.inc(V.memset(p.lr[:], 0.0))
        p.sV.inc(V.memset(p.lr2[:], 0.0))

    sbase = _snap(p)

    for grp in range(ngroups):
        s0 = _snap(p)

        @p.block.sync
        def _(S):
            if p.fused:
                S.wait_ge(p.sD.sem, p.sD.n)
                S.wait_ge(p.sT.sem, s0["sT"])
                S.wait_ge(p.sC.sem, max(p.cc_k[layer][grp], p.cc_v[layer][grp]))
                p.sD.inc(S.dma_start(
                    out=p.wb[:].rearrange("p (r n) -> p r n", r=4),
                    in_=(p.kg0s if layer == 0 else p.kg1s)[grp]
                        .rearrange("(r f) n -> f r n", f=128)), 16)
                p.sD.inc(S.dma_start(
                    out=p.vsl[:, 0:4 * NTT * VW].rearrange("p (r n) -> p r n", r=4),
                    in_=(p.vg0s if layer == 0 else p.vg1s)[grp]
                        .rearrange("(r p) n -> p r n", p=128)), 16)
            else:
                _wait(p, S, s0)
                S.wait_ge(p.sD.sem, p.sD.n)
                p.sD.inc(S.dma_start(
                    out=p.wb[:].rearrange("p (r n) -> p r n", r=4),
                    in_=kg.rearrange("(r f) n -> f r n", f=krows)[128 * grp:128 * (grp + 1)]), 16)
                p.sD.inc(S.dma_start(
                    out=p.vsl[:].rearrange("p (r n) -> p r n", r=4),
                    in_=vg.rearrange("(r p) n -> p r n", p=128)[:, :, NTT * VW * grp: NTT * VW * (grp + 1)]), 16)
            if layer == 0:
                p.sD.inc(S.dma_start(out=p.locsl[:, 0:2048],
                                     in_=p.k_loc0[128 * grp:128 * (grp + 1), :]), 16)
                vl0 = p.v_loc0s[grp][:] if p.fused \
                    else p.v_loc0[:, NTT * VW * grp: NTT * VW * (grp + 1)]
                p.sD.inc(S.dma_start(out=p.locsl[:, 2048:2048 + NTT * VW], in_=vl0), 16)

        hps = [grp] if layer == 1 else [grp, grp + 2, grp + 4, grp + 6]
        for hp in hps:
            for qb in range(NQB):
                emit_attn_block(p, layer, hp, qb, sbase)
    emit_norm_flush(p)


def emit_attn_block(p, layer, hp, qb, sbase):
    nloc = (4 * qb + 4) if layer == 0 else 0
    nglob = 48 if layer == 0 else NKT
    kts = list(range(nglob)) + list(range(NKT, NKT + nloc))
    R = len(kts)
    s0 = _snap(p)
    qs = slice(QB * qb, QB * (qb + 1))

    def k_lhsT(kt, half):
        if kt < NKT:
            src = p.wb[:, 2048 * (kt // 16) + 128 * (kt % 16): 2048 * (kt // 16) + 128 * (kt % 16) + 128]
        else:
            lt = kt - NKT
            src = p.locsl[:, 128 * lt:128 * (lt + 1)]
        return src[64 * half:64 * (half + 1), :]

    def v_lhsT(kt, half):
        if kt < NKT:
            base = 2080 * (kt // 16) + VW * (kt % 16)
            return p.vsl[:, base + 65 * half: base + 65 * (half + 1)]
        lt = kt - NKT
        base = 2048 + VW * lt
        return p.locsl[:, base + 65 * half: base + 65 * (half + 1)]

    def bias_ap(kt):
        if layer == 1 or kt >= NKT:
            return p.cneg[:]
        return p.biastbl[:, NKT * qb + kt: NKT * qb + kt + 1]

    def diag_mask(kt):
        if layer == 1 or kt < NKT:
            return None
        jj = (kt - NKT) - 4 * qb
        return jj if 0 <= jj <= 3 else None

    mask_rounds = {i: diag_mask(kts[i]) for i in range(R) if diag_mask(kts[i]) is not None}

    def pt_ready(i):
        if i in mask_rounds:
            nv = sum(1 for k_ in mask_rounds if k_ <= i)
            return (p.sV.sem, s0["sV"] + nv)
        return (p.sA.sem, s0["sA"] + i + 1)

    sSc = []

    @p.block.tensor
    def _(TE):
        TE.wait_ge(p.sD.sem, s0["sD"])
        if s0["sG"]:
            TE.wait_ge(p.sG.sem, s0["sG"])
        TE.wait_ge(p.sV.sem, sbase["sV"])
        for i in range(R):
            buf = i % 2
            if i == 0:
                TE.wait_ge(p.sA.sem, max(0, s0["sA"] - 1))
            else:
                TE.wait_ge(p.sA.sem, s0["sA"] + (i - 1) if i >= 2 else s0["sA"])
            kt = kts[i]
            TE.matmul(p.ps1k[buf][:, 0:512], k_lhsT(kt, 0), p.qa[hp][0:64, qs],
                      start=True, stop=True, tile_position=(0, 0))
            mm = TE.matmul(p.ps1k[buf][:, 512:1024], k_lhsT(kt, 1), p.qa[hp][64:128, qs],
                           start=True, stop=True, tile_position=(64, 0))
            sSc.append(p.sT.inc(mm))
            if i == 10 and p.pend is not None:
                _norm_bcast(p, TE)
            if i >= 1:
                _pv_round(p, TE, i - 1, kts, v_lhsT, R, pt_ready, s0)
        _pv_round(p, TE, R - 1, kts, v_lhsT, R, pt_ready, s0)
    lastpv = p.sT.n

    @p.block.scalar
    def _(A):
        A.wait_ge(p.sV.sem, sbase["sV"])
        A.wait_ge(p.sD.sem, s0["sD"])
        for i in range(R):
            buf = i % 2
            A.wait_ge(p.sT.sem, sSc[i])
            p.sA.inc(A.activation(out=p.pt[buf], in_=p.ps1k[buf][:], func=AF.Exp,
                                  scale=0.125, bias=bias_ap(kts[i])))

    if mask_rounds:
        @p.block.vector
        def _(V):
            for i, jj in sorted(mask_rounds.items()):
                V.wait_ge(p.sA.sem, s0["sA"] + i + 1)
                buf = i % 2
                msl = p.masks[:, 384 - 128 * jj: 896 - 128 * jj]
                V.tensor_tensor(p.pt[buf][:, 0:512], p.pt[buf][:, 0:512], msl, ALU.mult)
                p.sV.inc(V.tensor_tensor(p.pt[buf][:, 512:1024], p.pt[buf][:, 512:1024],
                                         msl, ALU.mult))

    _norm_mults(p)
    emit_attn_store(p, hp, qb, lastpv)


def _pv_round(p, TE, i, kts, v_lhsT, R, pt_ready, s0):
    buf = i % 2
    sem, val = pt_ready(i)
    TE.wait_ge(sem, val)
    kt = kts[i]
    first = (i == 0)
    last = (i == R - 1)
    if first:
        TE.wait_ge(p.sV.sem, p.pv_guard)
    TE.matmul(p.ps512[0][0:65, :], v_lhsT(kt, 0), p.pt[buf][:, 0:512],
              start=first, stop=last)
    mm = TE.matmul(p.ps512[1][0:65, :], v_lhsT(kt, 1), p.pt[buf][:, 512:1024],
                   start=first, stop=last)
    if last:
        p.sT.inc(mm)


def _norm_bcast(p, TE):
    pend = p.pend
    TE.wait_ge(p.sV.sem, pend["rec"])
    if p.norm_mults_cnt:
        TE.wait_ge(p.sV.sem, p.norm_mults_cnt)
    TE.matmul(p.ps512[2][:], p.onesf[:], p.lr[:], start=True, stop=True)
    pend["bcast"] = p.sT.inc(
        TE.matmul(p.ps512[3][:], p.onesf[:], p.lr2[:], start=True, stop=True))


def _norm_mults(p, flush=False):
    pend = p.pend
    if pend is None:
        return
    if flush:
        s = _snap(p)

        @p.block.tensor
        def _(TE):
            _wait(p, TE, s)
            _norm_bcast(p, TE)
    hp, qb, qs = pend["hp"], pend["qb"], pend["qs"]

    @p.block.vector
    def _(V):
        V.wait_ge(p.sT.sem, pend["bcast"])
        V.tensor_tensor(p.qa[hp][0:64, qs], p.qa[hp][0:64, qs],
                        p.ps512[2][0:64, :], ALU.mult)
        p.sV.inc(V.tensor_tensor(p.attB[:, qs], p.attB[:, qs],
                                 p.ps512[3][0:64, :], ALU.mult))
    p.norm_mults_cnt = p.sV.n
    if qb == NQB - 1:
        @p.block.sync
        def _(S):
            S.wait_ge(p.sV.sem, p.norm_mults_cnt)
            p.sD.inc(S.dma_start(out=p.qa[hp][64:128, :], in_=p.attB[:]), 16)
        p.attb_guard = p.sD.n
    p.pend = None


def emit_attn_store(p, hp, qb, lastpv):
    qs = slice(QB * qb, QB * (qb + 1))

    @p.block.vector
    def _(V):
        V.wait_ge(p.sT.sem, lastpv)
        if qb == 0 and p.attb_guard:
            V.wait_ge(p.sD.sem, p.attb_guard)
        V.tensor_copy(p.qa[hp][0:64, qs], p.ps512[0][0:64, :])
        V.tensor_copy(p.attB[:, qs], p.ps512[1][0:64, :])
        V.tensor_scalar_add(p.scr[64:65, 0:512], p.ps512[0][64:65, :], 1e-38)
        p.sV.inc(V.tensor_scalar_add(p.scr[64:65, 512:1024],
                                     p.ps512[1][64:65, :], 1e-38))
        p.pv_guard = p.sV.n
        V.reciprocal(p.lr[64:65, :], p.scr[64:65, 0:512])
        p.sV.inc(V.reciprocal(p.lr2[64:65, :], p.scr[64:65, 512:1024]))
    p.pend = {"hp": hp, "qb": qb, "qs": qs, "rec": p.sV.n, "bcast": None}


def emit_norm_flush(p):
    _norm_mults(p, flush=True)


def emit_oproj(p, w, bias, inner_res=False):
    emit_weight_cast_dma(p, w, rows=D, cols=D)
    s0 = _snap(p)
    if bias is not None:
        @p.block.sync
        def _(S):
            p.sD.inc(S.dma_start(out=p.bcol[:, 0:8], in_=getattr(p, bias)[:]), 16)
    s1 = _snap(p)

    @p.block.tensor
    def _(TE):
        _wait(p, TE, s1)
        for oc in range(NDC):
            for qb in range(NQB):
                i = oc * NQB + qb
                bank = p.ps512[i % 2]
                if i >= 2:
                    TE.wait_ge(p.sV.sem, s1["sV"] + (i - 1))
                for c in range(NDC):
                    mm = TE.matmul(bank[:], p.wb[:, 1024 * c + 128 * oc: 1024 * c + 128 * (oc + 1)],
                                   p.qa[c][:, QB * qb:QB * (qb + 1)],
                                   start=(c == 0), stop=(c == NDC - 1))
                p.sT.inc(mm)

    @p.block.vector
    def _(V):
        for oc in range(NDC):
            for qb in range(NQB):
                i = oc * NQB + qb
                bank = p.ps512[i % 2]
                V.wait_ge(p.sT.sem, s1["sT"] + i + 1)
                qs = slice(QB * qb, QB * (qb + 1))
                if inner_res:
                    V.tensor_add(p.scr[:, 0:512], bank[:], p.xf[oc][:, qs])
                    ins = V.tensor_tensor(p.xf[oc][:, qs], p.scr[:, 0:512],
                                          p.xn[oc][:, qs], ALU.add)
                elif bias is None:
                    ins = V.tensor_add(p.xf[oc][:, qs], bank[:], p.xf[oc][:, qs])
                else:
                    ins = V.scalar_tensor_tensor(p.xf[oc][:, qs], bank[:],
                                                 p.bcol[:, oc:oc + 1], p.xf[oc][:, qs],
                                                 ALU.add, ALU.add)
                p.sV.inc(ins)


def emit_ffn0_build(p):
    wsl = p.vsl
    s0 = _snap(p)

    @p.block.sync
    def _(S):
        _wait(p, S, s0)
        p.sD.inc(S.dma_start(out=p.bcol[:, 0:8], in_=p.brbi[:]), 16)

    locf = p.locsl[:].bitcast(F32)
    s1 = _snap(p)

    @p.block.sync
    def _(S):
        _wait(p, S, s1)
        p.sD.inc(S.dma_start(out=p.scr[:, 0:1536], in_=p.fftt[:]), 16)
    ffD = p.sD.n

    @p.block.vector
    def _(V):
        _wait(p, V, s1)
        V.wait_ge(p.sD.sem, ffD)
        for sec in range(8):
            off = 0 if sec < 4 else 2048
            ch = sec % 4
            base = 192 * sec
            for c in range(4):
                a_b = p.scr[:, base + 32 * c: base + 32 * (c + 1)].unsqueeze(2).broadcast_to([128, 32, 16])
                b_b = p.scr[:, base + 128 + 16 * c: base + 128 + 16 * (c + 1)].unsqueeze(1).broadcast_to([128, 32, 16])
                if c == 0:
                    V.tensor_tensor(locf[:, 0:512].rearrange("p (b e) -> p b e", b=32),
                                    a_b, b_b, ALU.mult)
                else:
                    V.tensor_tensor(locf[:, 512:1024].rearrange("p (b e) -> p b e", b=32),
                                    a_b, b_b, ALU.mult)
                    V.tensor_add(locf[:, 0:512], locf[:, 0:512], locf[:, 512:1024])
            p.sV.inc(V.tensor_copy(wsl[:, off + 512 * ch: off + 512 * (ch + 1)],
                                   locf[:, 0:512]))

    s3 = _snap(p)

    @p.block.vector
    def _(V):
        _wait(p, V, s3)
        for ch in range(4):
            p.sV.inc(V.tensor_scalar_mul(wsl[:, 4096 + 512 * ch: 4096 + 512 * (ch + 1)],
                                         wsl[:, 2048 + 512 * ch: 2048 + 512 * (ch + 1)], -1.0))


def emit_ffn0(p):
    wsl = p.vsl
    s4 = _snap(p)

    @p.block.tensor
    def _(TE):
        _wait(p, TE, s4)
        for oc in range(8):
            real = oc < 4
            o = oc if real else oc - 4
            for qb in range(NQB):
                i = oc * NQB + qb
                bank = p.ps512[i % 2]
                if i >= 2:
                    TE.wait_ge(p.sV.sem, s4["sV"] + (i - 1))
                qsl = slice(QB * qb, QB * (qb + 1))
                for ic in range(4):
                    w_off = (0 if real else 2048) + 512 * ic + 128 * o
                    TE.matmul(bank[:], wsl[:, w_off:w_off + 128], p.xn[ic][:, qsl],
                              start=(ic == 0), stop=False)
                for ic in range(4):
                    w_off = (4096 if real else 0) + 512 * ic + 128 * o
                    mm = TE.matmul(bank[:], wsl[:, w_off:w_off + 128], p.xn[4 + ic][:, qsl],
                                   start=False, stop=(ic == 3))
                p.sT.inc(mm)

    @p.block.vector
    def _(V):
        for oc in range(8):
            for qb in range(NQB):
                i = oc * NQB + qb
                bank = p.ps512[i % 2]
                V.wait_ge(p.sT.sem, s4["sT"] + i + 1)
                qsl = slice(QB * qb, QB * (qb + 1))
                p.sV.inc(V.scalar_tensor_tensor(p.xf[oc][:, qsl], bank[:],
                                                p.bcol[:, oc:oc + 1], p.xf[oc][:, qsl],
                                                ALU.add, ALU.add))


def emit_qkv1(p):
    s0 = _snap(p)

    @p.block.sync
    def _(S):
        _wait(p, S, s0)
        p.sD.inc(S.dma_start(out=p.bcol[:, 0:8], in_=p.bq1[:]), 16)
        p.sD.inc(S.dma_start(out=p.bcol[:, 8:16], in_=p.bk1[:]), 16)
        p.sD.inc(S.dma_start(out=p.brow[0:1, 0:1024], in_=p.bv1[:]), 16)

    emit_weight_cast_dma(p, p.Wk1, rows=D, cols=D, dst=p.vsl)
    emit_proj_all(p, wcols=1024, ocs=list(range(NDC)), outs=p.qa, rope=False,
                  bias_cols=[p.bcol[:, oc:oc + 1] for oc in range(NDC)])
    emit_weight_cast_dma(p, p.Wv1, rows=D, cols=D)
    for pair in range(4):
        ocs = [2 * pair, 2 * pair + 1]
        emit_proj_all(p, wcols=1024, ocs=ocs, outs=p.kfm, rope=False,
                      bias_cols=[p.bcol[:, 8 + oc:9 + oc] for oc in ocs], wsrc=p.vsl)
        emit_norm2_max(p, tiles=[p.kfm[0], p.kfm[1]], dstcol=1, init=(pair == 0))
        if pair == 3 and p.fused:
            emit_kstat_allreduce(p)
        s2 = _snap(p)

        @p.block.sync
        def _(S):
            _wait(p, S, s2)
            for j in range(2):
                oc = 2 * pair + j
                p.sD.inc(S.dma_start(out=p.k_loc1[128 * oc:128 * (oc + 1), :],
                                     in_=p.kfm[j][:]), 16)

        if p.fused:
            for j in range(2):
                g = 2 * pair + j
                emit_gather(p, p.k_loc1[128 * g:128 * (g + 1), :], p.kg1s[g], 1, "k", g)

    emit_vproj(p, groups=8, wcols=1024, bias_row=p.brow,
               v_loc=p.v_loc1s if p.fused else p.v_loc1, layer=1)


def emit_qkv0(p):
    dbg = int(os.environ.get("KDBG", "99"))
    if dbg >= 4:
        emit_proj_all(p, wcols=1024, ocs=list(range(NDC)), outs=p.qa, rope=True, bias_cols=None)
    if dbg < 6:
        return
    emit_weight_gain_cast(p, p.w_k, cols=256)
    emit_proj_all(p, wcols=256, ocs=[0, 1], outs=p.kfm, rope=True, bias_cols=None)
    s1 = _snap(p)

    @p.block.sync
    def _(S):
        _wait(p, S, s1)
        for oc in range(2):
            p.sD.inc(S.dma_start(out=p.k_loc0[128 * oc:128 * (oc + 1), :], in_=p.kfm[oc][:]), 16)

    if p.fused:
        for g in range(2):
            emit_gather(p, p.k_loc0[128 * g:128 * (g + 1), :], p.kg0s[g], 0, "k", g)

    emit_norm2_max(p, tiles=[p.kfm[0], p.kfm[1]], dstcol=1, init=True)
    if p.fused:
        emit_kstat_allreduce(p)
    if int(os.environ.get("KDBG", "99")) < 7:
        return
    emit_weight_gain_cast(p, p.w_v, cols=256)
    emit_vproj(p, groups=2, wcols=256, bias_row=None,
               v_loc=p.v_loc0s if p.fused else p.v_loc0, rms=True, layer=0)


def emit_moe(p):
    s0 = _snap(p)

    @p.block.sync
    def _(S):
        _wait(p, S, s0)
        for c in range(NDC):
            p.sD.inc(S.dma_start(out=p.wtmp[:, 8 * c:8 * (c + 1)],
                                 in_=p.gwT[128 * c:128 * (c + 1), :]), 16)
        p.sD.inc(S.dma_start(out=p.brow[0:1, 1024:1032], in_=p.gbeb[0:1, :]), 16)
        p.sD.inc(S.dma_start(out=p.brow[0:1, 1032:1040], in_=p.gbeb[1:2, :]), 16)

    s1 = _snap(p)

    @p.block.tensor
    def _(TE):
        _wait(p, TE, s1)
        for c in range(NDC):
            p.sT.inc(TE.matmul(p.ps512[3][0:1, 0:8], p.onesf[:, 0:1],
                               p.wtmp[:, 8 * c:8 * (c + 1)],
                               start=(c == 0), stop=(c == NDC - 1)))

    s2 = _snap(p)

    @p.block.vector
    def _(V):
        V.wait_ge(p.sT.sem, s2["sT"])
        p.sV.inc(V.tensor_copy(p.srow[0:1, 1024:1032], p.ps512[3][0:1, 0:8]))
        VS(p, V, V.tensor_add(p.srow[0:1, 1032:1040], p.brow[0:1, 1024:1032],
                              p.brow[0:1, 1032:1040]))
        p.sV.inc(V.tensor_scalar_mul(p.srow[0:1, 1040:1048],
                                     p.srow[0:1, 1032:1040], -1.0))
        V.wait_ge(p.sV.sem, p.sV.n)

    for qb in range(NQB):
        qsl = slice(QB * qb, QB * (qb + 1))
        s3 = _snap(p)

        @p.block.tensor
        def _(TE):
            _wait(p, TE, s3)
            for c in range(NDC):
                p.sT.inc(TE.matmul(p.ps1k[0][0:8, 0:512], p.wtmp[:, 8 * c:8 * (c + 1)],
                                   p.xf[c][:, qsl], start=(c == 0), stop=(c == NDC - 1)))

        s4 = _snap(p)

        @p.block.vector
        def _(V):
            V.wait_ge(p.sT.sem, s4["sT"])
            p.sV.inc(V.tensor_copy(p.lg, p.ps1k[0][0:8, 0:512]))

        for tt in range(4):
            gt = 4 * qb + tt
            s5 = _snap(p)

            @p.block.tensor
            def _(TE):
                _wait(p, TE, s5)
                TE.transpose(p.ps512[2][:, 0:8], p.lg[0:8, 128 * tt:128 * (tt + 1)],
                             p.identsb[0:8, 0:8])
                TE.transpose(p.ps512[3][:, 8:9],
                             p.lnr[0:1, qsl.start + 128 * tt: qsl.start + 128 * (tt + 1)],
                             p.identsb[0:1, 0:1])
                TE.matmul(p.ps512[3][:, 0:8],
                          p.lnmr[0:1, qsl.start + 128 * tt: qsl.start + 128 * (tt + 1)],
                          p.srow[0:1, 1024:1032], start=True, stop=False)
                p.sT.inc(TE.matmul(p.ps512[3][:, 0:8], p.onesf[0:1, :],
                                   p.srow[0:1, 1040:1048], start=False, stop=True))

            s6 = _snap(p)

            @p.block.vector
            def _(V):
                V.wait_ge(p.sT.sem, s6["sT"])
                VS(p, V, V.tensor_copy(p.cst[:, 10:11], p.ps512[3][:, 8:9]))
                VS(p, V, V.tensor_scalar_mul(p.scr[:, 0:8], p.ps512[2][:, 0:8],
                                             p.cst[:, 10:11]))
                VS(p, V, V.tensor_tensor(p.scr[:, 0:8], p.scr[:, 0:8],
                                         p.ps512[3][:, 0:8], ALU.subtract))
                VS(p, V, V.reduce_max(p.cst[:, 9:10], p.scr[:, 0:8], axis=AX))
                p.sV.inc(V.tensor_scalar(p.mkall[:, 8 * gt:8 * (gt + 1)], p.scr[:, 0:8],
                                         p.cst[:, 9:10], None, ALU.is_ge))

    s8 = _snap(p)
    for e in range(NEXP):
        s8e = _snap(p)

        @p.block.tensor
        def _(TE):
            _wait(p, TE, s8e)
            for gt in range(NTT):
                if gt >= 1:
                    TE.wait_ge(p.sV.sem, s8e["sV"] + gt)
                p.sT.inc(TE.transpose(p.ps512[2][0:1, 8:136],
                                      p.mkall[:, 8 * gt + e:8 * gt + e + 1],
                                      p.identsb[:]))

        @p.block.vector
        def _(V):
            for gt in range(NTT):
                V.wait_ge(p.sT.sem, s8e["sT"] + gt + 1)
                p.sV.inc(V.tensor_copy(p.mrowst[0:1, 128 * gt:128 * (gt + 1)],
                                       p.ps512[2][0:1, 8:136]))

        s8f = _snap(p)

        @p.block.sync
        def _(S):
            _wait(p, S, s8f)
            p.sD.inc(S.dma_start(out=p.mrow1d[e:e + 1, :], in_=p.mrowst), 16)

    s_moe = _snap(p)
    hT = {}
    hA = {}
    yT = {}
    yV = {}
    hbanks = [p.ps1k[0][:, 0:512], p.ps1k[0][:, 512:1024],
              p.ps1k[1][:, 0:512], p.ps1k[1][:, 512:1024]]
    w1slab = [p.wb[:, 0:4096], p.wb[:, 4096:8192]]
    w2slab = [p.vsl[:, 0:4096], p.vsl[:, 4096:8192]]
    htmp = [p.locsl[:, 0:512], p.locsl[:, 512:1024]]
    mkb = p.ubuf[:, 0:2048]

    for e in range(NEXP):
        se0 = _snap(p)

        @p.block.sync
        def _(S):
            _wait(p, S, se0)
            p.sD.inc(S.dma_start(out=p.mrowst, in_=p.mrow1d[e:e + 1, :]), 16)
            p.sD.inc(S.dma_start(out=p.wtmp[:, 64:96], in_=p.b1c[e]), 16)
        b1D = p.sD.n

        se1 = _snap(p)

        @p.block.tensor
        def _(TE):
            _wait(p, TE, se1)
            for j in range(4):
                if j >= 1:
                    TE.wait_ge(p.sV.sem, se1["sV"] + j)
                p.sT.inc(TE.matmul(hbanks[3], p.onesb[0:1, 0:128],
                                   p.mrowst[0:1, 512 * j:512 * (j + 1)],
                                   start=True, stop=True))

        @p.block.vector
        def _(V):
            for j in range(4):
                V.wait_ge(p.sT.sem, se1["sT"] + j + 1)
                p.sV.inc(V.tensor_copy(mkb[:, 512 * j:512 * (j + 1)], hbanks[3]))
        bcV = p.sV.n

        for fg in range(8):
            t = e * 8 + fg
            sl = t % 2

            @p.block.gpsimd
            def _(G):
                if t >= 2:
                    G.wait_ge(p.sT.sem, yT[t - 2])
                else:
                    _wait(p, G, s_moe)
                for c in range(NDC):
                    p.sG.inc(G.dma_start(
                        out=w1slab[sl][:, 512 * c:512 * (c + 1)],
                        in_=p.w1[e, 128 * c:128 * (c + 1), 512 * fg:512 * (fg + 1)]), 16)
            w1D = p.sG.n

            @p.block.gpsimd
            def _(G):
                for sf in range(4):
                    p.sG.inc(G.dma_start(
                        out=w2slab[sl][:, 1024 * sf:1024 * (sf + 1)],
                        in_=p.w2[e, 512 * fg + 128 * sf: 512 * fg + 128 * (sf + 1), :]), 16)
                if fg == 1:
                    p.sG.inc(G.dma_start(out=p.b2row[:], in_=p.b2[e:e + 1, :]), 16)
            w2D = p.sG.n

            hTbase = p.sT.n

            @p.block.tensor
            def _(TE):
                if t == 0:
                    _wait(p, TE, s_moe)
                TE.wait_ge(p.sG.sem, w1D)
                for qb in range(NQB):
                    qsl = slice(QB * qb, QB * (qb + 1))
                    for sf in range(4):
                        j = qb * 4 + sf
                        if t >= 1:
                            TE.wait_ge(p.sA.sem, hA[t - 1] + max(0, j - 3))
                        if fg == 0 and j == 3:
                            TE.wait_ge(p.sV.sem, bcV)
                        for c in range(NDC):
                            mm = TE.matmul(
                                hbanks[j % 4],
                                w1slab[sl][:, 512 * c + 128 * sf: 512 * c + 128 * (sf + 1)],
                                p.xn[c][:, qsl], start=(c == 0), stop=(c == NDC - 1))
                        p.sT.inc(mm)
            hT[t] = p.sT.n

            hAbase = p.sA.n
            vA = p.sV.n

            @p.block.scalar
            def _(A):
                if t == 0:
                    _wait(p, A, s_moe)
                if fg == 0:
                    A.wait_ge(p.sD.sem, b1D)
                for j in range(16):
                    sf = j % 4
                    fc = 4 * fg + sf
                    A.wait_ge(p.sT.sem, hTbase + j + 1)
                    if j >= 2:
                        A.wait_ge(p.sV.sem, vA + j - 1)
                    p.sA.inc(A.activation(out=htmp[j % 2], in_=hbanks[j % 4],
                                          func=AF.Relu,
                                          bias=p.wtmp[:, 64 + fc:65 + fc]))
            hA[t] = p.sA.n

            @p.block.vector
            def _(V):
                if t < 2:
                    _wait(p, V, s_moe)
                else:
                    V.wait_ge(p.sT.sem, yT[t - 2])
                for j in range(16):
                    qb, sf = j // 4, j % 4
                    qsl = slice(QB * qb, QB * (qb + 1))
                    V.wait_ge(p.sA.sem, hAbase + j + 1)
                    p.sV.inc(V.tensor_tensor(p.qa[4 * sl + sf][:, qsl],
                                             htmp[j % 2], mkb[:, qsl], ALU.mult))

            yTbase = p.sT.n
            vY = p.sV.n

            @p.block.tensor
            def _(TE):
                TE.wait_ge(p.sG.sem, w2D)
                for qb in range(NQB):
                    qsl = slice(QB * qb, QB * (qb + 1))
                    TE.wait_ge(p.sV.sem, vA + 4 * qb + 4)
                    for dc in range(NDC):
                        g = qb * 8 + dc
                        bank = p.ps512[g % 4]
                        if g >= 4:
                            TE.wait_ge(p.sV.sem, vY + g - 3)
                        elif t >= 1:
                            TE.wait_ge(p.sV.sem, yV[t - 1])
                        for sf in range(4):
                            mm = TE.matmul(
                                bank[:],
                                w2slab[sl][:, 1024 * sf + 128 * dc: 1024 * sf + 128 * (dc + 1)],
                                p.qa[4 * sl + sf][:, qsl],
                                start=(sf == 0), stop=(sf == 3 and fg != 7))
                        if fg == 7:
                            mm = TE.matmul(bank[:], p.b2row[0:1, 128 * dc:128 * (dc + 1)],
                                           p.mrowst[0:1, qsl], start=False, stop=True)
                        p.sT.inc(mm)
            yT[t] = p.sT.n

            @p.block.vector
            def _(V):
                for qb in range(NQB):
                    qsl = slice(QB * qb, QB * (qb + 1))
                    for dc in range(NDC):
                        g = qb * 8 + dc
                        V.wait_ge(p.sT.sem, yTbase + g + 1)
                        p.sV.inc(V.tensor_add(p.xf[dc][:, qsl], p.ps512[g % 4][:],
                                              p.xf[dc][:, qsl]))
            yV[t] = p.sV.n



_NC_CACHE = {}


def _get_prog(part):
    if part not in _NC_CACHE:
        _NC_CACHE[part] = build_program(part)
    return _NC_CACHE[part]


def _per_core_shared(inp):
    aR, bR = tt_compact(inp["L0_ffn_G1_R"], inp["L0_ffn_G2_R"])
    aI, bI = tt_compact(inp["L0_ffn_G1_I"], inp["L0_ffn_G2_I"])
    f32 = lambda a: np.ascontiguousarray(np.asarray(a), dtype=np.float32)
    colsof = lambda v: np.ascontiguousarray(f32(v).reshape(8, 128).T)
    shared = {
        "w_q": f32(inp["L0_attn_w_q"]), "w_k": f32(inp["L0_attn_w_k"]),
        "w_v": f32(inp["L0_attn_w_v"]), "w_o": f32(inp["L0_attn_w_o"]),
        "normw": colsof(inp["L0_attn_norm_w"]),
        "fftt": tt_pack(aR, bR, aI, bI),
        "brbi": np.ascontiguousarray(np.concatenate(
            [f32(inp["L0_ffn_bias_real"]).reshape(4, 128).T,
             f32(inp["L0_ffn_bias_imag"]).reshape(4, 128).T], axis=1)),
        "Wq1": f32(inp["L1_attn_Wq"]), "Wk1": f32(inp["L1_attn_Wk"]),
        "Wv1": f32(inp["L1_attn_Wv"]), "Wo1": f32(inp["L1_attn_Wo"]),
        "bq1": colsof(inp["L1_attn_bq"]), "bk1": colsof(inp["L1_attn_bk"]),
        "bv1": f32(inp["L1_attn_bv"]).reshape(1, D),
        "bo1": colsof(inp["L1_attn_bo"]),
        "gwT": np.ascontiguousarray(f32(inp["L1_ffn_gate_w"]).T),
        "gbeb": np.ascontiguousarray(np.stack(
            [f32(inp["L1_ffn_gate_b"]), f32(inp["L1_ffn_expert_bias"])])),
        "w1": f32(inp["L1_ffn_w1"]),
        "b1c": np.ascontiguousarray(
            f32(inp["L1_ffn_b1"]).reshape(NEXP, 32, 128).transpose(0, 2, 1)),
        "w2": f32(inp["L1_ffn_w2"]), "b2": f32(inp["L1_ffn_b2"]),
    }
    return shared


def kernel(**inp):
    x = np.asarray(inp["L0_attn_x"], dtype=np.float32)
    shared = _per_core_shared(inp)
    consts = [host_constants(c) for c in range(NCORES)]
    xs = []
    for c in range(NCORES):
        b, r = c // 4, c % 4
        xs.append(np.ascontiguousarray(x[b, 2048 * r:2048 * (r + 1), :].T))

    def pick(names, extra):
        maps = []
        for c in range(NCORES):
            m = {}
            for n in names:
                if n == "x":
                    m[n] = xs[c]
                elif n in ("rf", "kill", "mask01", "ident"):
                    m[n] = consts[c][n]
                elif n in shared:
                    m[n] = shared[n]
                elif n in extra:
                    m[n] = extra[n][c]
                else:
                    raise KeyError(n)
            maps.append(m)
        return maps

    if os.environ.get("KMODE", "fused") == "fused":
        ncF, namesF = _get_prog("ALL")
        res = run_bass_kernel_spmd(ncF, pick(namesF, {}), core_ids=list(range(NCORES)))
        out = np.empty((2, SEQ, D), dtype=np.float32)
        for c in range(NCORES):
            b, r = c // 4, c % 4
            out[b, 2048 * r:2048 * (r + 1), :] = res.results[c]["out"].T
        return out

    ncA, namesA = _get_prog("A")
    resA = run_bass_kernel_spmd(ncA, pick(namesA, {}), core_ids=list(range(NCORES)))
    rA = resA.results
    kg0 = [np.concatenate([rA[g * 4 + i]["k_loc0"] for i in range(4)], axis=0)
           for g in range(2)]
    vg0 = [np.concatenate([rA[g * 4 + i]["v_loc0"] for i in range(4)], axis=0)
           for g in range(2)]
    extraB = {
        "qdump": [rA[c]["qdump"] for c in range(NCORES)],
        "kg0": [kg0[c // 4] for c in range(NCORES)],
        "vg0": [vg0[c // 4] for c in range(NCORES)],
        "k_loc0": [rA[c]["k_loc0"] for c in range(NCORES)],
        "v_loc0": [rA[c]["v_loc0"] for c in range(NCORES)],
    }
    ncB, namesB = _get_prog("B")
    resB = run_bass_kernel_spmd(ncB, pick(namesB, extraB), core_ids=list(range(NCORES)))
    rB = resB.results
    kg1 = [np.concatenate([rB[g * 4 + i]["k_loc1"] for i in range(4)], axis=0)
           for g in range(2)]
    vg1 = [np.concatenate([rB[g * 4 + i]["v_loc1"] for i in range(4)], axis=0)
           for g in range(2)]
    extraC = {
        "x1": [rB[c]["x1"] for c in range(NCORES)],
        "q1": [rB[c]["q1"] for c in range(NCORES)],
        "kg1": [kg1[c // 4] for c in range(NCORES)],
        "vg1": [vg1[c // 4] for c in range(NCORES)],
    }
    ncC, namesC = _get_prog("C")
    resC = run_bass_kernel_spmd(ncC, pick(namesC, extraC), core_ids=list(range(NCORES)))
    rC = resC.results

    out = np.empty((2, SEQ, D), dtype=np.float32)
    for c in range(NCORES):
        b, r = c // 4, c % 4
        out[b, 2048 * r:2048 * (r + 1), :] = rC[c]["out"].T
    return out


# version 14
# speedup vs baseline: 1.1119x; 1.0003x over previous
import contextlib
import os
import numpy as np
import ml_dtypes
import concourse.bass as bass
import concourse.mybir as mybir
from concourse.bass_utils import run_bass_kernel_spmd

F32 = mybir.dt.float32
BF16 = mybir.dt.bfloat16
AF = mybir.ActivationFunctionType
ALU = mybir.AluOpType
AX = mybir.AxisListType.X

NCORES = 8
T = 2048
D = 1024
NQB = 4
QB = 512
NTT = 16
NDC = 8
SEQ = 8192
NKT = 64
DFF = 4096
NEXP = 8
LN_EPS = 1e-5
RMS_EPS = 1.1920929e-07
KILL = -30000.0
VW = 130
L0_VW = 2 * NTT * VW
L1_VW = 8 * NTT * VW



def _rope_rf(n_total=SEQ, d_head=64):
    i = np.arange(d_head)
    theta = np.power(10000.0, -2.0 * (i // 2).astype(np.float64) / d_head)
    pos = np.arange(n_total, dtype=np.float64)
    ang = pos[:, None] * theta[None, :]
    sign = np.where(pos % 2 == 0, -1.0, 1.0)[:, None]
    return (np.cos(ang) + np.sin(ang) * sign).astype(np.float32)


def host_constants(core):
    r = core % 4
    p0 = 2048 * r
    out = {}
    rf = _rope_rf()[p0:p0 + T]
    out["rf"] = np.ascontiguousarray(np.tile(rf.T, (2, 1)))
    kill = np.zeros((NQB, NKT), dtype=np.float32)
    for qb in range(NQB):
        for kt in range(NKT):
            if kt >= 16 * r:
                kill[qb, kt] = 1.0
    out["kill"] = np.ascontiguousarray(
        np.broadcast_to(kill.reshape(1, NQB * NKT), (128, NQB * NKT))).astype(ml_dtypes.bfloat16)
    u = np.arange(896)[None, :]
    k = np.arange(128)[:, None]
    out["mask01"] = np.ascontiguousarray(
        (u >= k + 384).astype(np.float32)).astype(ml_dtypes.bfloat16)
    out["ident"] = np.eye(128, dtype=np.float32)
    return out


def tt_compact(G1, G2):
    A = np.empty((4, 512, 32), dtype=np.float32)
    B = np.empty((4, 512, 16), dtype=np.float32)
    for c in range(4):
        A[c] = np.repeat(np.asarray(G1)[0, :, c, :].T, 16, axis=0)
        B[c] = np.tile(np.asarray(G2)[c, :, 0, :].T, (32, 1))
    return A, B



class Ctr:
    def __init__(self, sem):
        self.sem = sem
        self.n = 0

    def inc(self, instr, k=1):
        instr.then_inc(self.sem, k)
        self.n += k
        return self.n


class P:
    pass


CTRS = ["sD", "sG", "sT", "sA", "sV", "sC"]


def _snap(p):
    return {k: getattr(p, k).n for k in CTRS}


def _wait(p, eng, s):
    for k, v in s.items():
        if v > 0 and k != "sC":
            eng.wait_ge(getattr(p, k).sem, v)


def VS(p, V, instr):
    p.sV.inc(instr)
    V.wait_ge(p.sV.sem, p.sV.n)


def AS(p, A, instr):
    p.sA.inc(instr)
    A.wait_ge(p.sA.sem, p.sA.n)



def build_program(part):
    nc = bass.Bass("TRN2", target_bir_lowering=False, debug=False)
    p = P()
    p.nc = nc
    st = contextlib.ExitStack()
    sA_ = part in ("A", "ALL")
    sB_ = part in ("B", "ALL")
    sC_ = part in ("C", "ALL")
    fused = part == "ALL"
    p.fused = fused

    inputs = []

    def din(name, shape, dt=F32):
        inputs.append(name)
        return nc.dram_tensor(name, list(shape), dt, kind="ExternalInput").ap()

    def dout(name, shape, dt=F32):
        return nc.dram_tensor(name, list(shape), dt, kind="ExternalOutput").ap()

    def dint(name, shape, dt):
        return nc.dram_tensor(name, list(shape), dt, kind="Internal").ap()

    if sA_ or sB_ == (part == "B"):
        pass
    x_in = din("x", [D, T]) if (sA_ or part == "B") else None
    p.ident = din("ident", [128, 128])
    if sA_:
        p.rf = din("rf", [128, T])
        p.w_q = din("w_q", [D, D])
        p.w_k = din("w_k", [D, 256])
        p.w_v = din("w_v", [D, 256])
        p.normw = din("normw", [128, 8])
    if sB_:
        p.kill_t = din("kill", [128, NQB * NKT], BF16)
        p.mask01 = din("mask01", [128, 896], BF16)
        p.w_o = din("w_o", [D, D])
        p.aR = din("aR", [4, 512, 32])
        p.bR = din("bR", [4, 512, 16])
        p.aI = din("aI", [4, 512, 32])
        p.bI = din("bI", [4, 512, 16])
        p.brbi = din("brbi", [128, 8])
        p.Wq1 = din("Wq1", [D, D])
        p.Wk1 = din("Wk1", [D, D])
        p.Wv1 = din("Wv1", [D, D])
        p.bq1 = din("bq1", [128, 8])
        p.bk1 = din("bk1", [128, 8])
        p.bv1 = din("bv1", [1, D])
    if sC_:
        p.Wo1 = din("Wo1", [D, D])
        p.bo1 = din("bo1", [128, 8])
        p.gwT = din("gwT", [D, 8])
        p.gbeb = din("gbeb", [2, 8])
        p.w1 = din("w1", [NEXP, D, DFF])
        p.b1c = din("b1c", [NEXP, 128, 32])
        p.w2 = din("w2", [NEXP, DFF, D])
        p.b2 = din("b2", [NEXP, D])
        p.mrow1d = dint("mrow1d", [NEXP, T], BF16)

    if fused:
        p.kg0s = [dint(f"kg0s{g}", [512, T], BF16) for g in range(2)]
        p.vg0s = [dint(f"vg0s{g}", [512, NTT * VW], BF16) for g in range(2)]
        p.kg1s = [dint(f"kg1s{g}", [512, T], BF16) for g in range(8)]
        p.vg1s = [dint(f"vg1s{g}", [512, NTT * VW], BF16) for g in range(8)]
        p.v_loc0s = [dint(f"v_loc0s{g}", [128, NTT * VW], BF16) for g in range(2)]
        p.v_loc1s = [dint(f"v_loc1s{g}", [128, NTT * VW], BF16) for g in range(8)]
        p.cc_k = {0: {}, 1: {}}
        p.cc_v = {0: {}, 1: {}}
    if sA_:
        if fused:
            p.k_loc0 = dint("k_loc0", [256, T], BF16)
            p.v_loc0 = dint("v_loc0", [128, L0_VW], BF16)
        else:
            p.k_loc0 = dout("k_loc0", [256, T], BF16)
            p.v_loc0 = dout("v_loc0", [128, L0_VW], BF16)
            p.qdump = dout("qdump", [D, T], BF16)
    if sB_:
        if fused:
            p.kg0 = dint("kg0", [4 * 256, T], BF16)
            p.vg0 = dint("vg0", [4 * 128, L0_VW], BF16)
        else:
            p.kg0 = din("kg0", [4 * 256, T], BF16)
            p.vg0 = din("vg0", [4 * 128, L0_VW], BF16)
            p.qdump_in = din("qdump", [D, T], BF16)
            p.k_loc0 = din("k_loc0", [256, T], BF16)
            p.v_loc0 = din("v_loc0", [128, L0_VW], BF16)
        if fused:
            p.k_loc1 = dint("k_loc1", [D, T], BF16)
            p.v_loc1 = dint("v_loc1", [128, L1_VW], BF16)
        else:
            p.k_loc1 = dout("k_loc1", [D, T], BF16)
            p.v_loc1 = dout("v_loc1", [128, L1_VW], BF16)
            p.x1dump = dout("x1", [D, T])
            p.q1dump = dout("q1", [D, T], BF16)
    if sC_:
        if fused:
            p.kg1 = dint("kg1", [4 * D, T], BF16)
            p.vg1 = dint("vg1", [4 * 128, L1_VW], BF16)
        else:
            p.kg1 = din("kg1", [4 * D, T], BF16)
            p.vg1 = din("vg1", [4 * 128, L1_VW], BF16)
            p.x1_in = din("x1", [D, T])
            p.q1_in = din("q1", [D, T], BF16)
        p.out = dout("out", [D, T])
    if fused:
        p.cc_in = dint("cc_in", [128, 128], F32)
        p.cc_out = dint("cc_out", [128, 128], F32)

    sb = lambda name, shape, dt: st.enter_context(nc.sbuf_tensor(name, list(shape), dt))
    p.xf = [sb(f"xf{c}", [128, T], F32) for c in range(NDC)]
    p.xn = [sb(f"xn{c}", [128, T], BF16) for c in range(NDC)]
    p.qa = [sb(f"qa{c}", [128, T], BF16) for c in range(NDC)]
    p.wb = sb("wb", [128, 8192], BF16)
    p.vsl = sb("vsl", [128, 8320], BF16)
    p.locsl = sb("locsl", [128, 4160], BF16)
    p.ubuf = sb("ubuf", [128, 4160], BF16)
    p.scr = sb("scr", [128, 2048], F32)
    p.wtmp = sb("wtmp", [128, 256], F32)
    p.attB = sb("attB", [64, T], BF16)
    p.rfsb = None
    p.masks = sb("masks", [128, 896], BF16) if sB_ else None
    p.killsb = sb("killsb", [128, 256], BF16) if sB_ else None
    p.biastbl = sb("biastbl", [128, 256], F32) if sB_ else None
    p.onesf = sb("onesf", [128, 128], F32)
    p.onesb = sb("onesb", [128, 512], BF16)
    p.identsb = sb("identsb", [128, 128], F32)

    p.bcol = sb("bcol", [128, 16], F32)
    p.brow = sb("brow", [1, 1056], F32)
    p.srow = sb("srow", [1, 1056], F32)
    p.cst = sb("cst", [128, 48], F32)
    p.ccol = sb("ccol", [128, 1], F32)
    p.epsc = sb("epsc", [1, 2], F32)
    p.cneg = sb("cneg", [128, 1], F32)
    p.rowscr = sb("rowscr", [1, 2048], BF16) if sC_ else None
    if sC_:
        p.mkall = p.scr[:, 1024:1152]
        p.browb = p.rowscr[0:1, 0:1536]
        p.mrowc = p.rowscr[0:1, 1536:2048]
        p.mrowst = p.rowscr[0:1, 0:2048]

    p.kfm = [lambda s=s: None for s in range(2)]
    p.pt = [p.ubuf[:, 0:1024], p.ubuf[:, 1024:2048]]
    p.vtb = p.ubuf[:, 0:2080]
    p.mkbc = p.ubuf[:, 0:512]
    p.b2row = p.attB[0:1, 0:1024]
    p.kfm = [p.locsl[:, 0:2048], p.locsl[:, 2048:4096]]
    p.attb_guard = 0
    p.pv_guard = 0
    p.pend = None
    p.norm_mults_cnt = 0
    p.lg = p.scr[0:8, 512:1024]
    p.lr = p.scr[:, 1024:1536]
    p.lr2 = p.scr[:, 1536:2048]
    vslf = p.vsl[:].bitcast(F32)
    p.lnr = vslf[0:1, 0:2048]
    p.lnmr = vslf[0:1, 2048:4096]
    p.rmsr = vslf[0:1, 2048:4096]

    ps = lambda name, shape: st.enter_context(nc.psum_tensor(name, list(shape), F32))
    p.ps512 = [ps(f"pb{i}", [128, 512]) for i in range(4)]
    p.ps1k = [ps(f"pk{i}", [128, 1024]) for i in range(2)]

    sem = lambda name: st.enter_context(nc.semaphore(name))
    for k in CTRS:
        setattr(p, k, Ctr(sem(k)))

    p.block = st.enter_context(nc.Block())

    dbg = int(os.environ.get("KDBG", "99"))
    emit_load(p, x_in, sA_, sB_, sC_, part)
    if sA_:
        emit_weight_gain_cast(p, p.w_q, cols=1024)
        if dbg >= 1:
            emit_ln(p, rms_after=(dbg >= 2))
        if dbg >= 3:
            emit_qkv0(p)
        if not fused:
            emit_dump(p, [(p.qdump, p.qa)])
    if sB_:
        if not fused:
            emit_load_chunks(p, p.qdump_in, p.qa)
        if dbg >= 10:
            emit_cstats(p, layer=0)
        if dbg >= 11:
            emit_attention(p, layer=0)
        if dbg >= 12:
            emit_oproj(p, w=p.w_o, bias=None, inner_res=True)
        if dbg >= 13:
            emit_ffn0_build(p)
            emit_ln(p, rms_after=False)
        if dbg >= 14:
            emit_ffn0(p)
            emit_weight_cast_dma(p, p.Wq1, rows=D, cols=D)
        if dbg >= 15:
            emit_ln(p, rms_after=False)
        if dbg >= 16:
            emit_qkv1(p)
        if not fused:
            emit_dump(p, [(p.x1dump, p.xf), (p.q1dump, p.qa)])
    if sC_:
        if not fused:
            emit_load_chunks(p, p.x1_in, p.xf)
            emit_load_chunks(p, p.q1_in, p.qa)
        emit_cstats(p, layer=1)
        emit_attention(p, layer=1)
        emit_oproj(p, w=p.Wo1, bias="bo1")
        emit_ln(p, rms_after=False, persist=True)
        emit_moe(p)
        emit_dump(p, [(p.out, p.xf)], final=True)
    else:
        emit_tail_wait(p)

    st.close()
    return nc, inputs



def emit_load(p, x_in, sA_, sB_, sC_, part):
    @p.block.sync
    def _(S):
        if x_in is not None:
            for c in range(NDC):
                p.sD.inc(S.dma_start(out=p.xf[c][:], in_=x_in[128 * c:128 * (c + 1), :]), 16)
        if sA_:
            p.sD.inc(S.dma_start(out=p.bcol[:, 8:16], in_=p.normw[:]), 16)
        p.sD.inc(S.dma_start(out=p.identsb[:], in_=p.ident[:]), 16)
        if sB_:
            p.sD.inc(S.dma_start(out=p.killsb[:], in_=p.kill_t[:]), 16)
            p.sD.inc(S.dma_start(out=p.masks[:], in_=p.mask01[:]), 16)

    @p.block.vector
    def _(V):
        p.sV.inc(V.memset(p.onesf[:], 1.0))
        p.sV.inc(V.memset(p.onesb[:], 1.0))
        p.sV.inc(V.memset(p.epsc[0:1, 0:1], LN_EPS))
        p.sV.inc(V.memset(p.epsc[0:1, 1:2], RMS_EPS))


def emit_load_chunks(p, dram, chunks):
    @p.block.sync
    def _(S):
        for c in range(NDC):
            p.sD.inc(S.dma_start(out=chunks[c][:], in_=dram[128 * c:128 * (c + 1), :]), 16)


def emit_dump(p, pairs, final=False):
    s = _snap(p)

    @p.block.sync
    def _(S):
        _wait(p, S, s)
        for dram, chunks in pairs:
            for c in range(NDC):
                p.sD.inc(S.dma_start(out=dram[128 * c:128 * (c + 1), :], in_=chunks[c][:]), 16)
        S.wait_ge(p.sD.sem, p.sD.n)


def emit_tail_wait(p):
    s = _snap(p)

    @p.block.sync
    def _(S):
        _wait(p, S, s)
        S.wait_ge(p.sD.sem, p.sD.n)
        S.wait_ge(p.sG.sem, p.sG.n)


def emit_gather(p, src_ap, dst_ap, layer, which, g):
    need = p.sD.n

    @p.block.gpsimd
    def _(G):
        G.wait_ge(p.sD.sem, need)
        cnt = p.sC.inc(G.collective_compute(
            "AllGather", ALU.bypass,
            replica_groups=[[0, 1, 2, 3], [4, 5, 6, 7]],
            ins=[src_ap.opt()], outs=[dst_ap.opt()]), 1)
        (p.cc_k if which == "k" else p.cc_v)[layer][g] = cnt


def emit_ln(p, rms_after, persist=False):
    for qb in range(NQB):
        qs = slice(QB * qb, QB * (qb + 1))
        rr_row = p.lnr[0:1, qs] if persist else p.brow[0:1, 0:512]
        mr_row = p.lnmr[0:1, qs] if persist else p.brow[0:1, 512:1024]
        s0 = _snap(p)

        @p.block.scalar
        def _(A):
            _wait(p, A, s0)
            for c in range(NDC):
                if c >= 2:
                    A.wait_ge(p.sT.sem, s0["sT"] + (c - 1))
                p.sA.inc(A.activation(out=p.scr[:, 512 * (c % 2):512 * (c % 2 + 1)],
                                      in_=p.xf[c][:, qs], func=AF.Square))

        @p.block.tensor
        def _(TE):
            _wait(p, TE, s0)
            for c in range(NDC):
                TE.matmul(p.ps512[0][0:1, :], p.onesf[:, 0:1], p.xf[c][:, qs],
                          start=(c == 0), stop=(c == NDC - 1))
                TE.wait_ge(p.sA.sem, s0["sA"] + c + 1)
                p.sT.inc(TE.matmul(p.ps512[1][0:1, :], p.onesf[:, 0:1],
                                   p.scr[:, 512 * (c % 2):512 * (c % 2 + 1)],
                                   start=(c == 0), stop=(c == NDC - 1)))

        s1 = _snap(p)

        @p.block.vector
        def _(V):
            V.wait_ge(p.sT.sem, s1["sT"])
            rr = rr_row
            V.tensor_scalar_mul(p.srow[0:1, 0:512], p.ps512[0][0:1, :], 1.0 / D)
            VS(p, V, V.tensor_scalar_mul(p.srow[0:1, 512:1024], p.ps512[1][0:1, :], 1.0 / D))
            VS(p, V, V.tensor_tensor(rr, p.srow[0:1, 0:512], p.srow[0:1, 0:512], ALU.mult))
            p.sV.inc(V.tensor_tensor(rr, p.srow[0:1, 512:1024], rr, ALU.subtract))

        s2 = _snap(p)

        @p.block.scalar
        def _(A):
            A.wait_ge(p.sV.sem, s2["sV"])
            rr = rr_row
            AS(p, A, A.activation(out=rr, in_=rr, func=AF.Ln, bias=p.epsc[0:1, 0:1]))
            p.sA.inc(A.activation(out=rr, in_=rr, func=AF.Exp, scale=-0.5))

        s3 = _snap(p)

        @p.block.vector
        def _(V):
            V.wait_ge(p.sA.sem, s3["sA"])
            p.sV.inc(V.tensor_tensor(mr_row, p.srow[0:1, 0:512],
                                     rr_row, ALU.mult))

        s4 = _snap(p)

        @p.block.tensor
        def _(TE):
            TE.wait_ge(p.sV.sem, s4["sV"])
            TE.matmul(p.ps512[2][:], p.onesf[0:1, :], rr_row,
                      start=True, stop=True)
            p.sT.inc(TE.matmul(p.ps512[3][:], p.onesf[0:1, :], mr_row,
                               start=True, stop=True))

        s5 = _snap(p)

        @p.block.vector
        def _(V):
            V.wait_ge(p.sT.sem, s5["sT"])
            for c in range(NDC):
                V.tensor_tensor(p.scr[:, 0:512], p.xf[c][:, qs], p.ps512[2][:], ALU.mult)
                p.sV.inc(V.tensor_tensor(p.xn[c][:, qs], p.scr[:, 0:512],
                                         p.ps512[3][:], ALU.subtract))

    if rms_after:
        emit_rms(p)


def emit_rms(p):
    for qb in range(NQB):
        qs = slice(QB * qb, QB * (qb + 1))
        s0 = _snap(p)

        @p.block.scalar
        def _(A):
            _wait(p, A, s0)
            for c in range(NDC):
                if c >= 2:
                    A.wait_ge(p.sT.sem, s0["sT"] + (c - 1))
                p.sA.inc(A.activation(out=p.scr[:, 512 * (c % 2):512 * (c % 2 + 1)],
                                      in_=p.xn[c][:, qs], func=AF.Square))

        @p.block.tensor
        def _(TE):
            _wait(p, TE, s0)
            for c in range(NDC):
                TE.wait_ge(p.sA.sem, s0["sA"] + c + 1)
                p.sT.inc(TE.matmul(p.ps512[1][0:1, :], p.onesf[:, 0:1],
                                   p.scr[:, 512 * (c % 2):512 * (c % 2 + 1)],
                                   start=(c == 0), stop=(c == NDC - 1)))

        s1 = _snap(p)

        @p.block.vector
        def _(V):
            V.wait_ge(p.sT.sem, s1["sT"])
            p.sV.inc(V.tensor_scalar_mul(p.rmsr[0:1, qs], p.ps512[1][0:1, :], 1.0 / D))

        s2 = _snap(p)

        @p.block.scalar
        def _(A):
            A.wait_ge(p.sV.sem, s2["sV"])
            AS(p, A, A.activation(out=p.rmsr[0:1, qs], in_=p.rmsr[0:1, qs],
                                  func=AF.Ln, bias=p.epsc[0:1, 1:2]))
            p.sA.inc(A.activation(out=p.rmsr[0:1, qs], in_=p.rmsr[0:1, qs],
                                  func=AF.Exp, scale=-0.5))


def emit_weight_gain_cast(p, w, cols):
    s0 = _snap(p)
    locf = p.locsl[:].bitcast(F32)

    @p.block.sync
    def _(S):
        _wait(p, S, s0)
        for c in range(NDC):
            if c >= 1:
                S.wait_ge(p.sV.sem, s0["sV"] + c)
            p.sD.inc(S.dma_start(out=locf[:, 0:cols],
                                 in_=w[128 * c:128 * (c + 1), :]), 16)

    @p.block.vector
    def _(V):
        _wait(p, V, s0)
        for c in range(NDC):
            V.wait_ge(p.sD.sem, s0["sD"] + 16 * (c + 1))
            p.sV.inc(V.tensor_scalar_mul(p.wb[:, cols * c: cols * (c + 1)],
                                         locf[:, 0:cols], p.bcol[:, 8 + c:9 + c]))


def emit_weight_cast_dma(p, w, rows, cols, off=0, dst=None):
    s0 = _snap(p)
    dst = p.wb if dst is None else dst

    @p.block.gpsimd
    def _(G):
        _wait(p, G, s0)
        for c in range(rows // 128):
            p.sG.inc(G.dma_start(out=dst[:, off + cols * c: off + cols * (c + 1)],
                                 in_=w[128 * c:128 * (c + 1), :]), 16)


def emit_proj_all(p, wcols, ocs, outs, rope, bias_cols, wsrc=None):
    wsrc = p.wb if wsrc is None else wsrc
    """qb-outer projections: outs[i][:, qb] = evac(sum_c wb[c, ocs[i]].T @ xn[c][:, qb]).
    rope: evac multiplies by (rf * rr) built per qb in scr[:, 1536:2048].
    V incs per qb: (1 rope-mult if rope) + n evacs. T incs: n proj mms per qb."""
    s0 = _snap(p)
    n = len(ocs)
    vper = n + (1 if rope else 0)

    def v_after_evac(qb, i):
        return s0["sV"] + qb * vper + (1 if rope else 0) + i + 1

    if rope:
        @p.block.sync
        def _(S):
            _wait(p, S, s0)
            for qb in range(NQB):
                if qb >= 1:
                    S.wait_ge(p.sV.sem, v_after_evac(qb - 1, n - 1))
                p.sD.inc(S.dma_start(out=p.scr[:, 1536:2048],
                                     in_=p.rf[:, QB * qb:QB * (qb + 1)]), 16)

    @p.block.tensor
    def _(TE):
        _wait(p, TE, s0)
        for qb in range(NQB):
            if rope:
                TE.wait_ge(p.sD.sem, s0["sD"] + 16 * (qb + 1))
                if qb >= 1:
                    TE.wait_ge(p.sV.sem, s0["sV"] + (qb - 1) * vper + 1)
                TE.matmul(p.ps512[2][:], p.onesf[0:1, :],
                          p.rmsr[0:1, QB * qb:QB * (qb + 1)], start=True, stop=True)
            for i, oc in enumerate(ocs):
                idx = qb * n + i
                bank = p.ps512[idx % 2]
                if idx >= 2:
                    q2, i2 = (idx - 2) // n, (idx - 2) % n
                    TE.wait_ge(p.sV.sem, v_after_evac(q2, i2))
                for c in range(NDC):
                    mm = TE.matmul(bank[:], wsrc[:, wcols * c + 128 * oc: wcols * c + 128 * (oc + 1)],
                                   p.xn[c][:, QB * qb:QB * (qb + 1)],
                                   start=(c == 0), stop=(c == NDC - 1))
                p.sT.inc(mm)

    @p.block.vector
    def _(V):
        for qb in range(NQB):
            if rope:
                V.wait_ge(p.sD.sem, s0["sD"] + 16 * (qb + 1))
                V.wait_ge(p.sT.sem, s0["sT"] + qb * n + 1)
                p.sV.inc(V.tensor_tensor(p.scr[:, 1536:2048], p.scr[:, 1536:2048],
                                         p.ps512[2][:], ALU.mult))
            for i, oc in enumerate(ocs):
                idx = qb * n + i
                bank = p.ps512[idx % 2]
                V.wait_ge(p.sT.sem, s0["sT"] + idx + 1)
                qs = slice(QB * qb, QB * (qb + 1))
                if rope:
                    ins = V.tensor_tensor(outs[i][:, qs], bank[:], p.scr[:, 1536:2048],
                                          ALU.mult)
                elif bias_cols is not None:
                    ins = V.tensor_scalar_add(outs[i][:, qs], bank[:], bias_cols[i])
                else:
                    ins = V.tensor_copy(outs[i][:, qs], bank[:])
                p.sV.inc(ins)


def emit_vproj(p, groups, wcols, bias_row, v_loc, rms=False, layer=0, wsrc=None):
    wsrc = p.wb if wsrc is None else wsrc
    """Token-major V projection into v_loc slab [128, groups*16*130].
    rms: multiply outputs by per-token rr (transposed from p.rmsr)."""
    s0 = _snap(p)

    @p.block.vector
    def _(V):
        _wait(p, V, s0)
        p.sV.inc(V.memset(p.vtb.rearrange("p (l w) -> p l w", w=VW)[:, :, 64:65], 1.0))
        p.sV.inc(V.memset(p.vtb.rearrange("p (l w) -> p l w", w=VW)[:, :, 129:130], 1.0))

    if isinstance(v_loc, list):
        v_dst = lambda g: v_loc[g][:]
    else:
        v_dst = lambda g: v_loc[:, NTT * VW * g: NTT * VW * (g + 1)]
    vper = 2 if rms else 1
    for g in range(groups):
        s1 = _snap(p)

        @p.block.tensor
        def _(TE):
            _wait(p, TE, s1)
            for lt in range(NTT):
                bank = p.ps512[lt % 2]
                if rms:
                    if lt >= 1:
                        TE.wait_ge(p.sV.sem, s1["sV"] + vper * (lt - 1) + 1)
                    TE.transpose(p.ps512[2][:, 0:1],
                                 p.rmsr[0:1, 128 * lt:128 * (lt + 1)],
                                 p.identsb[0:1, 0:1])
                if lt >= 2:
                    TE.wait_ge(p.sV.sem, s1["sV"] + vper * (lt - 1))
                for c in range(NDC):
                    mm = TE.matmul(bank[:, 0:128],
                                   p.xn[c][:, 128 * lt:128 * (lt + 1)],
                                   wsrc[:, wcols * c + 128 * g: wcols * c + 128 * (g + 1)],
                                   start=(c == 0), stop=(c == NDC - 1) and bias_row is None)
                if bias_row is not None:
                    mm = TE.matmul(bank[:, 0:128], p.onesf[0:1, :],
                                   bias_row[0:1, 128 * g:128 * (g + 1)],
                                   start=False, stop=True)
                p.sT.inc(mm)

        @p.block.vector
        def _(V):
            for lt in range(NTT):
                bank = p.ps512[lt % 2]
                V.wait_ge(p.sT.sem, s1["sT"] + lt + 1)
                if rms:
                    p.sV.inc(V.tensor_copy(p.cst[:, 13:14], p.ps512[2][:, 0:1]))
                    V.wait_ge(p.sV.sem, p.sV.n)
                    V.tensor_scalar_mul(p.vtb[:, VW * lt: VW * lt + 64],
                                        bank[:, 0:64], p.cst[:, 13:14])
                    p.sV.inc(V.tensor_scalar_mul(p.vtb[:, VW * lt + 65: VW * lt + 129],
                                                 bank[:, 64:128], p.cst[:, 13:14]))
                else:
                    V.tensor_copy(p.vtb[:, VW * lt: VW * lt + 64], bank[:, 0:64])
                    p.sV.inc(V.tensor_copy(p.vtb[:, VW * lt + 65: VW * lt + 129],
                                           bank[:, 64:128]))
        s2 = _snap(p)

        @p.block.sync
        def _(S):
            _wait(p, S, s2)
            p.sD.inc(S.dma_start(out=v_dst(g), in_=p.vtb[:]), 16)

        if p.fused:
            emit_gather(p, v_dst(g), (p.vg0s if layer == 0 else p.vg1s)[g],
                        layer, "v", g)


def emit_norm2_max(p, tiles, dstcol, init):
    s0 = _snap(p)
    n = len(tiles)
    nj = n * NQB
    assert nj <= 32

    @p.block.scalar
    def _(A):
        _wait(p, A, s0)
        for i in range(n):
            for qb in range(NQB):
                j = i * NQB + qb
                if j >= 2:
                    A.wait_ge(p.sT.sem, s0["sT"] + (j - 1))
                p.sA.inc(A.activation(out=p.scr[:, 512 * (j % 2):512 * (j % 2 + 1)],
                                      in_=tiles[i][:, QB * qb:QB * (qb + 1)],
                                      func=AF.Square))

    @p.block.tensor
    def _(TE):
        _wait(p, TE, s0)
        for j in range(nj):
            TE.wait_ge(p.sA.sem, s0["sA"] + j + 1)
            if j >= 1:
                TE.wait_ge(p.sV.sem, s0["sV"] + j)
            p.sT.inc(TE.matmul(p.ps512[3][0:1, :], p.onesf[:, 0:1],
                               p.scr[:, 512 * (j % 2):512 * (j % 2 + 1)],
                               start=True, stop=True))

    @p.block.vector
    def _(V):
        for j in range(nj):
            V.wait_ge(p.sT.sem, s0["sT"] + j + 1)
            p.sV.inc(V.reduce_max(p.cst[0:1, 16 + j:17 + j], p.ps512[3][0:1, :], axis=AX))
        V.wait_ge(p.sV.sem, p.sV.n)
        VS(p, V, V.reduce_max(p.cst[0:1, 15:16], p.cst[0:1, 16:16 + nj], axis=AX))
        if init:
            p.sV.inc(V.tensor_copy(p.cst[0:1, dstcol:dstcol + 1], p.cst[0:1, 15:16]))
        else:
            p.sV.inc(V.tensor_max(p.cst[0:1, dstcol:dstcol + 1],
                                  p.cst[0:1, dstcol:dstcol + 1], p.cst[0:1, 15:16]))
        V.wait_ge(p.sV.sem, p.sV.n)


def emit_kstat_allreduce(p):
    s = _snap(p)

    @p.block.vector
    def _(V):
        _wait(p, V, s)
        p.sV.inc(V.memset(p.scr[:, 0:128], 0.0))
        p.sV.inc(V.tensor_copy(p.scr[0:1, 0:1], p.cst[0:1, 1:2]))

    s1 = _snap(p)

    @p.block.gpsimd
    def _(G):
        _wait(p, G, s1)
        p.sG.inc(G.dma_start(out=p.cc_in[:], in_=p.scr[:, 0:128]), 16)
        G.wait_ge(p.sG.sem, p.sG.n)
        p.sC.inc(G.collective_compute(
            "AllReduce", ALU.max,
            replica_groups=[[0, 1, 2, 3], [4, 5, 6, 7]],
            ins=[p.cc_in.opt()], outs=[p.cc_out.opt()]), 1)
    arC = p.sC.n

    @p.block.scalar
    def _(A):
        A.wait_ge(p.sC.sem, arC)
        p.sG.inc(A.dma_start(out=p.cst[0:1, 1:2], in_=p.cc_out[0:1, 0:1]), 16)


def emit_cstats(p, layer):
    dbg2 = int(os.environ.get("KDBG2", "99"))
    emit_norm2_max(p, tiles=[p.qa[c] for c in range(NDC)], dstcol=0, init=True)
    if dbg2 < 1:
        s_ = _snap(p)

        @p.block.vector
        def _(V):
            _wait(p, V, s_)
            p.sV.inc(V.memset(p.cst[0:1, 1:2], 100.0))
        dbg2_done = True
    if not p.fused and dbg2 >= 1:
        kg = p.kg0 if layer == 0 else p.kg1
        rows = 1024 if layer == 0 else 4096
        for blk in range(min(dbg2, rows // 128)):
            s = _snap(p)

            @p.block.sync
            def _(S):
                _wait(p, S, s)
                p.sD.inc(S.dma_start(out=p.locsl[:, 0:T].bitcast(BF16),
                                     in_=kg[128 * blk:128 * (blk + 1), :]), 16)

            emit_norm2_max(p, tiles=[p.locsl[:, 0:T]], dstcol=1, init=(blk == 0))
    s2 = _snap(p)

    @p.block.scalar
    def _(A):
        _wait(p, A, s2)
        A.activation(out=p.cst[0:1, 2:3], in_=p.cst[0:1, 0:1], func=AF.Ln)
        p.sA.inc(A.activation(out=p.cst[0:1, 3:4], in_=p.cst[0:1, 1:2], func=AF.Ln))

    s3 = _snap(p)

    @p.block.vector
    def _(V):
        V.wait_ge(p.sA.sem, s3["sA"])
        VS(p, V, V.tensor_add(p.cst[0:1, 4:5], p.cst[0:1, 2:3], p.cst[0:1, 3:4]))
        p.sV.inc(V.tensor_scalar(p.cst[0:1, 4:5], p.cst[0:1, 4:5], 0.5,
                                 -float(np.log(8.0)), ALU.mult, ALU.add))

    s4 = _snap(p)

    @p.block.scalar
    def _(A):
        A.wait_ge(p.sV.sem, s4["sV"])
        p.sA.inc(A.activation(out=p.cst[0:1, 5:6], in_=p.cst[0:1, 4:5], func=AF.Exp))

    s5 = _snap(p)

    @p.block.tensor
    def _(TE):
        TE.wait_ge(p.sA.sem, s5["sA"])
        p.sT.inc(TE.matmul(p.ps512[3][:, 0:1], p.onesf[0:1, :], p.cst[0:1, 5:6],
                           start=True, stop=True))

    s6 = _snap(p)

    @p.block.vector
    def _(V):
        V.wait_ge(p.sT.sem, s6["sT"])
        VS(p, V, V.tensor_copy(p.ccol[:], p.ps512[3][:, 0:1]))
        p.sV.inc(V.tensor_scalar_mul(p.cneg[:], p.ccol[:], -1.0))
        if layer == 0:
            p.sV.inc(V.tensor_scalar(p.biastbl[:], p.killsb[:], KILL, p.ccol[:],
                                     ALU.mult, ALU.subtract))
        V.wait_ge(p.sV.sem, p.sV.n)


def emit_attention(p, layer):
    kg = (p.kg0 if layer == 0 else p.kg1) if not p.fused else None
    vg = (p.vg0 if layer == 0 else p.vg1) if not p.fused else None
    ngroups = 2 if layer == 0 else 8
    krows = 256 if layer == 0 else 1024
    sz = _snap(p)

    @p.block.vector
    def _(V):
        _wait(p, V, sz)
        p.sV.inc(V.memset(p.lr[:], 0.0))
        p.sV.inc(V.memset(p.lr2[:], 0.0))

    sbase = _snap(p)

    for grp in range(ngroups):
        s0 = _snap(p)

        @p.block.sync
        def _(S):
            if p.fused:
                S.wait_ge(p.sD.sem, p.sD.n)
                S.wait_ge(p.sT.sem, s0["sT"])
                S.wait_ge(p.sC.sem, p.cc_k[layer][grp])
                p.sD.inc(S.dma_start(
                    out=p.wb[:].rearrange("p (r n) -> p r n", r=4),
                    in_=(p.kg0s if layer == 0 else p.kg1s)[grp]
                        .rearrange("(r f) n -> f r n", f=128)), 16)
            else:
                _wait(p, S, s0)
                S.wait_ge(p.sD.sem, p.sD.n)
                p.sD.inc(S.dma_start(
                    out=p.wb[:].rearrange("p (r n) -> p r n", r=4),
                    in_=kg.rearrange("(r f) n -> f r n", f=krows)[128 * grp:128 * (grp + 1)]), 16)
                p.sD.inc(S.dma_start(
                    out=p.vsl[:].rearrange("p (r n) -> p r n", r=4),
                    in_=vg.rearrange("(r p) n -> p r n", p=128)[:, :, NTT * VW * grp: NTT * VW * (grp + 1)]), 16)
            if layer == 0:
                p.sD.inc(S.dma_start(out=p.locsl[:, 0:2048],
                                     in_=p.k_loc0[128 * grp:128 * (grp + 1), :]), 16)
                vl0 = p.v_loc0s[grp][:] if p.fused \
                    else p.v_loc0[:, NTT * VW * grp: NTT * VW * (grp + 1)]
                p.sD.inc(S.dma_start(out=p.locsl[:, 2048:2048 + NTT * VW], in_=vl0), 16)

        if p.fused:
            @p.block.gpsimd
            def _(G):
                G.wait_ge(p.sT.sem, s0["sT"])
                G.wait_ge(p.sC.sem, p.cc_v[layer][grp])
                p.sG.inc(G.dma_start(
                    out=p.vsl[:, 0:4 * NTT * VW].rearrange("p (r n) -> p r n", r=4),
                    in_=(p.vg0s if layer == 0 else p.vg1s)[grp]
                        .rearrange("(r p) n -> p r n", p=128)), 16)

        hps = [grp] if layer == 1 else [grp, grp + 2, grp + 4, grp + 6]
        for hp in hps:
            for qb in range(NQB):
                emit_attn_block(p, layer, hp, qb, sbase)
    emit_norm_flush(p)


def emit_attn_block(p, layer, hp, qb, sbase):
    nloc = (4 * qb + 4) if layer == 0 else 0
    nglob = 48 if layer == 0 else NKT
    kts = list(range(nglob)) + list(range(NKT, NKT + nloc))
    R = len(kts)
    s0 = _snap(p)
    qs = slice(QB * qb, QB * (qb + 1))

    def k_lhsT(kt, half):
        if kt < NKT:
            src = p.wb[:, 2048 * (kt // 16) + 128 * (kt % 16): 2048 * (kt // 16) + 128 * (kt % 16) + 128]
        else:
            lt = kt - NKT
            src = p.locsl[:, 128 * lt:128 * (lt + 1)]
        return src[64 * half:64 * (half + 1), :]

    def v_lhsT(kt, half):
        if kt < NKT:
            base = 2080 * (kt // 16) + VW * (kt % 16)
            return p.vsl[:, base + 65 * half: base + 65 * (half + 1)]
        lt = kt - NKT
        base = 2048 + VW * lt
        return p.locsl[:, base + 65 * half: base + 65 * (half + 1)]

    def bias_ap(kt):
        if layer == 1 or kt >= NKT:
            return p.cneg[:]
        return p.biastbl[:, NKT * qb + kt: NKT * qb + kt + 1]

    def diag_mask(kt):
        if layer == 1 or kt < NKT:
            return None
        jj = (kt - NKT) - 4 * qb
        return jj if 0 <= jj <= 3 else None

    mask_rounds = {i: diag_mask(kts[i]) for i in range(R) if diag_mask(kts[i]) is not None}

    def pt_ready(i):
        if i in mask_rounds:
            nv = sum(1 for k_ in mask_rounds if k_ <= i)
            return (p.sV.sem, s0["sV"] + nv)
        return (p.sA.sem, s0["sA"] + i + 1)

    sSc = []

    @p.block.tensor
    def _(TE):
        TE.wait_ge(p.sD.sem, s0["sD"])
        if s0["sG"]:
            TE.wait_ge(p.sG.sem, s0["sG"])
        TE.wait_ge(p.sV.sem, sbase["sV"])
        for i in range(R):
            buf = i % 2
            if i == 0:
                TE.wait_ge(p.sA.sem, max(0, s0["sA"] - 1))
            else:
                TE.wait_ge(p.sA.sem, s0["sA"] + (i - 1) if i >= 2 else s0["sA"])
            kt = kts[i]
            TE.matmul(p.ps1k[buf][:, 0:512], k_lhsT(kt, 0), p.qa[hp][0:64, qs],
                      start=True, stop=True, tile_position=(0, 0))
            mm = TE.matmul(p.ps1k[buf][:, 512:1024], k_lhsT(kt, 1), p.qa[hp][64:128, qs],
                           start=True, stop=True, tile_position=(64, 0))
            sSc.append(p.sT.inc(mm))
            if i == 10 and p.pend is not None:
                _norm_bcast(p, TE)
            if i >= 1:
                _pv_round(p, TE, i - 1, kts, v_lhsT, R, pt_ready, s0)
        _pv_round(p, TE, R - 1, kts, v_lhsT, R, pt_ready, s0)
    lastpv = p.sT.n

    @p.block.scalar
    def _(A):
        A.wait_ge(p.sV.sem, sbase["sV"])
        A.wait_ge(p.sD.sem, s0["sD"])
        for i in range(R):
            buf = i % 2
            A.wait_ge(p.sT.sem, sSc[i])
            p.sA.inc(A.activation(out=p.pt[buf], in_=p.ps1k[buf][:], func=AF.Exp,
                                  scale=0.125, bias=bias_ap(kts[i])))

    if mask_rounds:
        @p.block.vector
        def _(V):
            for i, jj in sorted(mask_rounds.items()):
                V.wait_ge(p.sA.sem, s0["sA"] + i + 1)
                buf = i % 2
                msl = p.masks[:, 384 - 128 * jj: 896 - 128 * jj]
                V.tensor_tensor(p.pt[buf][:, 0:512], p.pt[buf][:, 0:512], msl, ALU.mult)
                p.sV.inc(V.tensor_tensor(p.pt[buf][:, 512:1024], p.pt[buf][:, 512:1024],
                                         msl, ALU.mult))

    _norm_mults(p)
    emit_attn_store(p, hp, qb, lastpv)


def _pv_round(p, TE, i, kts, v_lhsT, R, pt_ready, s0):
    buf = i % 2
    sem, val = pt_ready(i)
    TE.wait_ge(sem, val)
    kt = kts[i]
    first = (i == 0)
    last = (i == R - 1)
    if first:
        TE.wait_ge(p.sV.sem, p.pv_guard)
    TE.matmul(p.ps512[0][0:65, :], v_lhsT(kt, 0), p.pt[buf][:, 0:512],
              start=first, stop=last)
    mm = TE.matmul(p.ps512[1][0:65, :], v_lhsT(kt, 1), p.pt[buf][:, 512:1024],
                   start=first, stop=last)
    if last:
        p.sT.inc(mm)


def _norm_bcast(p, TE):
    pend = p.pend
    TE.wait_ge(p.sV.sem, pend["rec"])
    if p.norm_mults_cnt:
        TE.wait_ge(p.sV.sem, p.norm_mults_cnt)
    TE.matmul(p.ps512[2][:], p.onesf[:], p.lr[:], start=True, stop=True)
    pend["bcast"] = p.sT.inc(
        TE.matmul(p.ps512[3][:], p.onesf[:], p.lr2[:], start=True, stop=True))


def _norm_mults(p, flush=False):
    pend = p.pend
    if pend is None:
        return
    if flush:
        s = _snap(p)

        @p.block.tensor
        def _(TE):
            _wait(p, TE, s)
            _norm_bcast(p, TE)
    hp, qb, qs = pend["hp"], pend["qb"], pend["qs"]

    @p.block.vector
    def _(V):
        V.wait_ge(p.sT.sem, pend["bcast"])
        V.tensor_tensor(p.qa[hp][0:64, qs], p.qa[hp][0:64, qs],
                        p.ps512[2][0:64, :], ALU.mult)
        p.sV.inc(V.tensor_tensor(p.attB[:, qs], p.attB[:, qs],
                                 p.ps512[3][0:64, :], ALU.mult))
    p.norm_mults_cnt = p.sV.n
    if qb == NQB - 1:
        @p.block.sync
        def _(S):
            S.wait_ge(p.sV.sem, p.norm_mults_cnt)
            p.sD.inc(S.dma_start(out=p.qa[hp][64:128, :], in_=p.attB[:]), 16)
        p.attb_guard = p.sD.n
    p.pend = None


def emit_attn_store(p, hp, qb, lastpv):
    qs = slice(QB * qb, QB * (qb + 1))

    @p.block.vector
    def _(V):
        V.wait_ge(p.sT.sem, lastpv)
        if qb == 0 and p.attb_guard:
            V.wait_ge(p.sD.sem, p.attb_guard)
        V.tensor_copy(p.qa[hp][0:64, qs], p.ps512[0][0:64, :])
        V.tensor_copy(p.attB[:, qs], p.ps512[1][0:64, :])
        V.tensor_scalar_add(p.scr[64:65, 0:512], p.ps512[0][64:65, :], 1e-38)
        p.sV.inc(V.tensor_scalar_add(p.scr[64:65, 512:1024],
                                     p.ps512[1][64:65, :], 1e-38))
        p.pv_guard = p.sV.n
        V.reciprocal(p.lr[64:65, :], p.scr[64:65, 0:512])
        p.sV.inc(V.reciprocal(p.lr2[64:65, :], p.scr[64:65, 512:1024]))
    p.pend = {"hp": hp, "qb": qb, "qs": qs, "rec": p.sV.n, "bcast": None}


def emit_norm_flush(p):
    _norm_mults(p, flush=True)


def emit_oproj(p, w, bias, inner_res=False):
    emit_weight_cast_dma(p, w, rows=D, cols=D)
    s0 = _snap(p)
    if bias is not None:
        @p.block.sync
        def _(S):
            p.sD.inc(S.dma_start(out=p.bcol[:, 0:8], in_=getattr(p, bias)[:]), 16)
    s1 = _snap(p)

    @p.block.tensor
    def _(TE):
        _wait(p, TE, s1)
        for oc in range(NDC):
            for qb in range(NQB):
                i = oc * NQB + qb
                bank = p.ps512[i % 2]
                if i >= 2:
                    TE.wait_ge(p.sV.sem, s1["sV"] + (i - 1))
                for c in range(NDC):
                    mm = TE.matmul(bank[:], p.wb[:, 1024 * c + 128 * oc: 1024 * c + 128 * (oc + 1)],
                                   p.qa[c][:, QB * qb:QB * (qb + 1)],
                                   start=(c == 0), stop=(c == NDC - 1))
                p.sT.inc(mm)

    @p.block.vector
    def _(V):
        for oc in range(NDC):
            for qb in range(NQB):
                i = oc * NQB + qb
                bank = p.ps512[i % 2]
                V.wait_ge(p.sT.sem, s1["sT"] + i + 1)
                qs = slice(QB * qb, QB * (qb + 1))
                if inner_res:
                    V.tensor_add(p.scr[:, 0:512], bank[:], p.xf[oc][:, qs])
                    ins = V.tensor_tensor(p.xf[oc][:, qs], p.scr[:, 0:512],
                                          p.xn[oc][:, qs], ALU.add)
                elif bias is None:
                    ins = V.tensor_add(p.xf[oc][:, qs], bank[:], p.xf[oc][:, qs])
                else:
                    ins = V.scalar_tensor_tensor(p.xf[oc][:, qs], bank[:],
                                                 p.bcol[:, oc:oc + 1], p.xf[oc][:, qs],
                                                 ALU.add, ALU.add)
                p.sV.inc(ins)


def emit_ffn0_build(p):
    wsl = p.vsl
    s0 = _snap(p)

    @p.block.sync
    def _(S):
        _wait(p, S, s0)
        p.sD.inc(S.dma_start(out=p.bcol[:, 0:8], in_=p.brbi[:]), 16)

    locf = p.locsl[:].bitcast(F32)
    s1 = _snap(p)

    @p.block.sync
    def _(S):
        _wait(p, S, s1)
        for sec, (a_t, b_t, off) in enumerate(
                [(p.aR, p.bR, 0)] * 4 + [(p.aI, p.bI, 2048)] * 4):
            ch = sec % 4
            base = 192 * sec
            for c in range(4):
                p.sD.inc(S.dma_start(out=p.scr[:, base + 32 * c: base + 32 * (c + 1)],
                                     in_=a_t[c, 128 * ch:128 * (ch + 1), :]), 16)
                p.sD.inc(S.dma_start(out=p.scr[:, base + 128 + 16 * c: base + 128 + 16 * (c + 1)],
                                     in_=b_t[c, 128 * ch:128 * (ch + 1), :]), 16)
    ffD = p.sD.n

    @p.block.vector
    def _(V):
        _wait(p, V, s1)
        V.wait_ge(p.sD.sem, ffD)
        for sec in range(8):
            off = 0 if sec < 4 else 2048
            ch = sec % 4
            base = 192 * sec
            for c in range(4):
                a_b = p.scr[:, base + 32 * c: base + 32 * (c + 1)].unsqueeze(2).broadcast_to([128, 32, 16])
                b_b = p.scr[:, base + 128 + 16 * c: base + 128 + 16 * (c + 1)].unsqueeze(1).broadcast_to([128, 32, 16])
                if c == 0:
                    V.tensor_tensor(locf[:, 0:512].rearrange("p (b e) -> p b e", b=32),
                                    a_b, b_b, ALU.mult)
                else:
                    V.tensor_tensor(locf[:, 512:1024].rearrange("p (b e) -> p b e", b=32),
                                    a_b, b_b, ALU.mult)
                    V.tensor_add(locf[:, 0:512], locf[:, 0:512], locf[:, 512:1024])
            p.sV.inc(V.tensor_copy(wsl[:, off + 512 * ch: off + 512 * (ch + 1)],
                                   locf[:, 0:512]))

    s3 = _snap(p)

    @p.block.vector
    def _(V):
        _wait(p, V, s3)
        for ch in range(4):
            p.sV.inc(V.tensor_scalar_mul(wsl[:, 4096 + 512 * ch: 4096 + 512 * (ch + 1)],
                                         wsl[:, 2048 + 512 * ch: 2048 + 512 * (ch + 1)], -1.0))


def emit_ffn0(p):
    wsl = p.vsl
    s4 = _snap(p)

    @p.block.tensor
    def _(TE):
        _wait(p, TE, s4)
        for oc in range(8):
            real = oc < 4
            o = oc if real else oc - 4
            for qb in range(NQB):
                i = oc * NQB + qb
                bank = p.ps512[i % 2]
                if i >= 2:
                    TE.wait_ge(p.sV.sem, s4["sV"] + (i - 1))
                qsl = slice(QB * qb, QB * (qb + 1))
                for ic in range(4):
                    w_off = (0 if real else 2048) + 512 * ic + 128 * o
                    TE.matmul(bank[:], wsl[:, w_off:w_off + 128], p.xn[ic][:, qsl],
                              start=(ic == 0), stop=False)
                for ic in range(4):
                    w_off = (4096 if real else 0) + 512 * ic + 128 * o
                    mm = TE.matmul(bank[:], wsl[:, w_off:w_off + 128], p.xn[4 + ic][:, qsl],
                                   start=False, stop=(ic == 3))
                p.sT.inc(mm)

    @p.block.vector
    def _(V):
        for oc in range(8):
            for qb in range(NQB):
                i = oc * NQB + qb
                bank = p.ps512[i % 2]
                V.wait_ge(p.sT.sem, s4["sT"] + i + 1)
                qsl = slice(QB * qb, QB * (qb + 1))
                p.sV.inc(V.scalar_tensor_tensor(p.xf[oc][:, qsl], bank[:],
                                                p.bcol[:, oc:oc + 1], p.xf[oc][:, qsl],
                                                ALU.add, ALU.add))


def emit_qkv1(p):
    s0 = _snap(p)

    @p.block.sync
    def _(S):
        _wait(p, S, s0)
        p.sD.inc(S.dma_start(out=p.bcol[:, 0:8], in_=p.bq1[:]), 16)
        p.sD.inc(S.dma_start(out=p.bcol[:, 8:16], in_=p.bk1[:]), 16)
        p.sD.inc(S.dma_start(out=p.brow[0:1, 0:1024], in_=p.bv1[:]), 16)

    emit_weight_cast_dma(p, p.Wk1, rows=D, cols=D, dst=p.vsl)
    emit_proj_all(p, wcols=1024, ocs=list(range(NDC)), outs=p.qa, rope=False,
                  bias_cols=[p.bcol[:, oc:oc + 1] for oc in range(NDC)])
    emit_weight_cast_dma(p, p.Wv1, rows=D, cols=D)
    for pair in range(4):
        ocs = [2 * pair, 2 * pair + 1]
        emit_proj_all(p, wcols=1024, ocs=ocs, outs=p.kfm, rope=False,
                      bias_cols=[p.bcol[:, 8 + oc:9 + oc] for oc in ocs], wsrc=p.vsl)
        emit_norm2_max(p, tiles=[p.kfm[0], p.kfm[1]], dstcol=1, init=(pair == 0))
        if pair == 3 and p.fused:
            emit_kstat_allreduce(p)
        s2 = _snap(p)

        @p.block.sync
        def _(S):
            _wait(p, S, s2)
            for j in range(2):
                oc = 2 * pair + j
                p.sD.inc(S.dma_start(out=p.k_loc1[128 * oc:128 * (oc + 1), :],
                                     in_=p.kfm[j][:]), 16)

        if p.fused:
            for j in range(2):
                g = 2 * pair + j
                emit_gather(p, p.k_loc1[128 * g:128 * (g + 1), :], p.kg1s[g], 1, "k", g)

    emit_vproj(p, groups=8, wcols=1024, bias_row=p.brow,
               v_loc=p.v_loc1s if p.fused else p.v_loc1, layer=1)


def emit_qkv0(p):
    dbg = int(os.environ.get("KDBG", "99"))
    if dbg >= 4:
        emit_proj_all(p, wcols=1024, ocs=list(range(NDC)), outs=p.qa, rope=True, bias_cols=None)
    if dbg < 6:
        return
    emit_weight_gain_cast(p, p.w_k, cols=256)
    emit_proj_all(p, wcols=256, ocs=[0, 1], outs=p.kfm, rope=True, bias_cols=None)
    s1 = _snap(p)

    @p.block.sync
    def _(S):
        _wait(p, S, s1)
        for oc in range(2):
            p.sD.inc(S.dma_start(out=p.k_loc0[128 * oc:128 * (oc + 1), :], in_=p.kfm[oc][:]), 16)

    if p.fused:
        for g in range(2):
            emit_gather(p, p.k_loc0[128 * g:128 * (g + 1), :], p.kg0s[g], 0, "k", g)

    emit_norm2_max(p, tiles=[p.kfm[0], p.kfm[1]], dstcol=1, init=True)
    if p.fused:
        emit_kstat_allreduce(p)
    if int(os.environ.get("KDBG", "99")) < 7:
        return
    emit_weight_gain_cast(p, p.w_v, cols=256)
    emit_vproj(p, groups=2, wcols=256, bias_row=None,
               v_loc=p.v_loc0s if p.fused else p.v_loc0, rms=True, layer=0)


def emit_moe(p):
    s0 = _snap(p)

    @p.block.sync
    def _(S):
        _wait(p, S, s0)
        for c in range(NDC):
            p.sD.inc(S.dma_start(out=p.wtmp[:, 8 * c:8 * (c + 1)],
                                 in_=p.gwT[128 * c:128 * (c + 1), :]), 16)
        p.sD.inc(S.dma_start(out=p.brow[0:1, 1024:1032], in_=p.gbeb[0:1, :]), 16)
        p.sD.inc(S.dma_start(out=p.brow[0:1, 1032:1040], in_=p.gbeb[1:2, :]), 16)

    s1 = _snap(p)

    @p.block.tensor
    def _(TE):
        _wait(p, TE, s1)
        for c in range(NDC):
            p.sT.inc(TE.matmul(p.ps512[3][0:1, 0:8], p.onesf[:, 0:1],
                               p.wtmp[:, 8 * c:8 * (c + 1)],
                               start=(c == 0), stop=(c == NDC - 1)))

    s2 = _snap(p)

    @p.block.vector
    def _(V):
        V.wait_ge(p.sT.sem, s2["sT"])
        p.sV.inc(V.tensor_copy(p.srow[0:1, 1024:1032], p.ps512[3][0:1, 0:8]))
        VS(p, V, V.tensor_add(p.srow[0:1, 1032:1040], p.brow[0:1, 1024:1032],
                              p.brow[0:1, 1032:1040]))
        p.sV.inc(V.tensor_scalar_mul(p.srow[0:1, 1040:1048],
                                     p.srow[0:1, 1032:1040], -1.0))
        V.wait_ge(p.sV.sem, p.sV.n)

    for qb in range(NQB):
        qsl = slice(QB * qb, QB * (qb + 1))
        s3 = _snap(p)

        @p.block.tensor
        def _(TE):
            _wait(p, TE, s3)
            for c in range(NDC):
                p.sT.inc(TE.matmul(p.ps1k[0][0:8, 0:512], p.wtmp[:, 8 * c:8 * (c + 1)],
                                   p.xf[c][:, qsl], start=(c == 0), stop=(c == NDC - 1)))

        s4 = _snap(p)

        @p.block.vector
        def _(V):
            V.wait_ge(p.sT.sem, s4["sT"])
            p.sV.inc(V.tensor_copy(p.lg, p.ps1k[0][0:8, 0:512]))

        for tt in range(4):
            gt = 4 * qb + tt
            s5 = _snap(p)

            @p.block.tensor
            def _(TE):
                _wait(p, TE, s5)
                TE.transpose(p.ps512[2][:, 0:8], p.lg[0:8, 128 * tt:128 * (tt + 1)],
                             p.identsb[0:8, 0:8])
                TE.transpose(p.ps512[3][:, 8:9],
                             p.lnr[0:1, qsl.start + 128 * tt: qsl.start + 128 * (tt + 1)],
                             p.identsb[0:1, 0:1])
                TE.matmul(p.ps512[3][:, 0:8],
                          p.lnmr[0:1, qsl.start + 128 * tt: qsl.start + 128 * (tt + 1)],
                          p.srow[0:1, 1024:1032], start=True, stop=False)
                p.sT.inc(TE.matmul(p.ps512[3][:, 0:8], p.onesf[0:1, :],
                                   p.srow[0:1, 1040:1048], start=False, stop=True))

            s6 = _snap(p)

            @p.block.vector
            def _(V):
                V.wait_ge(p.sT.sem, s6["sT"])
                VS(p, V, V.tensor_copy(p.cst[:, 10:11], p.ps512[3][:, 8:9]))
                VS(p, V, V.tensor_scalar_mul(p.scr[:, 0:8], p.ps512[2][:, 0:8],
                                             p.cst[:, 10:11]))
                VS(p, V, V.tensor_tensor(p.scr[:, 0:8], p.scr[:, 0:8],
                                         p.ps512[3][:, 0:8], ALU.subtract))
                VS(p, V, V.reduce_max(p.cst[:, 9:10], p.scr[:, 0:8], axis=AX))
                p.sV.inc(V.tensor_scalar(p.mkall[:, 8 * gt:8 * (gt + 1)], p.scr[:, 0:8],
                                         p.cst[:, 9:10], None, ALU.is_ge))

    s8 = _snap(p)
    for e in range(NEXP):
        s8e = _snap(p)

        @p.block.tensor
        def _(TE):
            _wait(p, TE, s8e)
            for gt in range(NTT):
                if gt >= 1:
                    TE.wait_ge(p.sV.sem, s8e["sV"] + gt)
                p.sT.inc(TE.transpose(p.ps512[2][0:1, 8:136],
                                      p.mkall[:, 8 * gt + e:8 * gt + e + 1],
                                      p.identsb[:]))

        @p.block.vector
        def _(V):
            for gt in range(NTT):
                V.wait_ge(p.sT.sem, s8e["sT"] + gt + 1)
                p.sV.inc(V.tensor_copy(p.mrowst[0:1, 128 * gt:128 * (gt + 1)],
                                       p.ps512[2][0:1, 8:136]))

        s8f = _snap(p)

        @p.block.sync
        def _(S):
            _wait(p, S, s8f)
            p.sD.inc(S.dma_start(out=p.mrow1d[e:e + 1, :], in_=p.mrowst), 16)

    s_moe = _snap(p)
    hT = {}
    hA = {}
    hV = {}
    yT = {}
    yV = {}
    hbanks = [p.ps1k[0][:, 0:512], p.ps1k[0][:, 512:1024],
              p.ps1k[1][:, 0:512], p.ps1k[1][:, 512:1024]]
    w1slab = [p.wb[:, 0:4096], p.wb[:, 4096:8192]]
    w2slab = [p.vsl[:, 0:4096], p.vsl[:, 4096:8192]]
    htmp = [p.locsl[:, 512 * i:512 * (i + 1)] for i in range(4)]
    mkb = p.ubuf[:, 0:2048]

    for e in range(NEXP):
        se0 = _snap(p)

        @p.block.sync
        def _(S):
            _wait(p, S, se0)
            p.sD.inc(S.dma_start(out=p.mrowst, in_=p.mrow1d[e:e + 1, :]), 16)
            p.sD.inc(S.dma_start(out=p.wtmp[:, 64:96], in_=p.b1c[e]), 16)
        b1D = p.sD.n

        se1 = _snap(p)

        @p.block.tensor
        def _(TE):
            _wait(p, TE, se1)
            for j in range(4):
                if j >= 1:
                    TE.wait_ge(p.sV.sem, se1["sV"] + j)
                p.sT.inc(TE.matmul(hbanks[3], p.onesb[0:1, 0:128],
                                   p.mrowst[0:1, 512 * j:512 * (j + 1)],
                                   start=True, stop=True))

        @p.block.vector
        def _(V):
            for j in range(4):
                V.wait_ge(p.sT.sem, se1["sT"] + j + 1)
                p.sV.inc(V.tensor_copy(mkb[:, 512 * j:512 * (j + 1)], hbanks[3]))
        bcV = p.sV.n

        for fg in range(8):
            t = e * 8 + fg
            sl = t % 2

            @p.block.gpsimd
            def _(G):
                if t >= 2:
                    G.wait_ge(p.sT.sem, yT[t - 2])
                else:
                    _wait(p, G, s_moe)
                for c in range(NDC):
                    p.sG.inc(G.dma_start(
                        out=w1slab[sl][:, 512 * c:512 * (c + 1)],
                        in_=p.w1[e, 128 * c:128 * (c + 1), 512 * fg:512 * (fg + 1)]), 16)
            w1D = p.sG.n

            @p.block.gpsimd
            def _(G):
                for sf in range(4):
                    p.sG.inc(G.dma_start(
                        out=w2slab[sl][:, 1024 * sf:1024 * (sf + 1)],
                        in_=p.w2[e, 512 * fg + 128 * sf: 512 * fg + 128 * (sf + 1), :]), 16)
                if fg == 1:
                    p.sG.inc(G.dma_start(out=p.b2row[:], in_=p.b2[e:e + 1, :]), 16)
            w2D = p.sG.n

            hTbase = p.sT.n

            @p.block.tensor
            def _(TE):
                if t == 0:
                    _wait(p, TE, s_moe)
                TE.wait_ge(p.sG.sem, w1D)
                for qb in range(NQB):
                    qsl = slice(QB * qb, QB * (qb + 1))
                    for sf in range(4):
                        j = qb * 4 + sf
                        if t >= 1:
                            TE.wait_ge(p.sA.sem, hA[t - 1] + max(0, j - 3))
                        if fg == 0 and j == 3:
                            TE.wait_ge(p.sV.sem, bcV)
                        for c in range(NDC):
                            mm = TE.matmul(
                                hbanks[j % 4],
                                w1slab[sl][:, 512 * c + 128 * sf: 512 * c + 128 * (sf + 1)],
                                p.xn[c][:, qsl], start=(c == 0), stop=(c == NDC - 1))
                        p.sT.inc(mm)
            hT[t] = p.sT.n

            hAbase = p.sA.n
            vA = p.sV.n

            @p.block.scalar
            def _(A):
                if t == 0:
                    _wait(p, A, s_moe)
                if fg == 0:
                    A.wait_ge(p.sD.sem, b1D)
                for j in range(16):
                    sf = j % 4
                    fc = 4 * fg + sf
                    A.wait_ge(p.sT.sem, hTbase + j + 1)
                    if j >= 4:
                        A.wait_ge(p.sV.sem, vA + j - 3)
                    elif t >= 1:
                        A.wait_ge(p.sV.sem, hV[t - 1] + j - 3)
                    p.sA.inc(A.activation(out=htmp[j % 4], in_=hbanks[j % 4],
                                          func=AF.Relu,
                                          bias=p.wtmp[:, 64 + fc:65 + fc]))
            hA[t] = p.sA.n

            @p.block.vector
            def _(V):
                if t < 2:
                    _wait(p, V, s_moe)
                else:
                    V.wait_ge(p.sT.sem, yT[t - 2])
                for j in range(16):
                    qb, sf = j // 4, j % 4
                    qsl = slice(QB * qb, QB * (qb + 1))
                    V.wait_ge(p.sA.sem, hAbase + j + 1)
                    p.sV.inc(V.tensor_tensor(p.qa[4 * sl + sf][:, qsl],
                                             htmp[j % 4], mkb[:, qsl], ALU.mult))

            yTbase = p.sT.n
            vY = p.sV.n
            hV[t] = vY

            @p.block.tensor
            def _(TE):
                TE.wait_ge(p.sG.sem, w2D)
                for qb in range(NQB):
                    qsl = slice(QB * qb, QB * (qb + 1))
                    TE.wait_ge(p.sV.sem, vA + 4 * qb + 4)
                    for dc in range(NDC):
                        g = qb * 8 + dc
                        bank = p.ps512[g % 4]
                        if g >= 4:
                            TE.wait_ge(p.sV.sem, vY + g - 3)
                        elif t >= 1:
                            TE.wait_ge(p.sV.sem, yV[t - 1])
                        for sf in range(4):
                            mm = TE.matmul(
                                bank[:],
                                w2slab[sl][:, 1024 * sf + 128 * dc: 1024 * sf + 128 * (dc + 1)],
                                p.qa[4 * sl + sf][:, qsl],
                                start=(sf == 0), stop=(sf == 3 and fg != 7))
                        if fg == 7:
                            mm = TE.matmul(bank[:], p.b2row[0:1, 128 * dc:128 * (dc + 1)],
                                           p.mrowst[0:1, qsl], start=False, stop=True)
                        p.sT.inc(mm)
            yT[t] = p.sT.n

            @p.block.vector
            def _(V):
                for qb in range(NQB):
                    qsl = slice(QB * qb, QB * (qb + 1))
                    for dc in range(NDC):
                        g = qb * 8 + dc
                        V.wait_ge(p.sT.sem, yTbase + g + 1)
                        p.sV.inc(V.tensor_add(p.xf[dc][:, qsl], p.ps512[g % 4][:],
                                              p.xf[dc][:, qsl]))
            yV[t] = p.sV.n



_NC_CACHE = {}


def _get_prog(part):
    if part not in _NC_CACHE:
        _NC_CACHE[part] = build_program(part)
    return _NC_CACHE[part]


def _per_core_shared(inp):
    aR, bR = tt_compact(inp["L0_ffn_G1_R"], inp["L0_ffn_G2_R"])
    aI, bI = tt_compact(inp["L0_ffn_G1_I"], inp["L0_ffn_G2_I"])
    f32 = lambda a: np.ascontiguousarray(np.asarray(a), dtype=np.float32)
    colsof = lambda v: np.ascontiguousarray(f32(v).reshape(8, 128).T)
    shared = {
        "w_q": f32(inp["L0_attn_w_q"]), "w_k": f32(inp["L0_attn_w_k"]),
        "w_v": f32(inp["L0_attn_w_v"]), "w_o": f32(inp["L0_attn_w_o"]),
        "normw": colsof(inp["L0_attn_norm_w"]),
        "aR": aR, "bR": bR, "aI": aI, "bI": bI,
        "brbi": np.ascontiguousarray(np.concatenate(
            [f32(inp["L0_ffn_bias_real"]).reshape(4, 128).T,
             f32(inp["L0_ffn_bias_imag"]).reshape(4, 128).T], axis=1)),
        "Wq1": f32(inp["L1_attn_Wq"]), "Wk1": f32(inp["L1_attn_Wk"]),
        "Wv1": f32(inp["L1_attn_Wv"]), "Wo1": f32(inp["L1_attn_Wo"]),
        "bq1": colsof(inp["L1_attn_bq"]), "bk1": colsof(inp["L1_attn_bk"]),
        "bv1": f32(inp["L1_attn_bv"]).reshape(1, D),
        "bo1": colsof(inp["L1_attn_bo"]),
        "gwT": np.ascontiguousarray(f32(inp["L1_ffn_gate_w"]).T),
        "gbeb": np.ascontiguousarray(np.stack(
            [f32(inp["L1_ffn_gate_b"]), f32(inp["L1_ffn_expert_bias"])])),
        "w1": f32(inp["L1_ffn_w1"]),
        "b1c": np.ascontiguousarray(
            f32(inp["L1_ffn_b1"]).reshape(NEXP, 32, 128).transpose(0, 2, 1)),
        "w2": f32(inp["L1_ffn_w2"]), "b2": f32(inp["L1_ffn_b2"]),
    }
    return shared


def kernel(**inp):
    x = np.asarray(inp["L0_attn_x"], dtype=np.float32)
    shared = _per_core_shared(inp)
    consts = [host_constants(c) for c in range(NCORES)]
    xs = []
    for c in range(NCORES):
        b, r = c // 4, c % 4
        xs.append(np.ascontiguousarray(x[b, 2048 * r:2048 * (r + 1), :].T))

    def pick(names, extra):
        maps = []
        for c in range(NCORES):
            m = {}
            for n in names:
                if n == "x":
                    m[n] = xs[c]
                elif n in ("rf", "kill", "mask01", "ident"):
                    m[n] = consts[c][n]
                elif n in shared:
                    m[n] = shared[n]
                elif n in extra:
                    m[n] = extra[n][c]
                else:
                    raise KeyError(n)
            maps.append(m)
        return maps

    if os.environ.get("KMODE", "fused") == "fused":
        ncF, namesF = _get_prog("ALL")
        res = run_bass_kernel_spmd(ncF, pick(namesF, {}), core_ids=list(range(NCORES)))
        out = np.empty((2, SEQ, D), dtype=np.float32)
        for c in range(NCORES):
            b, r = c // 4, c % 4
            out[b, 2048 * r:2048 * (r + 1), :] = res.results[c]["out"].T
        return out

    ncA, namesA = _get_prog("A")
    resA = run_bass_kernel_spmd(ncA, pick(namesA, {}), core_ids=list(range(NCORES)))
    rA = resA.results
    kg0 = [np.concatenate([rA[g * 4 + i]["k_loc0"] for i in range(4)], axis=0)
           for g in range(2)]
    vg0 = [np.concatenate([rA[g * 4 + i]["v_loc0"] for i in range(4)], axis=0)
           for g in range(2)]
    extraB = {
        "qdump": [rA[c]["qdump"] for c in range(NCORES)],
        "kg0": [kg0[c // 4] for c in range(NCORES)],
        "vg0": [vg0[c // 4] for c in range(NCORES)],
        "k_loc0": [rA[c]["k_loc0"] for c in range(NCORES)],
        "v_loc0": [rA[c]["v_loc0"] for c in range(NCORES)],
    }
    ncB, namesB = _get_prog("B")
    resB = run_bass_kernel_spmd(ncB, pick(namesB, extraB), core_ids=list(range(NCORES)))
    rB = resB.results
    kg1 = [np.concatenate([rB[g * 4 + i]["k_loc1"] for i in range(4)], axis=0)
           for g in range(2)]
    vg1 = [np.concatenate([rB[g * 4 + i]["v_loc1"] for i in range(4)], axis=0)
           for g in range(2)]
    extraC = {
        "x1": [rB[c]["x1"] for c in range(NCORES)],
        "q1": [rB[c]["q1"] for c in range(NCORES)],
        "kg1": [kg1[c // 4] for c in range(NCORES)],
        "vg1": [vg1[c // 4] for c in range(NCORES)],
    }
    ncC, namesC = _get_prog("C")
    resC = run_bass_kernel_spmd(ncC, pick(namesC, extraC), core_ids=list(range(NCORES)))
    rC = resC.results

    out = np.empty((2, SEQ, D), dtype=np.float32)
    for c in range(NCORES):
        b, r = c // 4, c % 4
        out[b, 2048 * r:2048 * (r + 1), :] = rC[c]["out"].T
    return out
